# version 09
# baseline (speedup 1.0000x reference)
import numpy as np
import concourse.bass as bass
import concourse.mybir as mybir
from concourse.bass_utils import run_bass_kernel_spmd
from concourse.bass_core_rust import IndirectOffsetOnAxis

F32 = mybir.dt.float32
BF16 = mybir.dt.bfloat16
U32 = mybir.dt.uint32
AF = mybir.ActivationFunctionType
OP = mybir.AluOpType
AX = mybir.AxisListType

NCORES = 8
T = 2048
DM = 1024
DS = 64
LL = 64
FF = 4096
NE = 8
TH = 1024
NC_DM = 8
NG = 4
GF = 8
RA = 6
RB = 5
CAP = 768
NSC = CAP // 128
NSLOT = NE * CAP
PADL, PADR = 32, 31
CW = PADL + T + PADR


class _Sched:
    def __init__(self):
        self.ops = {k: [] for k in ("sync", "gpsimd", "scalar", "vector", "tensor")}
        self.nv = 0
        self.ns = 0
        self.nt = 0
        self.lw = {k: {} for k in self.ops}
        self.dma = {}

    def emit(self, eng, fn):
        self.ops[eng].append(fn)

    def wait(self, eng, key, val):
        if val is None or val <= 0:
            return
        if self.lw[eng].get(key, 0) >= val:
            return
        self.lw[eng][key] = val
        self.emit(eng, lambda E, sems, k=key, v=val: E.wait_ge(sems[k], v))


def build_nc():
    nc = bass.Bass("TRN2", target_bir_lowering=False, detect_race_conditions=False)

    u_t = nc.dram_tensor("u", [T, DM], F32, kind="ExternalInput")
    w1_t = nc.dram_tensor("w1", [NE, DM, FF], F32, kind="ExternalInput")
    w3_t = nc.dram_tensor("w3", [NE, DM, FF], F32, kind="ExternalInput")
    w2_t = nc.dram_tensor("w2", [NE, FF, DM], F32, kind="ExternalInput")
    wg_t = nc.dram_tensor("wg2", [128, 64], F32, kind="ExternalInput")
    a_t = nc.dram_tensor("a_v", [1, DS], F32, kind="ExternalInput")
    b_t = nc.dram_tensor("b_m", [DS, DM], F32, kind="ExternalInput")
    c_t = nc.dram_tensor("c_m", [DM, DS], F32, kind="ExternalInput")
    d_t = nc.dram_tensor("d2", [128, NC_DM], F32, kind="ExternalInput")
    ldt_t = nc.dram_tensor("logdt2", [128, NC_DM], F32, kind="ExternalInput")
    id_t = nc.dram_tensor("ident", [128, 128], F32, kind="ExternalInput")
    out_t = nc.dram_tensor("out", [T, DM], F32, kind="ExternalOutput")
    xnt_d = nc.dram_tensor("xnt_d", [T + 128, DM], BF16, kind="Internal")
    yac_d = nc.dram_tensor("yac_d", [T + 128, DM], F32, kind="Internal")
    tok_d = nc.dram_tensor("tok_d", [NSLOT + 128, 1], U32, kind="Internal")
    wt_d = nc.dram_tensor("wt_d", [T + 128, NE], F32, kind="Internal")

    from contextlib import ExitStack
    with ExitStack() as ctx:
        EC = ctx.enter_context
        BIG = EC(nc.sbuf_tensor("BIG", [128, 16384], F32))
        xnf = EC(nc.sbuf_tensor("xnf", [128, NC_DM * CW], BF16))
        w1r = EC(nc.sbuf_tensor("w1r", [128, RA * 2048], BF16))
        w3r = EC(nc.sbuf_tensor("w3r", [128, RA * 2048], BF16))
        w2r = EC(nc.sbuf_tensor("w2r", [128, RB * 2048], BF16))
        convb = EC(nc.sbuf_tensor("convb", [128, TH], F32))
        kscr = EC(nc.sbuf_tensor("kscr", [128, 1024], F32))
        tmp32 = EC(nc.sbuf_tensor("tmp32", [128, 1024], F32))
        kt = EC(nc.sbuf_tensor("kt", [128, NC_DM * LL], F32))
        bct = EC(nc.sbuf_tensor("bct", [128, NC_DM * DS], F32))
        pt_sb = EC(nc.sbuf_tensor("pt_sb", [128, NC_DM * DS], F32))
        c_sb = EC(nc.sbuf_tensor("c_sb", [128, NC_DM * DS], F32))
        a_b = EC(nc.sbuf_tensor("a_b", [128, DS], F32))
        dtc = EC(nc.sbuf_tensor("dtc", [128, NC_DM], F32))
        d_sb = EC(nc.sbuf_tensor("d_sb", [128, NC_DM], F32))
        ldt_sb = EC(nc.sbuf_tensor("ldt_sb", [128, NC_DM], F32))
        wg_sb = EC(nc.sbuf_tensor("wg_sb", [128, 64], F32))
        ident = EC(nc.sbuf_tensor("identt", [128, 128], F32))
        idb = EC(nc.sbuf_tensor("idb", [128, 128], BF16))
        diag = EC(nc.sbuf_tensor("diag", [128, 2 * 128], BF16))
        csr = EC(nc.sbuf_tensor("csr", [1, DS], F32))
        csc = EC(nc.sbuf_tensor("csc", [DS, 1], F32))
        ones_c = EC(nc.sbuf_tensor("ones_c", [128, 1], F32))
        ones_r = EC(nc.sbuf_tensor("ones_r", [1, 128], F32))
        stats = EC(nc.sbuf_tensor("stats", [128, 12], F32))
        aggr16 = EC(nc.sbuf_tensor("aggr16", [128, 32], F32))
        rstd16 = EC(nc.sbuf_tensor("rstd16", [128, 16], F32))
        sq16 = EC(nc.sbuf_tensor("sq16", [128, 16], F32))
        lgb = EC(nc.sbuf_tensor("lgb", [128, 8], F32))
        eps_sb = EC(nc.sbuf_tensor("eps_sb", [128, 1], F32))
        rsc = EC(nc.sbuf_tensor("rsc", [128, 48], F32))
        wtok = EC(nc.sbuf_tensor("wtok", [128, 128], F32))
        eqsav = EC(nc.sbuf_tensor("eqsav", [128, 256], F32))
        e12 = EC(nc.sbuf_tensor("e12", [128, 32], F32))
        rktok = EC(nc.sbuf_tensor("rktok", [128, 128], F32))
        iotap = EC(nc.sbuf_tensor("iotap", [128, 1], F32))
        iota8 = EC(nc.sbuf_tensor("iota8", [128, 8], F32))
        iotu = EC(nc.sbuf_tensor("iotu", [128, 49], U32))
        carry = EC(nc.sbuf_tensor("carry", [8, 2], F32))
        t8 = EC(nc.sbuf_tensor("t8", [128, 8], F32))
        scol = EC(nc.sbuf_tensor("scol", [128, 8], F32))
        sidx = EC(nc.sbuf_tensor("sidx", [128, 4], U32))
        toku = EC(nc.sbuf_tensor("toku", [128, 2], U32))
        idx_sb = EC(nc.sbuf_tensor("idx_sb", [128, 3 * NSC], U32))
        wsl = EC(nc.sbuf_tensor("wsl", [128, 2 * NSC * NE], F32))
        s_sb = EC(nc.sbuf_tensor("s_sb", [128, 2 * CAP], BF16))
        xgt = EC(nc.sbuf_tensor("xgt", [128, 3 * 2048], BF16))
        b_sb = convb
        big_addr = nc.lookup_mloc(BIG).addr

        def at(name, shape, dtype, off):
            return nc.alloc_sbuf_tensor_at(name, shape, dtype, offset=big_addr + off)
        g_sb = at("g_sb", [128, GF * CAP], BF16, 0)
        xgA = at("xgA", [128, NC_DM * CAP], BF16, 12288)
        ysl = at("ysl", [128, NSC * 1024], F32, 24576)
        ysw = at("ysw", [128, NSC * 1024], BF16, 49152)

        H1 = EC(nc.psum_tensor("H1", [128, 1024], F32))
        H3 = EC(nc.psum_tensor("H3", [128, 1024], F32))
        YS = EC(nc.psum_tensor("YS", [128, 1024], F32))
        TRA = EC(nc.psum_tensor("TRA", [128, 1024], BF16))
        TRB = EC(nc.psum_tensor("TRB", [128, 1024], BF16))

        sem_handles = {}
        for k in ("T", "V", "S", "DW", "GFS", "U", "P", "XT", "TI", "TK", "WD",
                  "GA", "GAF", "SC", "SCF", "XD", "XDF", "XO", "CA", "CAF",
                  "ZY", "FA", "IO", "TKF", "FL", "FL1", "FL2", "FL3"):
            sem_handles[k] = EC(nc.semaphore("sem" + k))

        S = _Sched()

        def vop(fn, waits=()):
            for k, v in waits:
                S.wait("vector", k, v)
            S.wait("vector", "V", S.nv)
            S.emit("vector", lambda E, sems, f=fn: f(E).then_inc(sems["V"], 1))
            S.nv += 1
            return S.nv

        def sop(fn, waits=()):
            for k, v in waits:
                S.wait("scalar", k, v)
            S.wait("scalar", "S", S.ns)
            S.emit("scalar", lambda E, sems, f=fn: f(E).then_inc(sems["S"], 1))
            S.ns += 1
            return S.ns

        def top(fn, waits=(), inc=False):
            for k, v in waits:
                S.wait("tensor", k, v)
            if inc:
                S.emit("tensor", lambda E, sems, f=fn: f(E).then_inc(sems["T"], 1))
                S.nt += 1
                return S.nt
            S.emit("tensor", lambda E, sems, f=fn: f(E))
            return None

        def dq(eng, fn, sem, waits=()):
            for k, v in waits:
                S.wait(eng, k, v)
            S.emit(eng, lambda E, sems, f=fn, sm=sem: f(E).then_inc(sems[sm], 16))
            S.dma[sem] = S.dma.get(sem, 0) + 16
            return S.dma[sem]

        def fence(eng, sem_dma, sem_f):
            S.wait(eng, sem_dma, S.dma.get(sem_dma, 0))
            S.emit(eng, lambda E, sems, sf=sem_f: E.sem_inc(sems[sf], 1))
            S.dma[sem_f] = S.dma.get(sem_f, 0) + 1
            return S.dma[sem_f]

        A = bass.AP

        dq("sync", lambda E: E.dma_start(
            out=A(BIG, 0, [[16384, 128], [1024, 16], [1, 1024]]),
            in_=A(u_t, 0, [[1024, 128], [131072, 16], [1, 1024]])), "U")
        for fn in (lambda E: E.dma_start(out=ident[:, :], in_=id_t[:, :]),
                   lambda E: E.dma_start(out=A(c_sb, 0, [[512, 128], [64, 8], [1, 64]]),
                                         in_=A(c_t, 0, [[64, 128], [8192, 8], [1, 64]])),
                   lambda E: E.dma_start(out=b_sb[0:DS, 0:DM], in_=b_t[:, :]),
                   lambda E: E.dma_start(out=A(a_b, 0, [[64, 1], [1, 64]]), in_=a_t[:, :]),
                   lambda E: E.dma_start(out=ldt_sb[:, :], in_=ldt_t[:, :]),
                   lambda E: E.dma_start(out=d_sb[:, :], in_=d_t[:, :]),
                   lambda E: E.dma_start(out=wg_sb[:, :], in_=wg_t[:, :])):
            dq("sync", fn, "P")
        v_par = S.dma["P"]

        dq("gpsimd", lambda E: E.dma_start(out=out_t[:, :], in_=u_t[:, :]), "XO")
        S.emit("gpsimd", lambda E, sems: E.iota(
            iotu[:, :], [[0, 49]], base=T, channel_multiplier=1,
            allow_small_or_imprecise_dtypes=True).then_inc(sems["IO"], 1))
        S.emit("gpsimd", lambda E, sems: E.iota(
            iotap[:, 0:1], [[0, 1]], base=0, channel_multiplier=1,
            allow_small_or_imprecise_dtypes=True).then_inc(sems["IO"], 1))
        S.emit("gpsimd", lambda E, sems: E.iota(
            iota8[:, :], [[1, 8]], base=0, channel_multiplier=0,
            allow_small_or_imprecise_dtypes=True).then_inc(sems["IO"], 1))
        S.emit("gpsimd", lambda E, sems: E.memset(
            tmp32[:, :], 0).then_inc(sems["IO"], 1))
        S.wait("gpsimd", "IO", 4)
        dq("gpsimd", lambda E: E.dma_start(
            out=A(tok_d, 0, [[1, 128], [128, 49]]),
            in_=A(iotu, 0, [[49, 128], [1, 49]]),
            allow_slow_non_contiguous=True), "TI")
        v_ti = 3
        dq("gpsimd", lambda E: E.dma_start(
            out=A(yac_d, 0, [[1024, 128], [131072, 17], [1, 1024]]),
            in_=A(tmp32, 0, [[1024, 128], [0, 17], [1, 1024]])), "ZY")
        dq("gpsimd", lambda E: E.dma_start(
            out=A(xnt_d, T * DM, [[1024, 128], [1, 1024]]),
            in_=A(tmp32, 0, [[1024, 128], [1, 1024]])), "ZY")
        dq("gpsimd", lambda E: E.dma_start(
            out=A(wt_d, T * NE, [[8, 128], [1, 8]]),
            in_=A(tmp32, 0, [[1024, 128], [1, 8]])), "ZY")
        zy_total = S.dma["ZY"]

        vop(lambda E: E.memset(ones_c[:, :], 1.0))
        vop(lambda E: E.memset(eps_sb[:, :], 1e-5))
        vop(lambda E: E.memset(ones_r[:, :], 1.0))
        vop(lambda E: E.memset(A(xnf, 0, [[NC_DM * CW, 128], [CW, NC_DM], [1, PADL]]), 0.0))
        v_const = vop(lambda E: E.memset(
            A(xnf, PADL + T, [[NC_DM * CW, 128], [CW, NC_DM], [1, PADR]]), 0.0))
        v_idb = vop(lambda E: E.tensor_copy(idb[:, :], ident[:, :]),
                    waits=[("P", v_par)])

        for cp in range(NC_DM):
            t_csum = top(lambda E, cp=cp: E.matmul(
                H1[0:1, cp * 64:(cp + 1) * 64], ones_c[:, 0:1],
                c_sb[:, cp * 64:(cp + 1) * 64], start=True, stop=True),
                waits=([("P", v_par), ("V", v_const)] if cp == 0 else ()),
                inc=(cp == NC_DM - 1))
        v_csr = vop(lambda E: E.tensor_reduce(
            csr[0:1, 0:DS], A(H1, 0, [[1024, 1], [1, DS], [64, NC_DM]]),
            axis=AX.X, op=OP.add), waits=[("T", t_csum)])
        t_csc = top(lambda E: E.transpose(H3[0:DS, 0:1], csr[0:1, 0:DS], ident[0:1, 0:1]),
                    waits=[("V", v_csr)], inc=True)
        v_csc = vop(lambda E: E.tensor_copy(csc[:, :], H3[0:DS, 0:1]), waits=[("T", t_csc)])
        v_bc = vop(lambda E: E.tensor_scalar(b_sb[0:DS, 0:DM], b_sb[0:DS, 0:DM],
                                             csc[:, 0:1], None, op0=OP.mult))
        for cp in range(NC_DM):
            t_bct = top(lambda E, cp=cp: E.transpose(
                H1[:, cp * 64:(cp + 1) * 64], b_sb[0:DS, cp * 128:(cp + 1) * 128],
                ident[0:DS, 0:DS]),
                waits=([("V", v_bc)] if cp == 0 else ()), inc=(cp == NC_DM - 1))
        v_bct = vop(lambda E: E.tensor_copy(bct[:, :], H1[:, 0:512]), waits=[("T", t_bct)])
        t_ab = top(lambda E: E.matmul(H3[:, 0:DS], ones_r[0:1, 0:128],
                                      A(a_b, 0, [[64, 1], [1, 64]]), start=True, stop=True),
                   waits=[("V", v_csc)], inc=True)
        v_ab = vop(lambda E: E.tensor_copy(a_b[:, :], H3[:, 0:DS]), waits=[("T", t_ab)])
        s_dtc = sop(lambda E: E.activation(dtc[:, :], ldt_sb[:, :], AF.Exp),
                    waits=[("P", v_par)])
        for c in range(NC_DM):
            v_pt = vop(lambda E, c=c: E.tensor_scalar(
                pt_sb[:, c * DS:(c + 1) * DS], a_b[:, :], dtc[:, c:c + 1], None,
                op0=OP.mult), waits=[("S", s_dtc)])
        v_ptdone = v_pt

        for tb in range(16):
            vop(lambda E, tb=tb: E.bn_stats(stats[:, 0:6], BIG[:, tb * 1024:tb * 1024 + 512]),
                waits=[("U", 16)])
            vop(lambda E, tb=tb: E.bn_stats(stats[:, 6:12],
                                            BIG[:, tb * 1024 + 512:tb * 1024 + 1024]))
            vop(lambda E, tb=tb: E.bn_aggr(aggr16[:, 2 * tb:2 * tb + 2], stats[:, 0:12]))
        v_aggr = S.nv
        for tb in range(16):
            sop(lambda E, tb=tb: E.activation(sq16[:, tb:tb + 1],
                                              aggr16[:, 2 * tb + 1:2 * tb + 2],
                                              AF.Sqrt, bias=eps_sb[:, 0:1]),
                waits=[("V", v_aggr)])
        s_sq = S.ns
        for tb in range(16):
            vop(lambda E, tb=tb: E.reciprocal(rstd16[:, tb:tb + 1], sq16[:, tb:tb + 1]),
                waits=[("S", s_sq)])
            vop(lambda E, tb=tb: E.tensor_scalar(
                BIG[:, tb * 1024:(tb + 1) * 1024], BIG[:, tb * 1024:(tb + 1) * 1024],
                aggr16[:, 2 * tb:2 * tb + 1], rstd16[:, tb:tb + 1],
                op0=OP.subtract, op1=OP.mult))
        v_lnapply = S.nv
        dq("gpsimd", lambda E: E.dma_start(
            out=A(xnt_d, 0, [[1024, 128], [131072, 16], [1, 1024]]),
            in_=A(BIG, 0, [[16384, 128], [1024, 16], [1, 1024]])), "XT",
            waits=[("V", v_lnapply)])

        ev_half = {}
        ev_clk = {}
        mm_clk = {}
        lg_clk = {}
        for tb in range(16):
            t_half = {}
            for half in range(2):
                for c in range(4 * half, 4 * half + 4):
                    w = [("V", v_lnapply)]
                    if tb > 0:
                        w.append(("V", ev_half[(tb - 1, half)]))
                    elif half == 0 and c == 0:
                        w.append(("V", v_ab))
                    top(lambda E, tb=tb, c=c: E.transpose(
                        H3[:, c * 128:(c + 1) * 128],
                        BIG[:, tb * 1024 + c * 128:tb * 1024 + (c + 1) * 128],
                        ident[:, :]), waits=(w if c == 4 * half else ()),
                        inc=(c == 4 * half + 3))
                t_half[half] = S.nt
            for half in range(2):
                for c in range(4 * half, 4 * half + 4):
                    vop(lambda E, tb=tb, c=c: E.tensor_copy(
                        xnf[:, c * CW + PADL + tb * 128:c * CW + PADL + (tb + 1) * 128],
                        H3[:, c * 128:(c + 1) * 128]), waits=[("T", t_half[half])])
                    ev_clk[(tb, c)] = vop(lambda E, tb=tb, c=c: E.tensor_copy(
                        tmp32[:, c * 128:(c + 1) * 128],
                        H3[:, c * 128:(c + 1) * 128]),
                        waits=([("ZY", zy_total)] if (tb, c) == (0, 0) else ()))
                ev_half[(tb, half)] = ev_clk[(tb, 4 * half + 3)]
            for c in range(NC_DM):
                w = [("V", ev_clk[(tb, c)]), ("P", v_par)]
                if c == 0 and tb > 0:
                    w.append(("V", lg_clk[tb - 1]))
                elif c == 0:
                    w.append(("V", v_bct))
                mm_clk[(tb, c)] = top(lambda E, tb=tb, c=c: E.matmul(
                    H1[:, 0:8], tmp32[:, c * 128:(c + 1) * 128],
                    wg_sb[:, c * 8:(c + 1) * 8], start=(c == 0), stop=(c == NC_DM - 1)),
                    waits=w, inc=True)
            t_logit_last = mm_clk[(tb, NC_DM - 1)]
            m1 = rsc[:, 0:1]; m2 = rsc[:, 1:2]; dd = rsc[:, 2:3]
            w1s = rsc[:, 3:4]; w2s = rsc[:, 4:5]
            l2 = rsc[:, 8:16]
            eq1 = eqsav[:, tb * 16:tb * 16 + 8]
            eq2 = eqsav[:, tb * 16 + 8:tb * 16 + 16]
            sc1 = rsc[:, 16:24]; sc2 = rsc[:, 24:32]
            lg_clk[tb] = vop(lambda E: E.tensor_copy(lgb[:, :], H1[:, 0:8]),
                             waits=[("T", t_logit_last)])
            vop(lambda E: E.reduce_max(m1, lgb[:, :], axis=AX.X))
            vop(lambda E, eq1=eq1: E.tensor_scalar(eq1, lgb[:, :], m1, None,
                                                   op0=OP.is_equal))
            vop(lambda E, eq1=eq1: E.scalar_tensor_tensor(l2, eq1, -1e30, lgb[:, :],
                                                          op0=OP.mult, op1=OP.add))
            vop(lambda E: E.reduce_max(m2, l2, axis=AX.X))
            vop(lambda E, eq2=eq2: E.tensor_scalar(eq2, l2, m2, None, op0=OP.is_equal))
            vop(lambda E, eq1=eq1: E.tensor_tensor(t8[:, :], eq1, iota8[:, :], op=OP.mult),
                waits=[("IO", v_ti)])
            vop(lambda E, tb=tb: E.tensor_reduce(e12[:, 2 * tb:2 * tb + 1], t8[:, :],
                                                 axis=AX.X, op=OP.add))
            vop(lambda E, eq2=eq2: E.tensor_tensor(t8[:, :], eq2, iota8[:, :], op=OP.mult))
            vop(lambda E, tb=tb: E.tensor_reduce(e12[:, 2 * tb + 1:2 * tb + 2], t8[:, :],
                                                 axis=AX.X, op=OP.add))
            v_d2 = vop(lambda E: E.tensor_tensor(dd, m1, m2, op=OP.subtract))
            s_sig = sop(lambda E: E.activation(w1s, dd, AF.Sigmoid), waits=[("V", v_d2)])
            vop(lambda E: E.tensor_scalar(w2s, w1s, -1.0, 1.0, op0=OP.mult, op1=OP.add),
                waits=[("S", s_sig)])
            vop(lambda E, eq1=eq1: E.tensor_scalar(sc1, eq1, w1s, None, op0=OP.mult))
            vop(lambda E, eq2=eq2: E.tensor_scalar(sc2, eq2, w2s, None, op0=OP.mult))
            vop(lambda E, tb=tb: E.tensor_tensor(wtok[:, tb * 8:(tb + 1) * 8],
                                                 sc1, sc2, op=OP.add))
        v_passE = S.nv
        t_passD = t_logit_last
        dq("sync", lambda E: E.dma_start(
            out=A(wt_d, 0, [[8, 128], [1024, 16], [1, 8]]),
            in_=A(wtok, 0, [[128, 128], [8, 16], [1, 8]])), "WD",
            waits=[("V", v_passE)])

        maskh = kscr[0:8, 0:1024]
        inclh = convb[0:8, 0:1024]
        v_mev = {}
        t_mtr = {}
        v_rkev = {}
        for h2 in range(2):
            v_t8v = {}
            for j in range(8):
                tb = h2 * 8 + j
                w = []
                if h2 == 1 and j == 0:
                    w.append(("T", t_mtr[(0, 1)]))
                v_t8v[j] = vop(lambda E, tb=tb, j=j: E.tensor_tensor(
                    tmp32[:, j * 8:(j + 1) * 8], eqsav[:, tb * 16:tb * 16 + 8],
                    eqsav[:, tb * 16 + 8:tb * 16 + 16], op=OP.add), waits=w)
            for jh in range(2):
                w = [("V", v_t8v[7]), ("T", t_passD)]
                if h2 == 1:
                    w.append(("V", v_mev[(0, 4 * jh + 3)]))
                for j in range(4 * jh, 4 * jh + 4):
                    top(lambda E, j=j: E.transpose(
                        A(YS, j * 128, [[1024, 8], [1, 128]]),
                        tmp32[:, j * 8:(j + 1) * 8], ident[:, :]),
                        waits=(w if j == 4 * jh else ()), inc=(j == 4 * jh + 3))
                t_mtr[(h2, jh)] = S.nt
                for j in range(4 * jh, 4 * jh + 4):
                    v_mev[(h2, j)] = vop(lambda E, j=j: E.tensor_copy(
                        maskh[0:8, j * 128:(j + 1) * 128],
                        A(YS, j * 128, [[1024, 8], [1, 128]])),
                        waits=[("T", t_mtr[(h2, jh)])])
            zb8 = ysl[0:8, 0:1024]
            vop(lambda E, zb8=zb8: E.memset(zb8, 0.0))
            if h2 == 0:
                vop(lambda E, zb8=zb8: E.tensor_tensor_scan(
                    inclh, maskh, zb8, 0.0, op0=OP.add, op1=OP.add))
            else:
                vop(lambda E, zb8=zb8: E.tensor_tensor_scan(
                    inclh, maskh, zb8, carry[:, 0:1], op0=OP.add, op1=OP.add))
            vop(lambda E: E.tensor_copy(carry[:, 0:1], inclh[0:8, 1023:1024]))
            v_sub = vop(lambda E: E.tensor_tensor(inclh, inclh, maskh, op=OP.subtract))
            for j in range(8):
                w = [("V", v_sub)]
                if h2 == 1 and j == 0:
                    w.append(("V", v_rkev[0]))
                top(lambda E, j=j: E.transpose(
                    A(YS, j * 8, [[1024, 128], [1, 8]]),
                    inclh[0:8, j * 128:(j + 1) * 128], ident[0:8, 0:8]),
                    waits=(w if j == 0 else ()), inc=(j == 7))
            t_rt = S.nt
            v_rkev[h2] = vop(lambda E, h2=h2: E.tensor_copy(
                rktok[:, h2 * 64:(h2 + 1) * 64],
                A(YS, 0, [[1024, 128], [1, 64]])), waits=[("T", t_rt)])
        tk_clk = {}
        n_sc = [0]
        for tb in range(16):
            for k2 in range(2):
                eqk = eqsav[:, tb * 16 + k2 * 8:tb * 16 + k2 * 8 + 8]
                ek = e12[:, 2 * tb + k2:2 * tb + k2 + 1]
                r = scol[:, 0:1]; sl = scol[:, 1:2]; ov = scol[:, 2:3]
                ot = scol[:, 3:4]
                vop(lambda E, eqk=eqk, tb=tb: E.tensor_tensor(
                    t8[:, :], eqk, rktok[:, tb * 8:(tb + 1) * 8], op=OP.mult))
                vop(lambda E: E.tensor_reduce(r, t8[:, :], axis=AX.X, op=OP.add))
                vop(lambda E, ek=ek: E.scalar_tensor_tensor(
                    sl, ek, float(CAP), r, op0=OP.mult, op1=OP.add))
                vop(lambda E: E.tensor_scalar(ov, r, float(CAP), None, op0=OP.is_ge))
                vop(lambda E: E.tensor_scalar(ot, ov, -1.0, 1.0, op0=OP.mult, op1=OP.add))
                vop(lambda E: E.tensor_tensor(sl, sl, ot, op=OP.mult))
                vop(lambda E: E.tensor_scalar(t8[:, 0:1], iotap[:, 0:1], 1.0,
                                              float(NSLOT), op0=OP.mult, op1=OP.add))
                vop(lambda E: E.tensor_tensor(t8[:, 0:1], t8[:, 0:1], ov, op=OP.mult))
                vop(lambda E: E.tensor_tensor(sl, sl, t8[:, 0:1], op=OP.add))
                s_col = (2 * tb + k2) % 4
                v_si = vop(lambda E, s_col=s_col: E.tensor_copy(
                    sidx[:, s_col:s_col + 1], sl),
                    waits=[("TKF", max(0, 2 * tb + k2 - 3))])
                if k2 == 0:
                    vop(lambda E, tb=tb: E.tensor_scalar(
                        scol[:, 5:6], iotap[:, 0:1], float(tb * 128), None, op0=OP.add))
                    vop(lambda E, tb=tb: E.tensor_copy(
                        toku[:, tb % 2:tb % 2 + 1], scol[:, 5:6]),
                        waits=[("TKF", max(0, 2 * tb - 2))])
                v_tu = S.nv
                w_tk = [("V", v_tu), ("TI", 16)]
                if (tb, k2) == (0, 0):
                    w_tk += [("XO", 16), ("ZY", zy_total), ("XT", 16)]
                tk_clk[(tb, k2)] = dq("gpsimd", lambda E, s_col=s_col, tb=tb:
                                      E.indirect_dma_start(
                    out=tok_d[:, :],
                    out_offset=IndirectOffsetOnAxis(ap=sidx[:, s_col:s_col + 1], axis=0),
                    in_=toku[:, tb % 2:tb % 2 + 1], in_offset=None), "TK",
                    waits=w_tk)
                fence("gpsimd", "TK", "TKF")
                n_sc[0] += 1
        v_passF = S.nv
        tk_total = S.dma["TK"]

        fg = 0
        silu_clk = {}
        gmul_clk = {}
        kred = {}
        freeA = {}
        freeB = {}
        extra_h1 = [0]
        extra_h3 = [0]
        extra_ys = [0]
        hend = {}
        yend_v = {}
        yel = {}
        kstate = {"e1": None}

        def emit_kstep(l):
            E1s = kscr[:, 0:512]
            CH = kscr[:, 512:1024]
            M = convb[:, 0:512]
            if l == 0:
                kred[0] = vop(lambda E: E.tensor_reduce(
                    A(kt, 0, [[NC_DM * LL, 128], [LL, NC_DM], [1, 1]]),
                    A(bct, 0, [[512, 128], [64, 8], [1, DS]]),
                    axis=AX.X, op=OP.add),
                    waits=[("V", v_bct), ("V", v_passF)])
                kstate["e1"] = sop(lambda E: E.activation(E1s, pt_sb[:, :], AF.Exp),
                                   waits=[("V", v_ptdone), ("V", v_passF)])
                return
            if l == 1:
                cur = E1s
            elif l == 2:
                vop(lambda E: E.tensor_tensor(CH, E1s, E1s, op=OP.mult),
                    waits=[("S", kstate["e1"])])
                cur = CH
            else:
                vop(lambda E: E.tensor_tensor(CH, CH, E1s, op=OP.mult))
                cur = CH
            vop(lambda E, cur=cur: E.tensor_tensor(M, cur, bct[:, :], op=OP.mult),
                waits=([("S", kstate["e1"])] if l == 1 else []))
            kred[l] = vop(lambda E, l=l: E.tensor_reduce(
                A(kt, l, [[NC_DM * LL, 128], [LL, NC_DM], [1, 1]]),
                A(convb, 0, [[1024, 128], [64, 8], [1, DS]]),
                axis=AX.X, op=OP.add))

        def emit_conv():
            conv_ev = {}
            caf_prev = [0]
            ev_bank = {0: 0, 1: 0}
            for c in range(NC_DM):
                dg_clk = {}
                cmm_clk = {}
                for l in range(LL):
                    w = [("V", kred[LL - 1])]
                    if l >= 2:
                        w.append(("T", cmm_clk[l - 2]))
                    dg_clk[l] = vop(lambda E, c=c, l=l: E.tensor_scalar(
                        A(diag, (l % 2) * 128, [[256, 128], [1, 128]]),
                        ident[:, :], kt[:, c * LL + l:c * LL + l + 1], None,
                        op0=OP.mult), waits=w)
                    tw = [("V", dg_clk[l])]
                    if l == 0 and c > 0:
                        tw.append(("V", conv_ev[(c - 1, 1)]))
                    elif l == 0:
                        tw.append(("S", silu_clk[fg - 1]))
                        tw.append(("V", gmul_clk[fg - 1]))
                    first = True
                    for h in range(2):
                        PS = H1 if h == 0 else H3
                        for tc in range(2):
                            top(lambda E, c=c, l=l, tc=tc, h=h, PS=PS: E.matmul(
                                PS[:, tc * 512:(tc + 1) * 512],
                                A(diag, (l % 2) * 128, [[256, 128], [1, 128]]),
                                xnf[:, c * CW + h * TH + tc * 512 + 63 - l:
                                    c * CW + h * TH + tc * 512 + 63 - l + 512],
                                start=(l == 0), stop=(l == LL - 1)),
                                waits=(tw if first else ()),
                                inc=(h == 1 and tc == 1))
                            first = False
                    cmm_clk[l] = S.nt
                for h in range(2):
                    PS = H1 if h == 0 else H3
                    conv_ev[(c, h)] = vop(lambda E, c=c, h=h, PS=PS:
                                          E.scalar_tensor_tensor(
                        convb[:, :],
                        xnf[:, c * CW + PADL + h * TH:c * CW + PADL + (h + 1) * TH],
                        d_sb[:, c:c + 1], PS[:, :], op0=OP.mult, op1=OP.add),
                        waits=[("T", cmm_clk[LL - 1])])
                    tw = [("V", conv_ev[(c, h)])]
                    if c == 0 and h == 0:
                        tw.append(("V", yel[1]))
                    if ev_bank[0]:
                        tw.append(("V", ev_bank[0]))
                    for tb in range(4):
                        top(lambda E, tb=tb: E.transpose(
                            YS[:, tb * 128:(tb + 1) * 128],
                            convb[:, tb * 128:(tb + 1) * 128], ident[:, :]),
                            waits=(tw if tb == 0 else ()), inc=(tb == 3))
                    t_b0 = S.nt
                    for tb in range(4, 8):
                        top(lambda E, tb=tb: E.transpose(
                            YS[:, tb * 128:(tb + 1) * 128],
                            convb[:, tb * 128:(tb + 1) * 128], ident[:, :]),
                            waits=([("V", ev_bank[1])]
                                   if (tb == 4 and ev_bank[1]) else ()),
                            inc=(tb == 7))
                    t_b1 = S.nt
                    w2 = [("T", t_b0)]
                    if caf_prev[0]:
                        w2.append(("CAF", caf_prev[0]))
                    ev_bank[0] = vop(lambda E: E.tensor_copy(
                        tmp32[:, 0:512], YS[:, 0:512]), waits=w2)
                    ev_bank[1] = vop(lambda E: E.tensor_copy(
                        tmp32[:, 512:1024], YS[:, 512:1024]), waits=[("T", t_b1)])
                    v_ev = ev_bank[1]
                    dq("gpsimd", lambda E, c=c, h=h: E.dma_start(
                        out=A(out_t, h * TH * DM + c * 128,
                              [[1024, 128], [131072, 8], [1, 128]]),
                        in_=A(tmp32, 0, [[1024, 128], [128, 8], [1, 128]]),
                        accum_op=OP.add), "CA",
                        waits=[("V", v_ev), ("XO", 16)])
                    caf_prev[0] = fence("gpsimd", "CA", "CAF")
            extra_h1[0] = conv_ev[(NC_DM - 1, 0)]
            extra_h3[0] = conv_ev[(NC_DM - 1, 1)]
            extra_ys[0] = S.nt

        def emit_wgroup(e, g):
            segi = e * NG + g
            for j in range(4):
                gsA = segi * 4 + j
                pA = gsA % RA
                pB = gsA % RB
                fcol = g * 1024 + j * 256
                if gsA >= RA:
                    S.wait("gpsimd", "T", freeA[gsA - RA])
                dq("gpsimd", lambda E, e=e, fcol=fcol, pA=pA: E.dma_start(
                    out=A(w1r, pA * 2048, [[RA * 2048, 128], [256, 8], [1, 256]]),
                    in_=A(w1_t, e * DM * FF + fcol,
                          [[FF, 128], [128 * FF, 8], [1, 256]])), "DW")
                dq("gpsimd", lambda E, e=e, fcol=fcol, pA=pA: E.dma_start(
                    out=A(w3r, pA * 2048, [[RA * 2048, 128], [256, 8], [1, 256]]),
                    in_=A(w3_t, e * DM * FF + fcol,
                          [[FF, 128], [128 * FF, 8], [1, 256]])), "DW")
                if gsA >= RB:
                    S.wait("gpsimd", "T", freeB[gsA - RB])
                dq("gpsimd", lambda E, e=e, fcol=fcol, pB=pB: E.dma_start(
                    out=A(w2r, pB * 2048, [[RB * 2048, 128], [1024, 2], [1, 1024]]),
                    in_=A(w2_t, e * FF * DM + fcol * DM,
                          [[DM, 128], [128 * DM, 2], [1, 1024]])), "DW")
                S.wait("gpsimd", "DW", S.dma["DW"])
                S.emit("gpsimd", lambda E, sems: E.sem_inc(sems["GFS"], 1))
                S.dma["GFS"] = S.dma.get("GFS", 0) + 1

        gaf_pair = {}
        xg_ev = {}
        v_xg = {}
        tr_last = {}
        scf_of = {}

        def xg_of(ee):
            return xgA

        def emit_prefetch(ee):
            ib = (ee % 3) * NSC
            for i in range(NSC):
                dq("sync", lambda E, ee=ee, i=i, ib=ib: E.dma_start(
                    out=A(idx_sb, ib + i, [[3 * NSC, 128], [1, 1]]),
                    in_=A(tok_d, ee * CAP + i * 128, [[1, 128], [1, 1]])), "XD",
                    waits=([("TK", tk_total), ("GAF", S.dma.get("GAF", 0)),
                            ("SCF", S.dma.get("SCF", 0))]
                           if i == 0 else []))
            S.wait("sync", "XD", S.dma["XD"])
            S.emit("sync", lambda E, sems: E.sem_inc(sems["XDF"], 1))
            S.dma["XDF"] = S.dma.get("XDF", 0) + 1
            xdf_e = S.dma["XDF"]
            gw = [("XDF", xdf_e), ("XT", 16), ("WD", 16)]
            if ee >= 1:
                gw.append(("T", tr_last[ee - 1]))
            if ee >= 2:
                gw.append(("V", yend_v[ee - 2]))
            wb = (ee % 2) * NSC * NE
            for pr in range(3):
                for q in range(2):
                    i = pr * 2 + q
                    dq("gpsimd", lambda E, pr=pr, q=q, i=i, ib=ib:
                       E.indirect_dma_start(
                        out=A(xgt, pr * 2048 + q * 1024, [[6144, 128], [1, 1024]]),
                        out_offset=None,
                        in_=xnt_d[:, :],
                        in_offset=IndirectOffsetOnAxis(
                            ap=idx_sb[:, ib + i:ib + i + 1], axis=0)),
                        "GA", waits=gw)
                    gw = []
                for q in range(2):
                    i = pr * 2 + q
                    dq("gpsimd", lambda E, i=i, ib=ib, wb=wb:
                       E.indirect_dma_start(
                        out=wsl[:, wb + i * 8:wb + (i + 1) * 8], out_offset=None,
                        in_=wt_d[:, :],
                        in_offset=IndirectOffsetOnAxis(
                            ap=idx_sb[:, ib + i:ib + i + 1], axis=0)),
                        "GA")
                gaf_pair[(ee, pr)] = fence("gpsimd", "GA", "GAF")

        def emit_transposes(ee):
            xge = xg_of(ee)
            for pr in range(3):
                for q in range(2):
                    i = pr * 2 + q
                    TR = TRA if i % 2 == 0 else TRB
                    w = [("GAF", gaf_pair[(ee, pr)]), ("V", v_idb)]
                    if i >= 2:
                        w.append(("V", xg_ev[(ee, i - 2)]))
                    elif ee >= 1:
                        w.append(("V", xg_ev[(ee - 1, 4 + i)]))
                    for c in range(NC_DM):
                        top(lambda E, pr=pr, q=q, c=c, TR=TR: E.transpose(
                            TR[:, c * 128:(c + 1) * 128],
                            A(xgt, pr * 2048 + q * 1024 + c * 128,
                              [[6144, 128], [1, 128]]),
                            idb[:, :]), waits=(w if c == 0 else ()), inc=(c == 7))
                        w = ()
                    t_tr = S.nt
                    xg_ev[(ee, i)] = vop(lambda E, i=i, TR=TR, xge=xge:
                                         E.tensor_copy(
                        A(xge, i * 128, [[NC_DM * CAP, 128], [CAP, NC_DM], [1, 128]]),
                        A(TR, 0, [[1024, 128], [128, 8], [1, 128]])),
                        waits=[("T", t_tr)])
            tr_last[ee] = S.nt
            v_xg[ee] = xg_ev[(ee, 5)]

        for e in range(NE):
            if e == 0:
                emit_wgroup(0, 0)
            if e > 0:
                ibp = ((e - 1) % 3) * NSC
                for sc in range(NSC):
                    dq("gpsimd", lambda E, sc=sc, ibp=ibp: E.indirect_dma_start(
                        out=yac_d[:, :],
                        out_offset=IndirectOffsetOnAxis(
                            ap=idx_sb[:, ibp + sc:ibp + sc + 1], axis=0),
                        in_=ysw[:, sc * 1024:(sc + 1) * 1024],
                        in_offset=None, compute_op=OP.add), "SC",
                        waits=[("V", yend_v[e - 1]), ("ZY", zy_total),
                               ("SC", S.dma.get("SC", 0))])
                scf_of[e - 1] = fence("gpsimd", "SC", "SCF")
            if e == 0:
                emit_prefetch(0)
            emit_transposes(e)
            xg = xg_of(e)
            ysl_clk = 0
            for g in range(NG):
                segi = e * NG + g
                for fl in range(GF):
                    gsA = segi * 4 + fl // 2
                    pA = gsA % RA
                    scol_w = pA * 2048 + (fl % 2) * 128
                    w = [("GFS", segi * 4 + fl // 2 + 1), ("V", v_xg[e])]
                    if fg > 0:
                        w.append(("S", silu_clk[fg - 1]))
                    if extra_h1[0]:
                        w.append(("V", extra_h1[0])); extra_h1[0] = 0
                    for k in range(NC_DM):
                        for nc2 in range(2):
                            n0 = nc2 * 512
                            nn = 512 if nc2 == 0 else CAP - 512
                            top(lambda E, k=k, n0=n0, nn=nn, sw=scol_w,
                                xg=xg: E.matmul(
                                H1[:, n0:n0 + nn],
                                w1r[:, sw + k * 256:sw + k * 256 + 128],
                                xg[:, k * CAP + n0:k * CAP + n0 + nn],
                                start=(k == 0), stop=(k == NC_DM - 1)),
                                waits=(w if (k == 0 and nc2 == 0) else ()),
                                inc=(k == NC_DM - 1 and nc2 == 1))
                    t_h1 = S.nt
                    w = []
                    if fg > 0:
                        w.append(("V", gmul_clk[fg - 1]))
                    if extra_h3[0]:
                        w.append(("V", extra_h3[0])); extra_h3[0] = 0
                    for k in range(NC_DM):
                        for nc2 in range(2):
                            n0 = nc2 * 512
                            nn = 512 if nc2 == 0 else CAP - 512
                            top(lambda E, k=k, n0=n0, nn=nn, sw=scol_w,
                                xg=xg: E.matmul(
                                H3[:, n0:n0 + nn],
                                w3r[:, sw + k * 256:sw + k * 256 + 128],
                                xg[:, k * CAP + n0:k * CAP + n0 + nn],
                                start=(k == 0), stop=(k == NC_DM - 1)),
                                waits=(w if (k == 0 and nc2 == 0) else ()),
                                inc=(k == NC_DM - 1 and nc2 == 1))
                    t_h3 = S.nt
                    if fl % 2 == 1:
                        freeA[gsA] = t_h3
                    hend[e] = t_h3
                    w = [("T", t_h1)]
                    if fg >= 2:
                        w.append(("V", gmul_clk[fg - 2]))
                    silu_clk[fg] = sop(lambda E, fl=fl: E.activation(
                        s_sb[:, (fl % 2) * CAP:(fl % 2 + 1) * CAP],
                        H1[:, 0:CAP], AF.Silu), waits=w)
                    w = [("T", t_h3), ("S", silu_clk[fg])]
                    if ysl_clk:
                        w.append(("T", ysl_clk))
                    gmul_clk[fg] = vop(lambda E, fl=fl: E.tensor_tensor(
                        g_sb[:, fl * CAP:(fl + 1) * CAP],
                        s_sb[:, (fl % 2) * CAP:(fl % 2 + 1) * CAP],
                        H3[:, 0:CAP], op=OP.mult), waits=w)
                    if e < 2 and fg < LL:
                        emit_kstep(fg)
                    fg += 1
                for sc in range(NSC):
                    w = [("V", gmul_clk[fg - 1])]
                    if extra_ys[0]:
                        w.append(("T", extra_ys[0])); extra_ys[0] = 0
                    if (g, sc) != (0, 0):
                        w.append(("V", S.nv))
                    for fl in range(GF):
                        pB = (segi * 4 + fl // 2) % RB
                        for dmh in range(2):
                            top(lambda E, fl=fl, dmh=dmh, sc=sc, pB=pB: E.matmul(
                                YS[:, dmh * 512:(dmh + 1) * 512],
                                g_sb[:, fl * CAP + sc * 128:fl * CAP + (sc + 1) * 128],
                                w2r[:, pB * 2048 + (fl % 2) * 1024 + dmh * 512:
                                    pB * 2048 + (fl % 2) * 1024 + dmh * 512 + 512],
                                start=(fl == 0), stop=(fl == GF - 1)),
                                waits=(w if (fl == 0 and dmh == 0) else ()),
                                inc=(fl == GF - 1 and dmh == 1))
                    t_y = S.nt
                    ysl_clk = t_y
                    if g == 0:
                        vop(lambda E, sc=sc: E.tensor_copy(
                            ysl[:, sc * 1024:(sc + 1) * 1024], YS[:, :]),
                            waits=[("T", t_y)])
                    else:
                        vop(lambda E, sc=sc: E.tensor_tensor(
                            ysl[:, sc * 1024:(sc + 1) * 1024],
                            ysl[:, sc * 1024:(sc + 1) * 1024], YS[:, :], op=OP.add),
                            waits=[("T", t_y)])
                for j in range(4):
                    freeB[segi * 4 + j] = ysl_clk
                if g == 0 and e + 1 < NE:
                    emit_prefetch(e + 1)
                if g < NG - 1:
                    emit_wgroup(e, g + 1)
                elif e + 1 < NE:
                    emit_wgroup(e + 1, 0)
            yel[e] = S.nv
            for sc in range(NSC):
                wb = (e % 2) * NSC * NE
                wgs = ([("SCF", scf_of[e - 1])] if (e > 0 and sc == 0) else [])
                yend_v[e] = vop(lambda E, sc=sc, e=e, wb=wb: E.tensor_scalar(
                    ysw[:, sc * 1024:(sc + 1) * 1024],
                    ysl[:, sc * 1024:(sc + 1) * 1024],
                    wsl[:, wb + sc * 8 + e:wb + sc * 8 + e + 1], None,
                    op0=OP.mult), waits=wgs)
            if e == 1:
                emit_conv()
        ib7 = ((NE - 1) % 3) * NSC
        for sc in range(NSC):
            dq("gpsimd", lambda E, sc=sc: E.indirect_dma_start(
                out=yac_d[:, :],
                out_offset=IndirectOffsetOnAxis(
                    ap=idx_sb[:, ib7 + sc:ib7 + sc + 1], axis=0),
                in_=ysw[:, sc * 1024:(sc + 1) * 1024],
                in_offset=None, compute_op=OP.add), "SC",
                waits=([("V", yend_v[NE - 1]), ("SC", S.dma.get("SC", 0))]
                       if sc == 0 else []))
        S.wait("gpsimd", "SC", S.dma["SC"])
        S.wait("gpsimd", "CA", S.dma.get("CA", 0))
        S.wait("gpsimd", "XO", 16)
        def fl_ap(sl):
            if sl < 3:
                return [tmp32, kscr, convb][sl][:, :]
            return A(BIG, 0, [[16384, 128], [1, 1024]])
        fsem = ["FL", "FL1", "FL2", "FL3"]

        def emit_fl(c):
            slq = c % 4
            w = [("FA", 16 * (c - 3))] if c >= 4 else []
            dq("gpsimd", lambda E, c=c, slq=slq: E.dma_start(
                out=fl_ap(slq),
                in_=yac_d[c * 128:(c + 1) * 128, :]), fsem[slq], waits=w)

        for c in range(4):
            emit_fl(c)
        for c in range(16):
            slq = c % 4
            dq("gpsimd", lambda E, c=c, slq=slq: E.dma_start(
                out=out_t[c * 128:(c + 1) * 128, :],
                in_=fl_ap(slq),
                accum_op=OP.add), "FA",
                waits=[(fsem[slq], 16 * (c // 4 + 1))])
            if c + 4 < 16:
                emit_fl(c + 4)
        S.wait("gpsimd", "FA", 256)


        with nc.Block() as block:
            @block.sync
            def _(E):
                for fn in S.ops["sync"]:
                    fn(E, sem_handles)

            @block.gpsimd
            def _(E):
                for fn in S.ops["gpsimd"]:
                    fn(E, sem_handles)

            @block.scalar
            def _(E):
                for fn in S.ops["scalar"]:
                    fn(E, sem_handles)

            @block.vector
            def _(E):
                for fn in S.ops["vector"]:
                    fn(E, sem_handles)

            @block.tensor
            def _(E):
                for fn in S.ops["tensor"]:
                    fn(E, sem_handles)
    return nc


def _layouts(inputs):
    Wg = np.ascontiguousarray(np.asarray(inputs["L0_ffn_Wg"], np.float32))
    wg2 = np.ascontiguousarray(Wg.reshape(8, 128, 8).transpose(1, 0, 2).reshape(128, 64))
    d2 = np.ascontiguousarray(np.asarray(inputs["L0_attn_D"], np.float32).reshape(8, 128).T)
    ldt2 = np.ascontiguousarray(
        np.asarray(inputs["L0_attn_log_dt"], np.float32).reshape(8, 128).T)
    ident = np.eye(128, dtype=np.float32)
    return wg2, d2, ldt2, ident


def kernel(L0_attn_u, L0_attn_A, L0_attn_B, L0_attn_C, L0_attn_D, L0_attn_log_dt,
           L0_ffn_Wg, L0_ffn_W1, L0_ffn_W3, L0_ffn_W2):
    u = np.ascontiguousarray(np.asarray(L0_attn_u, np.float32))
    W1 = np.ascontiguousarray(np.asarray(L0_ffn_W1, np.float32))
    W3 = np.ascontiguousarray(np.asarray(L0_ffn_W3, np.float32))
    W2 = np.ascontiguousarray(np.asarray(L0_ffn_W2, np.float32))
    Am = np.ascontiguousarray(np.asarray(L0_attn_A, np.float32).reshape(1, DS))
    Bm = np.ascontiguousarray(np.asarray(L0_attn_B, np.float32))
    Cm = np.ascontiguousarray(np.asarray(L0_attn_C, np.float32))
    ins = {"L0_ffn_Wg": L0_ffn_Wg, "L0_attn_D": L0_attn_D,
           "L0_attn_log_dt": L0_attn_log_dt}
    wg2, d2, ldt2, ident = _layouts(ins)
    nc = build_nc()
    in_maps = []
    for b in range(NCORES):
        in_maps.append({
            "u": np.ascontiguousarray(u[b]),
            "w1": W1, "w3": W3, "w2": W2, "wg2": wg2,
            "a_v": Am, "b_m": Bm, "c_m": Cm, "d2": d2, "logdt2": ldt2,
            "ident": ident,
        })
    res = run_bass_kernel_spmd(nc, in_maps, core_ids=list(range(NCORES)))
    out = np.stack([np.asarray(res.results[b]["out"]) for b in range(NCORES)], axis=0)
    return out


if __name__ == "__main__":
    nc = build_nc()
    print("build ok")


# version 10
# speedup vs baseline: 1.0389x; 1.0389x over previous
import numpy as np
import concourse.bass as bass
import concourse.mybir as mybir
from concourse.bass_utils import run_bass_kernel_spmd
from concourse.bass_core_rust import IndirectOffsetOnAxis

F32 = mybir.dt.float32
BF16 = mybir.dt.bfloat16
U32 = mybir.dt.uint32
AF = mybir.ActivationFunctionType
OP = mybir.AluOpType
AX = mybir.AxisListType

NCORES = 8
T = 2048
DM = 1024
DS = 64
LL = 64
FF = 4096
NE = 8
TH = 1024
NC_DM = 8
NG = 4
GF = 8
RA = 6
RB = 5
CAP = 768
NSC = CAP // 128
NSLOT = NE * CAP
PADL, PADR = 32, 31
CW = PADL + T + PADR


class _Sched:
    def __init__(self):
        self.ops = {k: [] for k in ("sync", "gpsimd", "scalar", "vector", "tensor")}
        self.nv = 0
        self.ns = 0
        self.nt = 0
        self.lw = {k: {} for k in self.ops}
        self.dma = {}

    def emit(self, eng, fn):
        self.ops[eng].append(fn)

    def wait(self, eng, key, val):
        if val is None or val <= 0:
            return
        if self.lw[eng].get(key, 0) >= val:
            return
        self.lw[eng][key] = val
        self.emit(eng, lambda E, sems, k=key, v=val: E.wait_ge(sems[k], v))


def build_nc():
    nc = bass.Bass("TRN2", target_bir_lowering=False, detect_race_conditions=False)

    u_t = nc.dram_tensor("u", [T, DM], F32, kind="ExternalInput")
    w1_t = nc.dram_tensor("w1", [NE, DM, FF], F32, kind="ExternalInput")
    w3_t = nc.dram_tensor("w3", [NE, DM, FF], F32, kind="ExternalInput")
    w2_t = nc.dram_tensor("w2", [NE, FF, DM], F32, kind="ExternalInput")
    wg_t = nc.dram_tensor("wg2", [128, 64], F32, kind="ExternalInput")
    a_t = nc.dram_tensor("a_v", [1, DS], F32, kind="ExternalInput")
    b_t = nc.dram_tensor("b_m", [DS, DM], F32, kind="ExternalInput")
    c_t = nc.dram_tensor("c_m", [DM, DS], F32, kind="ExternalInput")
    d_t = nc.dram_tensor("d2", [128, NC_DM], F32, kind="ExternalInput")
    ldt_t = nc.dram_tensor("logdt2", [128, NC_DM], F32, kind="ExternalInput")
    id_t = nc.dram_tensor("ident", [128, 128], F32, kind="ExternalInput")
    out_t = nc.dram_tensor("out", [T, DM], F32, kind="ExternalOutput")
    xnt_d = nc.dram_tensor("xnt_d", [T + 128, DM], BF16, kind="Internal")
    yac_d = nc.dram_tensor("yac_d", [T + 128, DM], F32, kind="Internal")
    tok_d = nc.dram_tensor("tok_d", [NSLOT + 128, 1], U32, kind="Internal")
    wt_d = nc.dram_tensor("wt_d", [T + 128, NE], F32, kind="Internal")

    from contextlib import ExitStack
    with ExitStack() as ctx:
        EC = ctx.enter_context
        BIG = EC(nc.sbuf_tensor("BIG", [128, 16384], F32))
        xnf = EC(nc.sbuf_tensor("xnf", [128, NC_DM * CW], BF16))
        w1r = EC(nc.sbuf_tensor("w1r", [128, RA * 2048], BF16))
        w3r = EC(nc.sbuf_tensor("w3r", [128, RA * 2048], BF16))
        w2r = EC(nc.sbuf_tensor("w2r", [128, RB * 2048], BF16))
        convb = EC(nc.sbuf_tensor("convb", [128, TH], F32))
        kscr = EC(nc.sbuf_tensor("kscr", [128, 1024], F32))
        tmp32 = EC(nc.sbuf_tensor("tmp32", [128, 1024], F32))
        kt = EC(nc.sbuf_tensor("kt", [128, NC_DM * LL], F32))
        bct = EC(nc.sbuf_tensor("bct", [128, NC_DM * DS], F32))
        pt_sb = EC(nc.sbuf_tensor("pt_sb", [128, NC_DM * DS], F32))
        c_sb = EC(nc.sbuf_tensor("c_sb", [128, NC_DM * DS], F32))
        a_b = EC(nc.sbuf_tensor("a_b", [128, DS], F32))
        dtc = EC(nc.sbuf_tensor("dtc", [128, NC_DM], F32))
        d_sb = EC(nc.sbuf_tensor("d_sb", [128, NC_DM], F32))
        ldt_sb = EC(nc.sbuf_tensor("ldt_sb", [128, NC_DM], F32))
        wg_sb = EC(nc.sbuf_tensor("wg_sb", [128, 64], F32))
        ident = EC(nc.sbuf_tensor("identt", [128, 128], F32))
        idb = EC(nc.sbuf_tensor("idb", [128, 128], BF16))
        diag = EC(nc.sbuf_tensor("diag", [128, 2 * 128], BF16))
        csr = EC(nc.sbuf_tensor("csr", [1, DS], F32))
        csc = EC(nc.sbuf_tensor("csc", [DS, 1], F32))
        ones_c = EC(nc.sbuf_tensor("ones_c", [128, 1], F32))
        ones_r = EC(nc.sbuf_tensor("ones_r", [1, 128], F32))
        stats = EC(nc.sbuf_tensor("stats", [128, 12], F32))
        aggr16 = EC(nc.sbuf_tensor("aggr16", [128, 32], F32))
        rstd16 = EC(nc.sbuf_tensor("rstd16", [128, 16], F32))
        sq16 = EC(nc.sbuf_tensor("sq16", [128, 16], F32))
        lgb = EC(nc.sbuf_tensor("lgb", [128, 8], F32))
        eps_sb = EC(nc.sbuf_tensor("eps_sb", [128, 1], F32))
        rsc = EC(nc.sbuf_tensor("rsc", [128, 48], F32))
        wtok = EC(nc.sbuf_tensor("wtok", [128, 128], F32))
        eqsav = EC(nc.sbuf_tensor("eqsav", [128, 256], F32))
        e12 = EC(nc.sbuf_tensor("e12", [128, 32], F32))
        rktok = EC(nc.sbuf_tensor("rktok", [128, 128], F32))
        iotap = EC(nc.sbuf_tensor("iotap", [128, 1], F32))
        iota8 = EC(nc.sbuf_tensor("iota8", [128, 8], F32))
        iotu = EC(nc.sbuf_tensor("iotu", [128, 49], U32))
        carry = EC(nc.sbuf_tensor("carry", [8, 2], F32))
        t8 = EC(nc.sbuf_tensor("t8", [128, 8], F32))
        scol = EC(nc.sbuf_tensor("scol", [128, 8], F32))
        sidx = EC(nc.sbuf_tensor("sidx", [128, 4], U32))
        toku = EC(nc.sbuf_tensor("toku", [128, 2], U32))
        tki = EC(nc.sbuf_tensor("tki", [128, 32], U32))
        idx_sb = EC(nc.sbuf_tensor("idx_sb", [128, 3 * NSC], U32))
        wsl = EC(nc.sbuf_tensor("wsl", [128, 2 * NSC * NE], F32))
        s_sb = EC(nc.sbuf_tensor("s_sb", [128, 2 * CAP], BF16))
        xgt = EC(nc.sbuf_tensor("xgt", [128, 3 * 2048], BF16))
        b_sb = convb
        big_addr = nc.lookup_mloc(BIG).addr

        def at(name, shape, dtype, off):
            return nc.alloc_sbuf_tensor_at(name, shape, dtype, offset=big_addr + off)
        g_sb = at("g_sb", [128, GF * CAP], BF16, 0)
        xgA = at("xgA", [128, NC_DM * CAP], BF16, 12288)
        ysl = at("ysl", [128, NSC * 1024], F32, 24576)
        ysw = at("ysw", [128, NSC * 1024], BF16, 49152)

        H1 = EC(nc.psum_tensor("H1", [128, 1024], F32))
        H3 = EC(nc.psum_tensor("H3", [128, 1024], F32))
        YS = EC(nc.psum_tensor("YS", [128, 1024], F32))
        TRA = EC(nc.psum_tensor("TRA", [128, 1024], BF16))
        TRB = EC(nc.psum_tensor("TRB", [128, 1024], BF16))

        sem_handles = {}
        for k in ("T", "V", "S", "DW", "GFS", "U", "P", "XT", "TI", "TK", "WD",
                  "GA", "GAF", "SC", "SCF", "XD", "XDF", "XO", "CA", "CAF",
                  "ZY", "FA", "IO", "TKF", "FL", "FL1", "FL2", "FL3",
                  "U1", "U2", "U3"):
            sem_handles[k] = EC(nc.semaphore("sem" + k))

        S = _Sched()

        def vop(fn, waits=()):
            for k, v in waits:
                S.wait("vector", k, v)
            S.wait("vector", "V", S.nv)
            S.emit("vector", lambda E, sems, f=fn: f(E).then_inc(sems["V"], 1))
            S.nv += 1
            return S.nv

        def sop(fn, waits=()):
            for k, v in waits:
                S.wait("scalar", k, v)
            S.wait("scalar", "S", S.ns)
            S.emit("scalar", lambda E, sems, f=fn: f(E).then_inc(sems["S"], 1))
            S.ns += 1
            return S.ns

        def top(fn, waits=(), inc=False):
            for k, v in waits:
                S.wait("tensor", k, v)
            if inc:
                S.emit("tensor", lambda E, sems, f=fn: f(E).then_inc(sems["T"], 1))
                S.nt += 1
                return S.nt
            S.emit("tensor", lambda E, sems, f=fn: f(E))
            return None

        def dq(eng, fn, sem, waits=()):
            for k, v in waits:
                S.wait(eng, k, v)
            S.emit(eng, lambda E, sems, f=fn, sm=sem: f(E).then_inc(sems[sm], 16))
            S.dma[sem] = S.dma.get(sem, 0) + 16
            return S.dma[sem]

        def fence(eng, sem_dma, sem_f):
            S.wait(eng, sem_dma, S.dma.get(sem_dma, 0))
            S.emit(eng, lambda E, sems, sf=sem_f: E.sem_inc(sems[sf], 1))
            S.dma[sem_f] = S.dma.get(sem_f, 0) + 1
            return S.dma[sem_f]

        A = bass.AP

        usem = ["U", "U1", "U2", "U3"]
        for tb in range(16):
            if tb >= 4:
                S.wait("sync", usem[tb % 4], 16 * (tb // 4))
            dq("sync", lambda E, tb=tb: E.dma_start(
                out=BIG[:, tb * 1024:(tb + 1) * 1024],
                in_=A(u_t, tb * 128 * 1024, [[1024, 128], [1, 1024]])),
                usem[tb % 4])
        for fn in (lambda E: E.dma_start(out=ident[:, :], in_=id_t[:, :]),
                   lambda E: E.dma_start(out=A(c_sb, 0, [[512, 128], [64, 8], [1, 64]]),
                                         in_=A(c_t, 0, [[64, 128], [8192, 8], [1, 64]])),
                   lambda E: E.dma_start(out=b_sb[0:DS, 0:DM], in_=b_t[:, :]),
                   lambda E: E.dma_start(out=A(a_b, 0, [[64, 1], [1, 64]]), in_=a_t[:, :]),
                   lambda E: E.dma_start(out=ldt_sb[:, :], in_=ldt_t[:, :]),
                   lambda E: E.dma_start(out=d_sb[:, :], in_=d_t[:, :]),
                   lambda E: E.dma_start(out=wg_sb[:, :], in_=wg_t[:, :])):
            dq("sync", fn, "P")
        v_par = S.dma["P"]

        dq("gpsimd", lambda E: E.dma_start(out=out_t[:, :], in_=u_t[:, :]), "XO")
        S.emit("gpsimd", lambda E, sems: E.iota(
            iotu[:, :], [[0, 49]], base=T, channel_multiplier=1,
            allow_small_or_imprecise_dtypes=True).then_inc(sems["IO"], 1))
        S.emit("gpsimd", lambda E, sems: E.iota(
            iotap[:, 0:1], [[0, 1]], base=0, channel_multiplier=1,
            allow_small_or_imprecise_dtypes=True).then_inc(sems["IO"], 1))
        S.emit("gpsimd", lambda E, sems: E.iota(
            iota8[:, :], [[1, 8]], base=0, channel_multiplier=0,
            allow_small_or_imprecise_dtypes=True).then_inc(sems["IO"], 1))
        S.emit("gpsimd", lambda E, sems: E.memset(
            tmp32[:, :], 0).then_inc(sems["IO"], 1))
        S.emit("gpsimd", lambda E, sems: E.iota(
            tki[:, :], [[128, 16], [0, 2]], base=0, channel_multiplier=1,
            allow_small_or_imprecise_dtypes=True).then_inc(sems["IO"], 1))
        S.wait("gpsimd", "IO", 5)
        dq("gpsimd", lambda E: E.dma_start(
            out=A(tok_d, 0, [[1, 128], [128, 49]]),
            in_=A(iotu, 0, [[49, 128], [1, 49]]),
            allow_slow_non_contiguous=True), "TI")
        v_ti = 3
        dq("gpsimd", lambda E: E.dma_start(
            out=A(yac_d, 0, [[1024, 128], [131072, 17], [1, 1024]]),
            in_=A(tmp32, 0, [[1024, 128], [0, 17], [1, 1024]])), "ZY")
        dq("gpsimd", lambda E: E.dma_start(
            out=A(xnt_d, T * DM, [[1024, 128], [1, 1024]]),
            in_=A(tmp32, 0, [[1024, 128], [1, 1024]])), "ZY")
        dq("gpsimd", lambda E: E.dma_start(
            out=A(wt_d, T * NE, [[8, 128], [1, 8]]),
            in_=A(tmp32, 0, [[1024, 128], [1, 8]])), "ZY")
        zy_total = S.dma["ZY"]

        vop(lambda E: E.memset(ones_c[:, :], 1.0))
        vop(lambda E: E.memset(eps_sb[:, :], 1e-5))
        vop(lambda E: E.memset(ones_r[:, :], 1.0))
        vop(lambda E: E.memset(A(xnf, 0, [[NC_DM * CW, 128], [CW, NC_DM], [1, PADL]]), 0.0))
        v_const = vop(lambda E: E.memset(
            A(xnf, PADL + T, [[NC_DM * CW, 128], [CW, NC_DM], [1, PADR]]), 0.0))
        v_idb = vop(lambda E: E.tensor_copy(idb[:, :], ident[:, :]),
                    waits=[("P", v_par)])

        for cp in range(NC_DM):
            t_csum = top(lambda E, cp=cp: E.matmul(
                H1[0:1, cp * 64:(cp + 1) * 64], ones_c[:, 0:1],
                c_sb[:, cp * 64:(cp + 1) * 64], start=True, stop=True),
                waits=([("P", v_par), ("V", v_const)] if cp == 0 else ()),
                inc=(cp == NC_DM - 1))
        v_csr = vop(lambda E: E.tensor_reduce(
            csr[0:1, 0:DS], A(H1, 0, [[1024, 1], [1, DS], [64, NC_DM]]),
            axis=AX.X, op=OP.add), waits=[("T", t_csum)])
        t_csc = top(lambda E: E.transpose(H3[0:DS, 0:1], csr[0:1, 0:DS], ident[0:1, 0:1]),
                    waits=[("V", v_csr)], inc=True)
        v_csc = vop(lambda E: E.tensor_copy(csc[:, :], H3[0:DS, 0:1]), waits=[("T", t_csc)])
        v_bc = vop(lambda E: E.tensor_scalar(b_sb[0:DS, 0:DM], b_sb[0:DS, 0:DM],
                                             csc[:, 0:1], None, op0=OP.mult))
        for cp in range(NC_DM):
            t_bct = top(lambda E, cp=cp: E.transpose(
                H1[:, cp * 64:(cp + 1) * 64], b_sb[0:DS, cp * 128:(cp + 1) * 128],
                ident[0:DS, 0:DS]),
                waits=([("V", v_bc)] if cp == 0 else ()), inc=(cp == NC_DM - 1))
        v_bct = vop(lambda E: E.tensor_copy(bct[:, :], H1[:, 0:512]), waits=[("T", t_bct)])
        t_ab = top(lambda E: E.matmul(H3[:, 0:DS], ones_r[0:1, 0:128],
                                      A(a_b, 0, [[64, 1], [1, 64]]), start=True, stop=True),
                   waits=[("V", v_csc)], inc=True)
        v_ab = vop(lambda E: E.tensor_copy(a_b[:, :], H3[:, 0:DS]), waits=[("T", t_ab)])
        s_dtc = sop(lambda E: E.activation(dtc[:, :], ldt_sb[:, :], AF.Exp),
                    waits=[("P", v_par)])
        for c in range(NC_DM):
            v_pt = vop(lambda E, c=c: E.tensor_scalar(
                pt_sb[:, c * DS:(c + 1) * DS], a_b[:, :], dtc[:, c:c + 1], None,
                op0=OP.mult), waits=[("S", s_dtc)])
        v_ptdone = v_pt

        v_apply = {}
        for tb in range(16):
            vop(lambda E, tb=tb: E.bn_stats(stats[:, 0:6], BIG[:, tb * 1024:tb * 1024 + 512]),
                waits=[(usem[tb % 4], 16 * (tb // 4 + 1))])
            vop(lambda E, tb=tb: E.bn_stats(stats[:, 6:12],
                                            BIG[:, tb * 1024 + 512:tb * 1024 + 1024]))
            v_ag = vop(lambda E, tb=tb: E.bn_aggr(aggr16[:, 2 * tb:2 * tb + 2],
                                                  stats[:, 0:12]))
            s_sq = sop(lambda E, tb=tb: E.activation(sq16[:, tb:tb + 1],
                                                     aggr16[:, 2 * tb + 1:2 * tb + 2],
                                                     AF.Sqrt, bias=eps_sb[:, 0:1]),
                       waits=[("V", v_ag)])
            vop(lambda E, tb=tb: E.reciprocal(rstd16[:, tb:tb + 1], sq16[:, tb:tb + 1]),
                waits=[("S", s_sq)])
            v_apply[tb] = vop(lambda E, tb=tb: E.tensor_scalar(
                BIG[:, tb * 1024:(tb + 1) * 1024], BIG[:, tb * 1024:(tb + 1) * 1024],
                aggr16[:, 2 * tb:2 * tb + 1], rstd16[:, tb:tb + 1],
                op0=OP.subtract, op1=OP.mult))
        v_lnapply = S.nv
        dq("gpsimd", lambda E: E.dma_start(
            out=A(xnt_d, 0, [[1024, 128], [131072, 16], [1, 1024]]),
            in_=A(BIG, 0, [[16384, 128], [1024, 16], [1, 1024]])), "XT",
            waits=[("V", v_lnapply)])

        ev_half = {}
        ev_clk = {}
        mm_clk = {}
        lg_clk = {}
        for tb in range(16):
            t_half = {}
            for half in range(2):
                for c in range(4 * half, 4 * half + 4):
                    w = [("V", v_apply[tb])]
                    if tb > 0:
                        w.append(("V", ev_half[(tb - 1, half)]))
                    elif half == 0 and c == 0:
                        w.append(("V", v_ab))
                    top(lambda E, tb=tb, c=c: E.transpose(
                        H3[:, c * 128:(c + 1) * 128],
                        BIG[:, tb * 1024 + c * 128:tb * 1024 + (c + 1) * 128],
                        ident[:, :]), waits=(w if c == 4 * half else ()),
                        inc=(c == 4 * half + 3))
                t_half[half] = S.nt
            for half in range(2):
                for c in range(4 * half, 4 * half + 4):
                    vop(lambda E, tb=tb, c=c: E.tensor_copy(
                        xnf[:, c * CW + PADL + tb * 128:c * CW + PADL + (tb + 1) * 128],
                        H3[:, c * 128:(c + 1) * 128]), waits=[("T", t_half[half])])
                    ev_clk[(tb, c)] = vop(lambda E, tb=tb, c=c: E.tensor_copy(
                        tmp32[:, c * 128:(c + 1) * 128],
                        H3[:, c * 128:(c + 1) * 128]),
                        waits=([("ZY", zy_total)] if (tb, c) == (0, 0) else ()))
                ev_half[(tb, half)] = ev_clk[(tb, 4 * half + 3)]
            for c in range(NC_DM):
                w = [("V", ev_clk[(tb, c)]), ("P", v_par)]
                if c == 0 and tb > 0:
                    w.append(("V", lg_clk[tb - 1]))
                elif c == 0:
                    w.append(("V", v_bct))
                mm_clk[(tb, c)] = top(lambda E, tb=tb, c=c: E.matmul(
                    H1[:, 0:8], tmp32[:, c * 128:(c + 1) * 128],
                    wg_sb[:, c * 8:(c + 1) * 8], start=(c == 0), stop=(c == NC_DM - 1)),
                    waits=w, inc=True)
            t_logit_last = mm_clk[(tb, NC_DM - 1)]
            m1 = rsc[:, 0:1]; m2 = rsc[:, 1:2]; dd = rsc[:, 2:3]
            w1s = rsc[:, 3:4]; w2s = rsc[:, 4:5]
            l2 = rsc[:, 8:16]
            eq1 = eqsav[:, tb * 16:tb * 16 + 8]
            eq2 = eqsav[:, tb * 16 + 8:tb * 16 + 16]
            sc1 = rsc[:, 16:24]; sc2 = rsc[:, 24:32]
            lg_clk[tb] = vop(lambda E: E.tensor_copy(lgb[:, :], H1[:, 0:8]),
                             waits=[("T", t_logit_last)])
            vop(lambda E: E.reduce_max(m1, lgb[:, :], axis=AX.X))
            vop(lambda E, eq1=eq1: E.tensor_scalar(eq1, lgb[:, :], m1, None,
                                                   op0=OP.is_equal))
            vop(lambda E, eq1=eq1: E.scalar_tensor_tensor(l2, eq1, -1e30, lgb[:, :],
                                                          op0=OP.mult, op1=OP.add))
            vop(lambda E: E.reduce_max(m2, l2, axis=AX.X))
            vop(lambda E, eq2=eq2: E.tensor_scalar(eq2, l2, m2, None, op0=OP.is_equal))
            vop(lambda E, eq1=eq1: E.tensor_tensor(t8[:, :], eq1, iota8[:, :], op=OP.mult),
                waits=[("IO", v_ti)])
            vop(lambda E, tb=tb: E.tensor_reduce(e12[:, 2 * tb:2 * tb + 1], t8[:, :],
                                                 axis=AX.X, op=OP.add))
            vop(lambda E, eq2=eq2: E.tensor_tensor(t8[:, :], eq2, iota8[:, :], op=OP.mult))
            vop(lambda E, tb=tb: E.tensor_reduce(e12[:, 2 * tb + 1:2 * tb + 2], t8[:, :],
                                                 axis=AX.X, op=OP.add))
            v_d2 = vop(lambda E: E.tensor_tensor(dd, m1, m2, op=OP.subtract))
            s_sig = sop(lambda E: E.activation(w1s, dd, AF.Sigmoid), waits=[("V", v_d2)])
            vop(lambda E: E.tensor_scalar(w2s, w1s, -1.0, 1.0, op0=OP.mult, op1=OP.add),
                waits=[("S", s_sig)])
            vop(lambda E, eq1=eq1: E.tensor_scalar(sc1, eq1, w1s, None, op0=OP.mult))
            vop(lambda E, eq2=eq2: E.tensor_scalar(sc2, eq2, w2s, None, op0=OP.mult))
            vop(lambda E, tb=tb: E.tensor_tensor(wtok[:, tb * 8:(tb + 1) * 8],
                                                 sc1, sc2, op=OP.add))
        v_passE = S.nv
        t_passD = t_logit_last
        dq("sync", lambda E: E.dma_start(
            out=A(wt_d, 0, [[8, 128], [1024, 16], [1, 8]]),
            in_=A(wtok, 0, [[128, 128], [8, 16], [1, 8]])), "WD",
            waits=[("V", v_passE)])

        maskh = kscr[0:8, 0:1024]
        inclh = convb[0:8, 0:1024]
        v_mev = {}
        t_mtr = {}
        v_rkev = {}
        for h2 in range(2):
            v_t8v = {}
            for j in range(8):
                tb = h2 * 8 + j
                w = []
                if h2 == 1 and j == 0:
                    w.append(("T", t_mtr[(0, 1)]))
                v_t8v[j] = vop(lambda E, tb=tb, j=j: E.tensor_tensor(
                    tmp32[:, j * 8:(j + 1) * 8], eqsav[:, tb * 16:tb * 16 + 8],
                    eqsav[:, tb * 16 + 8:tb * 16 + 16], op=OP.add), waits=w)
            for jh in range(2):
                w = [("V", v_t8v[7]), ("T", t_passD)]
                if h2 == 1:
                    w.append(("V", v_mev[(0, 4 * jh + 3)]))
                for j in range(4 * jh, 4 * jh + 4):
                    top(lambda E, j=j: E.transpose(
                        A(YS, j * 128, [[1024, 8], [1, 128]]),
                        tmp32[:, j * 8:(j + 1) * 8], ident[:, :]),
                        waits=(w if j == 4 * jh else ()), inc=(j == 4 * jh + 3))
                t_mtr[(h2, jh)] = S.nt
                for j in range(4 * jh, 4 * jh + 4):
                    v_mev[(h2, j)] = vop(lambda E, j=j: E.tensor_copy(
                        maskh[0:8, j * 128:(j + 1) * 128],
                        A(YS, j * 128, [[1024, 8], [1, 128]])),
                        waits=[("T", t_mtr[(h2, jh)])])
            zb8 = ysl[0:8, 0:1024]
            vop(lambda E, zb8=zb8: E.memset(zb8, 0.0))
            if h2 == 0:
                vop(lambda E, zb8=zb8: E.tensor_tensor_scan(
                    inclh, maskh, zb8, 0.0, op0=OP.add, op1=OP.add))
            else:
                vop(lambda E, zb8=zb8: E.tensor_tensor_scan(
                    inclh, maskh, zb8, carry[:, 0:1], op0=OP.add, op1=OP.add))
            vop(lambda E: E.tensor_copy(carry[:, 0:1], inclh[0:8, 1023:1024]))
            v_sub = vop(lambda E: E.tensor_tensor(inclh, inclh, maskh, op=OP.subtract))
            for j in range(8):
                w = [("V", v_sub)]
                if h2 == 1 and j == 0:
                    w.append(("V", v_rkev[0]))
                top(lambda E, j=j: E.transpose(
                    A(YS, j * 8, [[1024, 128], [1, 8]]),
                    inclh[0:8, j * 128:(j + 1) * 128], ident[0:8, 0:8]),
                    waits=(w if j == 0 else ()), inc=(j == 7))
            t_rt = S.nt
            v_rkev[h2] = vop(lambda E, h2=h2: E.tensor_copy(
                rktok[:, h2 * 64:(h2 + 1) * 64],
                A(YS, 0, [[1024, 128], [1, 64]])), waits=[("T", t_rt)])
        rk2 = kscr
        prod = convb
        r_all = rsc[:, 0:32]
        sl_all = wtok[:, 32:64]
        ov_all = wtok[:, 0:32]
        ot_all = wtok[:, 64:96]
        tr_all = wtok[:, 96:128]
        for k2 in range(2):
            vop(lambda E, k2=k2: E.tensor_copy(
                A(rk2, k2 * 8, [[1024, 128], [16, 16], [1, 8]]),
                A(rktok, 0, [[128, 128], [8, 16], [1, 8]])),
                waits=([("WD", 16)] if k2 == 0 else []))
        vop(lambda E: E.tensor_tensor(prod[:, 0:256], eqsav[:, :],
                                      rk2[:, 0:256], op=OP.mult))
        vop(lambda E: E.tensor_reduce(
            r_all, A(prod, 0, [[1024, 128], [8, 32], [1, 8]]),
            axis=AX.X, op=OP.add))
        vop(lambda E: E.tensor_scalar(scol[:, 6:7], iotap[:, 0:1], 1.0,
                                      float(NSLOT), op0=OP.mult, op1=OP.add),
            waits=[("IO", v_ti)])
        vop(lambda E: E.scalar_tensor_tensor(
            sl_all, e12[:, :], float(CAP), r_all, op0=OP.mult, op1=OP.add))
        vop(lambda E: E.tensor_scalar(ov_all, r_all, float(CAP), None,
                                      op0=OP.is_ge))
        vop(lambda E: E.tensor_scalar(ot_all, ov_all, -1.0, 1.0,
                                      op0=OP.mult, op1=OP.add))
        vop(lambda E: E.tensor_tensor(sl_all, sl_all, ot_all, op=OP.mult))
        vop(lambda E: E.tensor_scalar(tr_all, ov_all, scol[:, 6:7], None,
                                      op0=OP.mult))
        vop(lambda E: E.tensor_tensor(sl_all, sl_all, tr_all, op=OP.add))
        v_slots = vop(lambda E: E.tensor_copy(iotu[:, 0:32], sl_all),
                      waits=[("TI", 16)])
        for ii in range(32):
            w_tk = [("V", v_slots), ("TI", 16), ("IO", 5)]
            if ii == 0:
                w_tk += [("XO", 16), ("ZY", zy_total), ("XT", 16)]
            dq("gpsimd", lambda E, ii=ii: E.indirect_dma_start(
                out=tok_d[:, :],
                out_offset=IndirectOffsetOnAxis(ap=iotu[:, ii:ii + 1], axis=0),
                in_=tki[:, ii:ii + 1], in_offset=None), "TK",
                waits=(w_tk if ii == 0 else []))
        v_passF = S.nv
        tk_total = S.dma["TK"]

        fg = 0
        silu_clk = {}
        gmul_clk = {}
        kred = {}
        freeA = {}
        freeB = {}
        extra_h1 = [0]
        extra_h3 = [0]
        extra_ys = [0]
        hend = {}
        yend_v = {}
        yel = {}
        kstate = {"e1": None}

        def emit_kstep(l):
            E1s = kscr[:, 0:512]
            CH = kscr[:, 512:1024]
            M = convb[:, 0:512]
            if l == 0:
                kred[0] = vop(lambda E: E.tensor_reduce(
                    A(kt, 0, [[NC_DM * LL, 128], [LL, NC_DM], [1, 1]]),
                    A(bct, 0, [[512, 128], [64, 8], [1, DS]]),
                    axis=AX.X, op=OP.add),
                    waits=[("V", v_bct), ("V", v_passF)])
                kstate["e1"] = sop(lambda E: E.activation(E1s, pt_sb[:, :], AF.Exp),
                                   waits=[("V", v_ptdone), ("V", v_passF)])
                return
            if l == 1:
                cur = E1s
            elif l == 2:
                vop(lambda E: E.tensor_tensor(CH, E1s, E1s, op=OP.mult),
                    waits=[("S", kstate["e1"])])
                cur = CH
            else:
                vop(lambda E: E.tensor_tensor(CH, CH, E1s, op=OP.mult))
                cur = CH
            vop(lambda E, cur=cur: E.tensor_tensor(M, cur, bct[:, :], op=OP.mult),
                waits=([("S", kstate["e1"])] if l == 1 else []))
            kred[l] = vop(lambda E, l=l: E.tensor_reduce(
                A(kt, l, [[NC_DM * LL, 128], [LL, NC_DM], [1, 1]]),
                A(convb, 0, [[1024, 128], [64, 8], [1, DS]]),
                axis=AX.X, op=OP.add))

        def emit_conv():
            conv_ev = {}
            caf_prev = [0]
            ev_bank = {0: 0, 1: 0}
            for c in range(NC_DM):
                dg_clk = {}
                cmm_clk = {}
                for l in range(LL):
                    w = [("V", kred[LL - 1])]
                    if l >= 2:
                        w.append(("T", cmm_clk[l - 2]))
                    dg_clk[l] = vop(lambda E, c=c, l=l: E.tensor_scalar(
                        A(diag, (l % 2) * 128, [[256, 128], [1, 128]]),
                        ident[:, :], kt[:, c * LL + l:c * LL + l + 1], None,
                        op0=OP.mult), waits=w)
                    tw = [("V", dg_clk[l])]
                    if l == 0 and c > 0:
                        tw.append(("V", conv_ev[(c - 1, 1)]))
                    elif l == 0:
                        tw.append(("S", silu_clk[fg - 1]))
                        tw.append(("V", gmul_clk[fg - 1]))
                    first = True
                    for h in range(2):
                        PS = H1 if h == 0 else H3
                        for tc in range(2):
                            top(lambda E, c=c, l=l, tc=tc, h=h, PS=PS: E.matmul(
                                PS[:, tc * 512:(tc + 1) * 512],
                                A(diag, (l % 2) * 128, [[256, 128], [1, 128]]),
                                xnf[:, c * CW + h * TH + tc * 512 + 63 - l:
                                    c * CW + h * TH + tc * 512 + 63 - l + 512],
                                start=(l == 0), stop=(l == LL - 1)),
                                waits=(tw if first else ()),
                                inc=(h == 1 and tc == 1))
                            first = False
                    cmm_clk[l] = S.nt
                for h in range(2):
                    PS = H1 if h == 0 else H3
                    conv_ev[(c, h)] = vop(lambda E, c=c, h=h, PS=PS:
                                          E.scalar_tensor_tensor(
                        convb[:, :],
                        xnf[:, c * CW + PADL + h * TH:c * CW + PADL + (h + 1) * TH],
                        d_sb[:, c:c + 1], PS[:, :], op0=OP.mult, op1=OP.add),
                        waits=[("T", cmm_clk[LL - 1])])
                    tw = [("V", conv_ev[(c, h)])]
                    if c == 0 and h == 0:
                        tw.append(("V", yel[1]))
                    if ev_bank[0]:
                        tw.append(("V", ev_bank[0]))
                    for tb in range(4):
                        top(lambda E, tb=tb: E.transpose(
                            YS[:, tb * 128:(tb + 1) * 128],
                            convb[:, tb * 128:(tb + 1) * 128], ident[:, :]),
                            waits=(tw if tb == 0 else ()), inc=(tb == 3))
                    t_b0 = S.nt
                    for tb in range(4, 8):
                        top(lambda E, tb=tb: E.transpose(
                            YS[:, tb * 128:(tb + 1) * 128],
                            convb[:, tb * 128:(tb + 1) * 128], ident[:, :]),
                            waits=([("V", ev_bank[1])]
                                   if (tb == 4 and ev_bank[1]) else ()),
                            inc=(tb == 7))
                    t_b1 = S.nt
                    w2 = [("T", t_b0)]
                    if caf_prev[0]:
                        w2.append(("CAF", caf_prev[0]))
                    ev_bank[0] = vop(lambda E: E.tensor_copy(
                        tmp32[:, 0:512], YS[:, 0:512]), waits=w2)
                    ev_bank[1] = vop(lambda E: E.tensor_copy(
                        tmp32[:, 512:1024], YS[:, 512:1024]), waits=[("T", t_b1)])
                    v_ev = ev_bank[1]
                    dq("gpsimd", lambda E, c=c, h=h: E.dma_start(
                        out=A(out_t, h * TH * DM + c * 128,
                              [[1024, 128], [131072, 8], [1, 128]]),
                        in_=A(tmp32, 0, [[1024, 128], [128, 8], [1, 128]]),
                        accum_op=OP.add), "CA",
                        waits=[("V", v_ev), ("XO", 16)])
                    caf_prev[0] = fence("gpsimd", "CA", "CAF")
            extra_h1[0] = conv_ev[(NC_DM - 1, 0)]
            extra_h3[0] = conv_ev[(NC_DM - 1, 1)]
            extra_ys[0] = S.nt

        def emit_wgroup(e, g):
            segi = e * NG + g
            for j in range(4):
                gsA = segi * 4 + j
                pA = gsA % RA
                pB = gsA % RB
                fcol = g * 1024 + j * 256
                if gsA >= RA:
                    S.wait("gpsimd", "T", freeA[gsA - RA])
                dq("gpsimd", lambda E, e=e, fcol=fcol, pA=pA: E.dma_start(
                    out=A(w1r, pA * 2048, [[RA * 2048, 128], [256, 8], [1, 256]]),
                    in_=A(w1_t, e * DM * FF + fcol,
                          [[FF, 128], [128 * FF, 8], [1, 256]])), "DW")
                dq("gpsimd", lambda E, e=e, fcol=fcol, pA=pA: E.dma_start(
                    out=A(w3r, pA * 2048, [[RA * 2048, 128], [256, 8], [1, 256]]),
                    in_=A(w3_t, e * DM * FF + fcol,
                          [[FF, 128], [128 * FF, 8], [1, 256]])), "DW")
                if gsA >= RB:
                    S.wait("gpsimd", "T", freeB[gsA - RB])
                dq("gpsimd", lambda E, e=e, fcol=fcol, pB=pB: E.dma_start(
                    out=A(w2r, pB * 2048, [[RB * 2048, 128], [1024, 2], [1, 1024]]),
                    in_=A(w2_t, e * FF * DM + fcol * DM,
                          [[DM, 128], [128 * DM, 2], [1, 1024]])), "DW")
                S.wait("gpsimd", "DW", S.dma["DW"])
                S.emit("gpsimd", lambda E, sems: E.sem_inc(sems["GFS"], 1))
                S.dma["GFS"] = S.dma.get("GFS", 0) + 1

        gaf_pair = {}
        xg_ev = {}
        v_xg = {}
        tr_last = {}
        scf_of = {}

        def xg_of(ee):
            return xgA

        def emit_prefetch(ee):
            ib = (ee % 3) * NSC
            for i in range(NSC):
                dq("sync", lambda E, ee=ee, i=i, ib=ib: E.dma_start(
                    out=A(idx_sb, ib + i, [[3 * NSC, 128], [1, 1]]),
                    in_=A(tok_d, ee * CAP + i * 128, [[1, 128], [1, 1]])), "XD",
                    waits=([("TK", tk_total), ("GAF", S.dma.get("GAF", 0)),
                            ("SCF", S.dma.get("SCF", 0))]
                           if i == 0 else []))
            S.wait("sync", "XD", S.dma["XD"])
            S.emit("sync", lambda E, sems: E.sem_inc(sems["XDF"], 1))
            S.dma["XDF"] = S.dma.get("XDF", 0) + 1
            xdf_e = S.dma["XDF"]
            gw = [("XDF", xdf_e), ("XT", 16), ("WD", 16)]
            if ee >= 1:
                gw.append(("T", tr_last[ee - 1]))
            if ee >= 2:
                gw.append(("V", yend_v[ee - 2]))
            wb = (ee % 2) * NSC * NE
            for pr in range(3):
                for q in range(2):
                    i = pr * 2 + q
                    dq("gpsimd", lambda E, pr=pr, q=q, i=i, ib=ib:
                       E.indirect_dma_start(
                        out=A(xgt, pr * 2048 + q * 1024, [[6144, 128], [1, 1024]]),
                        out_offset=None,
                        in_=xnt_d[:, :],
                        in_offset=IndirectOffsetOnAxis(
                            ap=idx_sb[:, ib + i:ib + i + 1], axis=0)),
                        "GA", waits=gw)
                    gw = []
                for q in range(2):
                    i = pr * 2 + q
                    dq("gpsimd", lambda E, i=i, ib=ib, wb=wb:
                       E.indirect_dma_start(
                        out=wsl[:, wb + i * 8:wb + (i + 1) * 8], out_offset=None,
                        in_=wt_d[:, :],
                        in_offset=IndirectOffsetOnAxis(
                            ap=idx_sb[:, ib + i:ib + i + 1], axis=0)),
                        "GA")
                gaf_pair[(ee, pr)] = fence("gpsimd", "GA", "GAF")

        def emit_transposes(ee):
            xge = xg_of(ee)
            for pr in range(3):
                for q in range(2):
                    i = pr * 2 + q
                    TR = TRA if i % 2 == 0 else TRB
                    w = [("GAF", gaf_pair[(ee, pr)]), ("V", v_idb)]
                    if i >= 2:
                        w.append(("V", xg_ev[(ee, i - 2)]))
                    elif ee >= 1:
                        w.append(("V", xg_ev[(ee - 1, 4 + i)]))
                    for c in range(NC_DM):
                        top(lambda E, pr=pr, q=q, c=c, TR=TR: E.transpose(
                            TR[:, c * 128:(c + 1) * 128],
                            A(xgt, pr * 2048 + q * 1024 + c * 128,
                              [[6144, 128], [1, 128]]),
                            idb[:, :]), waits=(w if c == 0 else ()), inc=(c == 7))
                        w = ()
                    t_tr = S.nt
                    xg_ev[(ee, i)] = vop(lambda E, i=i, TR=TR, xge=xge:
                                         E.tensor_copy(
                        A(xge, i * 128, [[NC_DM * CAP, 128], [CAP, NC_DM], [1, 128]]),
                        A(TR, 0, [[1024, 128], [128, 8], [1, 128]])),
                        waits=[("T", t_tr)])
            tr_last[ee] = S.nt
            v_xg[ee] = xg_ev[(ee, 5)]

        for e in range(NE):
            if e == 0:
                emit_wgroup(0, 0)
            if e > 0:
                ibp = ((e - 1) % 3) * NSC
                for sc in range(NSC):
                    dq("gpsimd", lambda E, sc=sc, ibp=ibp: E.indirect_dma_start(
                        out=yac_d[:, :],
                        out_offset=IndirectOffsetOnAxis(
                            ap=idx_sb[:, ibp + sc:ibp + sc + 1], axis=0),
                        in_=ysw[:, sc * 1024:(sc + 1) * 1024],
                        in_offset=None, compute_op=OP.add), "SC",
                        waits=[("V", yend_v[e - 1]), ("ZY", zy_total),
                               ("SC", S.dma.get("SC", 0))])
                scf_of[e - 1] = fence("gpsimd", "SC", "SCF")
            if e == 0:
                emit_prefetch(0)
            emit_transposes(e)
            xg = xg_of(e)
            ysl_clk = 0
            for g in range(NG):
                segi = e * NG + g
                for fl in range(GF):
                    gsA = segi * 4 + fl // 2
                    pA = gsA % RA
                    scol_w = pA * 2048 + (fl % 2) * 128
                    w = [("GFS", segi * 4 + fl // 2 + 1), ("V", v_xg[e])]
                    if fg > 0:
                        w.append(("S", silu_clk[fg - 1]))
                    if extra_h1[0]:
                        w.append(("V", extra_h1[0])); extra_h1[0] = 0
                    for k in range(NC_DM):
                        for nc2 in range(2):
                            n0 = nc2 * 512
                            nn = 512 if nc2 == 0 else CAP - 512
                            top(lambda E, k=k, n0=n0, nn=nn, sw=scol_w,
                                xg=xg: E.matmul(
                                H1[:, n0:n0 + nn],
                                w1r[:, sw + k * 256:sw + k * 256 + 128],
                                xg[:, k * CAP + n0:k * CAP + n0 + nn],
                                start=(k == 0), stop=(k == NC_DM - 1)),
                                waits=(w if (k == 0 and nc2 == 0) else ()),
                                inc=(k == NC_DM - 1 and nc2 == 1))
                    t_h1 = S.nt
                    w = []
                    if fg > 0:
                        w.append(("V", gmul_clk[fg - 1]))
                    if extra_h3[0]:
                        w.append(("V", extra_h3[0])); extra_h3[0] = 0
                    for k in range(NC_DM):
                        for nc2 in range(2):
                            n0 = nc2 * 512
                            nn = 512 if nc2 == 0 else CAP - 512
                            top(lambda E, k=k, n0=n0, nn=nn, sw=scol_w,
                                xg=xg: E.matmul(
                                H3[:, n0:n0 + nn],
                                w3r[:, sw + k * 256:sw + k * 256 + 128],
                                xg[:, k * CAP + n0:k * CAP + n0 + nn],
                                start=(k == 0), stop=(k == NC_DM - 1)),
                                waits=(w if (k == 0 and nc2 == 0) else ()),
                                inc=(k == NC_DM - 1 and nc2 == 1))
                    t_h3 = S.nt
                    if fl % 2 == 1:
                        freeA[gsA] = t_h3
                    hend[e] = t_h3
                    w = [("T", t_h1)]
                    if fg >= 2:
                        w.append(("V", gmul_clk[fg - 2]))
                    silu_clk[fg] = sop(lambda E, fl=fl: E.activation(
                        s_sb[:, (fl % 2) * CAP:(fl % 2 + 1) * CAP],
                        H1[:, 0:CAP], AF.Silu), waits=w)
                    w = [("T", t_h3), ("S", silu_clk[fg])]
                    if ysl_clk:
                        w.append(("T", ysl_clk))
                    gmul_clk[fg] = vop(lambda E, fl=fl: E.tensor_tensor(
                        g_sb[:, fl * CAP:(fl + 1) * CAP],
                        s_sb[:, (fl % 2) * CAP:(fl % 2 + 1) * CAP],
                        H3[:, 0:CAP], op=OP.mult), waits=w)
                    if e < 2 and fg < LL:
                        emit_kstep(fg)
                    fg += 1
                for sc in range(NSC):
                    w = [("V", gmul_clk[fg - 1])]
                    if extra_ys[0]:
                        w.append(("T", extra_ys[0])); extra_ys[0] = 0
                    if (g, sc) != (0, 0):
                        w.append(("V", S.nv))
                    for fl in range(GF):
                        pB = (segi * 4 + fl // 2) % RB
                        for dmh in range(2):
                            top(lambda E, fl=fl, dmh=dmh, sc=sc, pB=pB: E.matmul(
                                YS[:, dmh * 512:(dmh + 1) * 512],
                                g_sb[:, fl * CAP + sc * 128:fl * CAP + (sc + 1) * 128],
                                w2r[:, pB * 2048 + (fl % 2) * 1024 + dmh * 512:
                                    pB * 2048 + (fl % 2) * 1024 + dmh * 512 + 512],
                                start=(fl == 0), stop=(fl == GF - 1)),
                                waits=(w if (fl == 0 and dmh == 0) else ()),
                                inc=(fl == GF - 1 and dmh == 1))
                    t_y = S.nt
                    ysl_clk = t_y
                    if g == 0:
                        vop(lambda E, sc=sc: E.tensor_copy(
                            ysl[:, sc * 1024:(sc + 1) * 1024], YS[:, :]),
                            waits=[("T", t_y)])
                    else:
                        vop(lambda E, sc=sc: E.tensor_tensor(
                            ysl[:, sc * 1024:(sc + 1) * 1024],
                            ysl[:, sc * 1024:(sc + 1) * 1024], YS[:, :], op=OP.add),
                            waits=[("T", t_y)])
                for j in range(4):
                    freeB[segi * 4 + j] = ysl_clk
                if g == 0 and e + 1 < NE:
                    emit_prefetch(e + 1)
                if g < NG - 1:
                    emit_wgroup(e, g + 1)
                elif e + 1 < NE:
                    emit_wgroup(e + 1, 0)
            yel[e] = S.nv
            for sc in range(NSC):
                wb = (e % 2) * NSC * NE
                wgs = ([("SCF", scf_of[e - 1])] if (e > 0 and sc == 0) else [])
                yend_v[e] = vop(lambda E, sc=sc, e=e, wb=wb: E.tensor_scalar(
                    ysw[:, sc * 1024:(sc + 1) * 1024],
                    ysl[:, sc * 1024:(sc + 1) * 1024],
                    wsl[:, wb + sc * 8 + e:wb + sc * 8 + e + 1], None,
                    op0=OP.mult), waits=wgs)
            if e == 1:
                emit_conv()
        ib7 = ((NE - 1) % 3) * NSC
        for sc in range(NSC):
            dq("gpsimd", lambda E, sc=sc: E.indirect_dma_start(
                out=yac_d[:, :],
                out_offset=IndirectOffsetOnAxis(
                    ap=idx_sb[:, ib7 + sc:ib7 + sc + 1], axis=0),
                in_=ysw[:, sc * 1024:(sc + 1) * 1024],
                in_offset=None, compute_op=OP.add), "SC",
                waits=([("V", yend_v[NE - 1]), ("SC", S.dma.get("SC", 0))]
                       if sc == 0 else []))
        S.wait("gpsimd", "SC", S.dma["SC"])
        S.wait("gpsimd", "CA", S.dma.get("CA", 0))
        S.wait("gpsimd", "XO", 16)
        def fl_ap(sl):
            if sl < 3:
                return [tmp32, kscr, convb][sl][:, :]
            return A(BIG, 0, [[16384, 128], [1, 1024]])
        fsem = ["FL", "FL1", "FL2", "FL3"]

        def emit_fl(c):
            slq = c % 4
            w = [("FA", 16 * (c - 3))] if c >= 4 else []
            dq("gpsimd", lambda E, c=c, slq=slq: E.dma_start(
                out=fl_ap(slq),
                in_=yac_d[c * 128:(c + 1) * 128, :]), fsem[slq], waits=w)

        for c in range(4):
            emit_fl(c)
        for c in range(16):
            slq = c % 4
            dq("gpsimd", lambda E, c=c, slq=slq: E.dma_start(
                out=out_t[c * 128:(c + 1) * 128, :],
                in_=fl_ap(slq),
                accum_op=OP.add), "FA",
                waits=[(fsem[slq], 16 * (c // 4 + 1))])
            if c + 4 < 16:
                emit_fl(c + 4)
        S.wait("gpsimd", "FA", 256)


        with nc.Block() as block:
            @block.sync
            def _(E):
                for fn in S.ops["sync"]:
                    fn(E, sem_handles)

            @block.gpsimd
            def _(E):
                for fn in S.ops["gpsimd"]:
                    fn(E, sem_handles)

            @block.scalar
            def _(E):
                for fn in S.ops["scalar"]:
                    fn(E, sem_handles)

            @block.vector
            def _(E):
                for fn in S.ops["vector"]:
                    fn(E, sem_handles)

            @block.tensor
            def _(E):
                for fn in S.ops["tensor"]:
                    fn(E, sem_handles)
    return nc


def _layouts(inputs):
    Wg = np.ascontiguousarray(np.asarray(inputs["L0_ffn_Wg"], np.float32))
    wg2 = np.ascontiguousarray(Wg.reshape(8, 128, 8).transpose(1, 0, 2).reshape(128, 64))
    d2 = np.ascontiguousarray(np.asarray(inputs["L0_attn_D"], np.float32).reshape(8, 128).T)
    ldt2 = np.ascontiguousarray(
        np.asarray(inputs["L0_attn_log_dt"], np.float32).reshape(8, 128).T)
    ident = np.eye(128, dtype=np.float32)
    return wg2, d2, ldt2, ident


def kernel(L0_attn_u, L0_attn_A, L0_attn_B, L0_attn_C, L0_attn_D, L0_attn_log_dt,
           L0_ffn_Wg, L0_ffn_W1, L0_ffn_W3, L0_ffn_W2):
    u = np.ascontiguousarray(np.asarray(L0_attn_u, np.float32))
    W1 = np.ascontiguousarray(np.asarray(L0_ffn_W1, np.float32))
    W3 = np.ascontiguousarray(np.asarray(L0_ffn_W3, np.float32))
    W2 = np.ascontiguousarray(np.asarray(L0_ffn_W2, np.float32))
    Am = np.ascontiguousarray(np.asarray(L0_attn_A, np.float32).reshape(1, DS))
    Bm = np.ascontiguousarray(np.asarray(L0_attn_B, np.float32))
    Cm = np.ascontiguousarray(np.asarray(L0_attn_C, np.float32))
    ins = {"L0_ffn_Wg": L0_ffn_Wg, "L0_attn_D": L0_attn_D,
           "L0_attn_log_dt": L0_attn_log_dt}
    wg2, d2, ldt2, ident = _layouts(ins)
    nc = build_nc()
    in_maps = []
    for b in range(NCORES):
        in_maps.append({
            "u": np.ascontiguousarray(u[b]),
            "w1": W1, "w3": W3, "w2": W2, "wg2": wg2,
            "a_v": Am, "b_m": Bm, "c_m": Cm, "d2": d2, "logdt2": ldt2,
            "ident": ident,
        })
    res = run_bass_kernel_spmd(nc, in_maps, core_ids=list(range(NCORES)))
    out = np.stack([np.asarray(res.results[b]["out"]) for b in range(NCORES)], axis=0)
    return out


if __name__ == "__main__":
    nc = build_nc()
    print("build ok")


# version 12
# speedup vs baseline: 1.0421x; 1.0031x over previous
import numpy as np
import concourse.bass as bass
import concourse.mybir as mybir
from concourse.bass_utils import run_bass_kernel_spmd
from concourse.bass_core_rust import IndirectOffsetOnAxis

F32 = mybir.dt.float32
BF16 = mybir.dt.bfloat16
U32 = mybir.dt.uint32
AF = mybir.ActivationFunctionType
OP = mybir.AluOpType
AX = mybir.AxisListType

NCORES = 8
T = 2048
DM = 1024
DS = 64
LL = 64
FF = 4096
NE = 8
TH = 1024
NC_DM = 8
NG = 4
GF = 8
RA = 6
RB = 5
CAP = 768
NSC = CAP // 128
NSLOT = NE * CAP
PADL, PADR = 32, 31
CW = PADL + T + PADR


class _Sched:
    def __init__(self):
        self.ops = {k: [] for k in ("sync", "gpsimd", "scalar", "vector", "tensor")}
        self.nv = 0
        self.ns = 0
        self.nt = 0
        self.lw = {k: {} for k in self.ops}
        self.dma = {}

    def emit(self, eng, fn):
        self.ops[eng].append(fn)

    def wait(self, eng, key, val):
        if val is None or val <= 0:
            return
        if self.lw[eng].get(key, 0) >= val:
            return
        self.lw[eng][key] = val
        self.emit(eng, lambda E, sems, k=key, v=val: E.wait_ge(sems[k], v))


def build_nc():
    nc = bass.Bass("TRN2", target_bir_lowering=False, detect_race_conditions=False)

    u_t = nc.dram_tensor("u", [T, DM], F32, kind="ExternalInput")
    w1_t = nc.dram_tensor("w1", [NE, DM, FF], F32, kind="ExternalInput")
    w3_t = nc.dram_tensor("w3", [NE, DM, FF], F32, kind="ExternalInput")
    w2_t = nc.dram_tensor("w2", [NE, FF, DM], F32, kind="ExternalInput")
    wg_t = nc.dram_tensor("wg2", [128, 64], F32, kind="ExternalInput")
    a_t = nc.dram_tensor("a_v", [1, DS], F32, kind="ExternalInput")
    b_t = nc.dram_tensor("b_m", [DS, DM], F32, kind="ExternalInput")
    c_t = nc.dram_tensor("c_m", [DM, DS], F32, kind="ExternalInput")
    d_t = nc.dram_tensor("d2", [128, NC_DM], F32, kind="ExternalInput")
    ldt_t = nc.dram_tensor("logdt2", [128, NC_DM], F32, kind="ExternalInput")
    id_t = nc.dram_tensor("ident", [128, 128], F32, kind="ExternalInput")
    out_t = nc.dram_tensor("out", [T, DM], F32, kind="ExternalOutput")
    xnt_d = nc.dram_tensor("xnt_d", [T + 128, DM], BF16, kind="Internal")
    yac_d = nc.dram_tensor("yac_d", [T + 128, DM], F32, kind="Internal")
    tok_d = nc.dram_tensor("tok_d", [NSLOT + 128, 1], U32, kind="Internal")
    wt_d = nc.dram_tensor("wt_d", [T + 128, NE], F32, kind="Internal")

    from contextlib import ExitStack
    with ExitStack() as ctx:
        EC = ctx.enter_context
        BIG = EC(nc.sbuf_tensor("BIG", [128, 16384], F32))
        xnf = EC(nc.sbuf_tensor("xnf", [128, NC_DM * CW], BF16))
        w1r = EC(nc.sbuf_tensor("w1r", [128, RA * 2048], BF16))
        w3r = EC(nc.sbuf_tensor("w3r", [128, RA * 2048], BF16))
        w2r = EC(nc.sbuf_tensor("w2r", [128, RB * 2048], BF16))
        convb = EC(nc.sbuf_tensor("convb", [128, TH], F32))
        kscr = EC(nc.sbuf_tensor("kscr", [128, 1024], F32))
        tmp32 = EC(nc.sbuf_tensor("tmp32", [128, 1024], F32))
        kt = EC(nc.sbuf_tensor("kt", [128, NC_DM * LL], F32))
        bct = EC(nc.sbuf_tensor("bct", [128, NC_DM * DS], F32))
        pt_sb = EC(nc.sbuf_tensor("pt_sb", [128, NC_DM * DS], F32))
        c_sb = EC(nc.sbuf_tensor("c_sb", [128, NC_DM * DS], F32))
        a_b = EC(nc.sbuf_tensor("a_b", [128, DS], F32))
        dtc = EC(nc.sbuf_tensor("dtc", [128, NC_DM], F32))
        d_sb = EC(nc.sbuf_tensor("d_sb", [128, NC_DM], F32))
        ldt_sb = EC(nc.sbuf_tensor("ldt_sb", [128, NC_DM], F32))
        wg_sb = EC(nc.sbuf_tensor("wg_sb", [128, 64], F32))
        ident = EC(nc.sbuf_tensor("identt", [128, 128], F32))
        idb = EC(nc.sbuf_tensor("idb", [128, 128], BF16))
        diag = EC(nc.sbuf_tensor("diag", [128, 2 * 128], BF16))
        csr = EC(nc.sbuf_tensor("csr", [1, DS], F32))
        csc = EC(nc.sbuf_tensor("csc", [DS, 1], F32))
        ones_c = EC(nc.sbuf_tensor("ones_c", [128, 1], F32))
        ones_r = EC(nc.sbuf_tensor("ones_r", [1, 128], F32))
        stats = EC(nc.sbuf_tensor("stats", [128, 12], F32))
        aggr16 = EC(nc.sbuf_tensor("aggr16", [128, 32], F32))
        rstd16 = EC(nc.sbuf_tensor("rstd16", [128, 16], F32))
        sq16 = EC(nc.sbuf_tensor("sq16", [128, 16], F32))
        lgb = EC(nc.sbuf_tensor("lgb", [128, 8], F32))
        eps_sb = EC(nc.sbuf_tensor("eps_sb", [128, 1], F32))
        rsc = EC(nc.sbuf_tensor("rsc", [128, 48], F32))
        wtok = EC(nc.sbuf_tensor("wtok", [128, 128], F32))
        eqsav = EC(nc.sbuf_tensor("eqsav", [128, 256], F32))
        e12 = EC(nc.sbuf_tensor("e12", [128, 32], F32))
        rktok = EC(nc.sbuf_tensor("rktok", [128, 128], F32))
        iotap = EC(nc.sbuf_tensor("iotap", [128, 1], F32))
        iota8 = EC(nc.sbuf_tensor("iota8", [128, 8], F32))
        iotu = EC(nc.sbuf_tensor("iotu", [128, 49], U32))
        carry = EC(nc.sbuf_tensor("carry", [8, 2], F32))
        t8 = EC(nc.sbuf_tensor("t8", [128, 8], F32))
        scol = EC(nc.sbuf_tensor("scol", [128, 8], F32))
        sidx = EC(nc.sbuf_tensor("sidx", [128, 4], U32))
        toku = EC(nc.sbuf_tensor("toku", [128, 2], U32))
        tki = EC(nc.sbuf_tensor("tki", [128, 32], U32))
        idx_sb = EC(nc.sbuf_tensor("idx_sb", [128, 3 * NSC], U32))
        wsl = EC(nc.sbuf_tensor("wsl", [128, 2 * NSC * NE], F32))
        s_sb = EC(nc.sbuf_tensor("s_sb", [128, 2 * CAP], BF16))
        xgt = EC(nc.sbuf_tensor("xgt", [128, 3 * 2048], BF16))
        b_sb = convb
        big_addr = nc.lookup_mloc(BIG).addr

        def at(name, shape, dtype, off):
            return nc.alloc_sbuf_tensor_at(name, shape, dtype, offset=big_addr + off)
        g_sb = at("g_sb", [128, GF * CAP], BF16, 0)
        xgA = at("xgA", [128, NC_DM * CAP], BF16, 12288)
        ysl = at("ysl", [128, NSC * 1024], F32, 24576)
        ysw = at("ysw", [128, NSC * 1024], BF16, 49152)

        H1 = EC(nc.psum_tensor("H1", [128, 1024], F32))
        H3 = EC(nc.psum_tensor("H3", [128, 1024], F32))
        YS = EC(nc.psum_tensor("YS", [128, 1024], F32))
        TRA = EC(nc.psum_tensor("TRA", [128, 1024], BF16))
        TRB = EC(nc.psum_tensor("TRB", [128, 1024], BF16))

        sem_handles = {}
        for k in ("T", "V", "S", "DW", "GFS", "U", "P", "XT", "TI", "TK", "WD",
                  "GA", "GAF", "SC", "SCF", "XD", "XDF", "XO", "CA", "CAF",
                  "ZY", "FA", "IO", "TKF", "FL", "FL1", "FL2", "FL3",
                  "U1", "U2", "U3"):
            sem_handles[k] = EC(nc.semaphore("sem" + k))

        S = _Sched()

        def vop(fn, waits=()):
            for k, v in waits:
                S.wait("vector", k, v)
            S.wait("vector", "V", S.nv)
            S.emit("vector", lambda E, sems, f=fn: f(E).then_inc(sems["V"], 1))
            S.nv += 1
            return S.nv

        def sop(fn, waits=()):
            for k, v in waits:
                S.wait("scalar", k, v)
            S.wait("scalar", "S", S.ns)
            S.emit("scalar", lambda E, sems, f=fn: f(E).then_inc(sems["S"], 1))
            S.ns += 1
            return S.ns

        def top(fn, waits=(), inc=False):
            for k, v in waits:
                S.wait("tensor", k, v)
            if inc:
                S.emit("tensor", lambda E, sems, f=fn: f(E).then_inc(sems["T"], 1))
                S.nt += 1
                return S.nt
            S.emit("tensor", lambda E, sems, f=fn: f(E))
            return None

        def dq(eng, fn, sem, waits=()):
            for k, v in waits:
                S.wait(eng, k, v)
            S.emit(eng, lambda E, sems, f=fn, sm=sem: f(E).then_inc(sems[sm], 16))
            S.dma[sem] = S.dma.get(sem, 0) + 16
            return S.dma[sem]

        def fence(eng, sem_dma, sem_f):
            S.wait(eng, sem_dma, S.dma.get(sem_dma, 0))
            S.emit(eng, lambda E, sems, sf=sem_f: E.sem_inc(sems[sf], 1))
            S.dma[sem_f] = S.dma.get(sem_f, 0) + 1
            return S.dma[sem_f]

        A = bass.AP

        usem = ["U", "U1", "U2", "U3"]
        for tb in range(16):
            if tb >= 4:
                S.wait("sync", usem[tb % 4], 16 * (tb // 4))
            dq("sync", lambda E, tb=tb: E.dma_start(
                out=BIG[:, tb * 1024:(tb + 1) * 1024],
                in_=A(u_t, tb * 128 * 1024, [[1024, 128], [1, 1024]])),
                usem[tb % 4])
        for fn in (lambda E: E.dma_start(out=ident[:, :], in_=id_t[:, :]),
                   lambda E: E.dma_start(out=A(c_sb, 0, [[512, 128], [64, 8], [1, 64]]),
                                         in_=A(c_t, 0, [[64, 128], [8192, 8], [1, 64]])),
                   lambda E: E.dma_start(out=b_sb[0:DS, 0:DM], in_=b_t[:, :]),
                   lambda E: E.dma_start(out=A(a_b, 0, [[64, 1], [1, 64]]), in_=a_t[:, :]),
                   lambda E: E.dma_start(out=ldt_sb[:, :], in_=ldt_t[:, :]),
                   lambda E: E.dma_start(out=d_sb[:, :], in_=d_t[:, :]),
                   lambda E: E.dma_start(out=wg_sb[:, :], in_=wg_t[:, :])):
            dq("sync", fn, "P")
        v_par = S.dma["P"]

        dq("gpsimd", lambda E: E.dma_start(out=out_t[:, :], in_=u_t[:, :]), "XO")
        S.emit("gpsimd", lambda E, sems: E.iota(
            iotu[:, :], [[0, 49]], base=T, channel_multiplier=1,
            allow_small_or_imprecise_dtypes=True).then_inc(sems["IO"], 1))
        S.emit("gpsimd", lambda E, sems: E.iota(
            iotap[:, 0:1], [[0, 1]], base=0, channel_multiplier=1,
            allow_small_or_imprecise_dtypes=True).then_inc(sems["IO"], 1))
        S.emit("gpsimd", lambda E, sems: E.iota(
            iota8[:, :], [[1, 8]], base=0, channel_multiplier=0,
            allow_small_or_imprecise_dtypes=True).then_inc(sems["IO"], 1))
        S.emit("gpsimd", lambda E, sems: E.memset(
            tmp32[:, :], 0).then_inc(sems["IO"], 1))
        S.emit("gpsimd", lambda E, sems: E.iota(
            tki[:, :], [[128, 16], [0, 2]], base=0, channel_multiplier=1,
            allow_small_or_imprecise_dtypes=True).then_inc(sems["IO"], 1))
        S.wait("gpsimd", "IO", 5)
        dq("gpsimd", lambda E: E.dma_start(
            out=A(tok_d, 0, [[49, 128], [1, 49]]),
            in_=A(iotu, 0, [[49, 128], [1, 49]])), "TI")
        v_ti = 3
        dq("gpsimd", lambda E: E.dma_start(
            out=A(yac_d, 0, [[1024, 128], [131072, 17], [1, 1024]]),
            in_=A(tmp32, 0, [[1024, 128], [0, 17], [1, 1024]])), "ZY")
        dq("gpsimd", lambda E: E.dma_start(
            out=A(xnt_d, T * DM, [[1024, 128], [1, 1024]]),
            in_=A(tmp32, 0, [[1024, 128], [1, 1024]])), "ZY")
        dq("gpsimd", lambda E: E.dma_start(
            out=A(wt_d, T * NE, [[8, 128], [1, 8]]),
            in_=A(tmp32, 0, [[1024, 128], [1, 8]])), "ZY")
        zy_total = S.dma["ZY"]

        vop(lambda E: E.memset(ones_c[:, :], 1.0))
        vop(lambda E: E.memset(eps_sb[:, :], 1e-5))
        vop(lambda E: E.memset(ones_r[:, :], 1.0))
        vop(lambda E: E.memset(A(xnf, 0, [[NC_DM * CW, 128], [CW, NC_DM], [1, PADL]]), 0.0))
        v_const = vop(lambda E: E.memset(
            A(xnf, PADL + T, [[NC_DM * CW, 128], [CW, NC_DM], [1, PADR]]), 0.0))
        v_idb = vop(lambda E: E.tensor_copy(idb[:, :], ident[:, :]),
                    waits=[("P", v_par)])

        for cp in range(NC_DM):
            t_csum = top(lambda E, cp=cp: E.matmul(
                H1[0:1, cp * 64:(cp + 1) * 64], ones_c[:, 0:1],
                c_sb[:, cp * 64:(cp + 1) * 64], start=True, stop=True),
                waits=([("P", v_par), ("V", v_const)] if cp == 0 else ()),
                inc=(cp == NC_DM - 1))
        v_csr = vop(lambda E: E.tensor_reduce(
            csr[0:1, 0:DS], A(H1, 0, [[1024, 1], [1, DS], [64, NC_DM]]),
            axis=AX.X, op=OP.add), waits=[("T", t_csum)])
        t_csc = top(lambda E: E.transpose(H3[0:DS, 0:1], csr[0:1, 0:DS], ident[0:1, 0:1]),
                    waits=[("V", v_csr)], inc=True)
        v_csc = vop(lambda E: E.tensor_copy(csc[:, :], H3[0:DS, 0:1]), waits=[("T", t_csc)])
        v_bc = vop(lambda E: E.tensor_scalar(b_sb[0:DS, 0:DM], b_sb[0:DS, 0:DM],
                                             csc[:, 0:1], None, op0=OP.mult))
        for cp in range(NC_DM):
            t_bct = top(lambda E, cp=cp: E.transpose(
                H1[:, cp * 64:(cp + 1) * 64], b_sb[0:DS, cp * 128:(cp + 1) * 128],
                ident[0:DS, 0:DS]),
                waits=([("V", v_bc)] if cp == 0 else ()), inc=(cp == NC_DM - 1))
        v_bct = vop(lambda E: E.tensor_copy(bct[:, :], H1[:, 0:512]), waits=[("T", t_bct)])
        t_ab = top(lambda E: E.matmul(H3[:, 0:DS], ones_r[0:1, 0:128],
                                      A(a_b, 0, [[64, 1], [1, 64]]), start=True, stop=True),
                   waits=[("V", v_csc)], inc=True)
        v_ab = vop(lambda E: E.tensor_copy(a_b[:, :], H3[:, 0:DS]), waits=[("T", t_ab)])
        s_dtc = sop(lambda E: E.activation(dtc[:, :], ldt_sb[:, :], AF.Exp),
                    waits=[("P", v_par)])
        for c in range(NC_DM):
            v_pt = vop(lambda E, c=c: E.tensor_scalar(
                pt_sb[:, c * DS:(c + 1) * DS], a_b[:, :], dtc[:, c:c + 1], None,
                op0=OP.mult), waits=[("S", s_dtc)])
        v_ptdone = v_pt

        v_apply = {}
        for tb in range(16):
            vop(lambda E, tb=tb: E.bn_stats(stats[:, 0:6], BIG[:, tb * 1024:tb * 1024 + 512]),
                waits=[(usem[tb % 4], 16 * (tb // 4 + 1))])
            vop(lambda E, tb=tb: E.bn_stats(stats[:, 6:12],
                                            BIG[:, tb * 1024 + 512:tb * 1024 + 1024]))
            v_ag = vop(lambda E, tb=tb: E.bn_aggr(aggr16[:, 2 * tb:2 * tb + 2],
                                                  stats[:, 0:12]))
            s_sq = sop(lambda E, tb=tb: E.activation(sq16[:, tb:tb + 1],
                                                     aggr16[:, 2 * tb + 1:2 * tb + 2],
                                                     AF.Sqrt, bias=eps_sb[:, 0:1]),
                       waits=[("V", v_ag)])
            vop(lambda E, tb=tb: E.reciprocal(rstd16[:, tb:tb + 1], sq16[:, tb:tb + 1]),
                waits=[("S", s_sq)])
            v_apply[tb] = vop(lambda E, tb=tb: E.tensor_scalar(
                BIG[:, tb * 1024:(tb + 1) * 1024], BIG[:, tb * 1024:(tb + 1) * 1024],
                aggr16[:, 2 * tb:2 * tb + 1], rstd16[:, tb:tb + 1],
                op0=OP.subtract, op1=OP.mult))
        v_lnapply = S.nv
        dq("gpsimd", lambda E: E.dma_start(
            out=A(xnt_d, 0, [[1024, 128], [131072, 16], [1, 1024]]),
            in_=A(BIG, 0, [[16384, 128], [1024, 16], [1, 1024]])), "XT",
            waits=[("V", v_lnapply)])

        ev_half = {}
        ev_clk = {}
        mm_clk = {}
        lg_clk = {}
        for tb in range(16):
            t_half = {}
            for half in range(2):
                for c in range(4 * half, 4 * half + 4):
                    w = [("V", v_apply[tb])]
                    if tb > 0:
                        w.append(("V", ev_half[(tb - 1, half)]))
                    elif half == 0 and c == 0:
                        w.append(("V", v_ab))
                    top(lambda E, tb=tb, c=c: E.transpose(
                        H3[:, c * 128:(c + 1) * 128],
                        BIG[:, tb * 1024 + c * 128:tb * 1024 + (c + 1) * 128],
                        ident[:, :]), waits=(w if c == 4 * half else ()),
                        inc=(c == 4 * half + 3))
                t_half[half] = S.nt
            for half in range(2):
                for c in range(4 * half, 4 * half + 4):
                    vop(lambda E, tb=tb, c=c: E.tensor_copy(
                        xnf[:, c * CW + PADL + tb * 128:c * CW + PADL + (tb + 1) * 128],
                        H3[:, c * 128:(c + 1) * 128]), waits=[("T", t_half[half])])
                    ev_clk[(tb, c)] = vop(lambda E, tb=tb, c=c: E.tensor_copy(
                        tmp32[:, c * 128:(c + 1) * 128],
                        H3[:, c * 128:(c + 1) * 128]),
                        waits=([("ZY", zy_total)] if (tb, c) == (0, 0) else ()))
                ev_half[(tb, half)] = ev_clk[(tb, 4 * half + 3)]
            for c in range(NC_DM):
                w = [("V", ev_clk[(tb, c)]), ("P", v_par)]
                if c == 0 and tb > 0:
                    w.append(("V", lg_clk[tb - 1]))
                elif c == 0:
                    w.append(("V", v_bct))
                mm_clk[(tb, c)] = top(lambda E, tb=tb, c=c: E.matmul(
                    H1[:, 0:8], tmp32[:, c * 128:(c + 1) * 128],
                    wg_sb[:, c * 8:(c + 1) * 8], start=(c == 0), stop=(c == NC_DM - 1)),
                    waits=w, inc=True)
            t_logit_last = mm_clk[(tb, NC_DM - 1)]
            m1 = rsc[:, 0:1]; m2 = rsc[:, 1:2]; dd = rsc[:, 2:3]
            w1s = rsc[:, 3:4]; w2s = rsc[:, 4:5]
            l2 = rsc[:, 8:16]
            eq1 = eqsav[:, tb * 16:tb * 16 + 8]
            eq2 = eqsav[:, tb * 16 + 8:tb * 16 + 16]
            sc1 = rsc[:, 16:24]; sc2 = rsc[:, 24:32]
            lg_clk[tb] = vop(lambda E: E.tensor_copy(lgb[:, :], H1[:, 0:8]),
                             waits=[("T", t_logit_last)])
            vop(lambda E: E.reduce_max(m1, lgb[:, :], axis=AX.X))
            vop(lambda E, eq1=eq1: E.tensor_scalar(eq1, lgb[:, :], m1, None,
                                                   op0=OP.is_equal))
            vop(lambda E, eq1=eq1: E.scalar_tensor_tensor(l2, eq1, -1e30, lgb[:, :],
                                                          op0=OP.mult, op1=OP.add))
            vop(lambda E: E.reduce_max(m2, l2, axis=AX.X))
            vop(lambda E, eq2=eq2: E.tensor_scalar(eq2, l2, m2, None, op0=OP.is_equal))
            vop(lambda E, eq1=eq1: E.tensor_tensor(t8[:, :], eq1, iota8[:, :], op=OP.mult),
                waits=[("IO", v_ti)])
            vop(lambda E, tb=tb: E.tensor_reduce(e12[:, 2 * tb:2 * tb + 1], t8[:, :],
                                                 axis=AX.X, op=OP.add))
            vop(lambda E, eq2=eq2: E.tensor_tensor(t8[:, :], eq2, iota8[:, :], op=OP.mult))
            vop(lambda E, tb=tb: E.tensor_reduce(e12[:, 2 * tb + 1:2 * tb + 2], t8[:, :],
                                                 axis=AX.X, op=OP.add))
            v_d2 = vop(lambda E: E.tensor_tensor(dd, m1, m2, op=OP.subtract))
            s_sig = sop(lambda E: E.activation(w1s, dd, AF.Sigmoid), waits=[("V", v_d2)])
            vop(lambda E: E.tensor_scalar(w2s, w1s, -1.0, 1.0, op0=OP.mult, op1=OP.add),
                waits=[("S", s_sig)])
            vop(lambda E, eq1=eq1: E.tensor_scalar(sc1, eq1, w1s, None, op0=OP.mult))
            vop(lambda E, eq2=eq2: E.tensor_scalar(sc2, eq2, w2s, None, op0=OP.mult))
            vop(lambda E, tb=tb: E.tensor_tensor(wtok[:, tb * 8:(tb + 1) * 8],
                                                 sc1, sc2, op=OP.add))
        v_passE = S.nv
        t_passD = t_logit_last
        dq("sync", lambda E: E.dma_start(
            out=A(wt_d, 0, [[8, 128], [1024, 16], [1, 8]]),
            in_=A(wtok, 0, [[128, 128], [8, 16], [1, 8]])), "WD",
            waits=[("V", v_passE)])

        maskh = kscr[0:8, 0:1024]
        inclh = convb[0:8, 0:1024]
        v_mev = {}
        t_mtr = {}
        v_rkev = {}
        for h2 in range(2):
            v_t8v = {}
            for j in range(8):
                tb = h2 * 8 + j
                w = []
                if h2 == 1 and j == 0:
                    w.append(("T", t_mtr[(0, 1)]))
                v_t8v[j] = vop(lambda E, tb=tb, j=j: E.tensor_tensor(
                    tmp32[:, j * 8:(j + 1) * 8], eqsav[:, tb * 16:tb * 16 + 8],
                    eqsav[:, tb * 16 + 8:tb * 16 + 16], op=OP.add), waits=w)
            for jh in range(2):
                w = [("V", v_t8v[7]), ("T", t_passD)]
                if h2 == 1:
                    w.append(("V", v_mev[(0, 4 * jh + 3)]))
                for j in range(4 * jh, 4 * jh + 4):
                    top(lambda E, j=j: E.transpose(
                        A(YS, j * 128, [[1024, 8], [1, 128]]),
                        tmp32[:, j * 8:(j + 1) * 8], ident[:, :]),
                        waits=(w if j == 4 * jh else ()), inc=(j == 4 * jh + 3))
                t_mtr[(h2, jh)] = S.nt
                for j in range(4 * jh, 4 * jh + 4):
                    v_mev[(h2, j)] = vop(lambda E, j=j: E.tensor_copy(
                        maskh[0:8, j * 128:(j + 1) * 128],
                        A(YS, j * 128, [[1024, 8], [1, 128]])),
                        waits=[("T", t_mtr[(h2, jh)])])
            zb8 = ysl[0:8, 0:1024]
            vop(lambda E, zb8=zb8: E.memset(zb8, 0.0))
            if h2 == 0:
                vop(lambda E, zb8=zb8: E.tensor_tensor_scan(
                    inclh, maskh, zb8, 0.0, op0=OP.add, op1=OP.add))
            else:
                vop(lambda E, zb8=zb8: E.tensor_tensor_scan(
                    inclh, maskh, zb8, carry[:, 0:1], op0=OP.add, op1=OP.add))
            vop(lambda E: E.tensor_copy(carry[:, 0:1], inclh[0:8, 1023:1024]))
            v_sub = vop(lambda E: E.tensor_tensor(inclh, inclh, maskh, op=OP.subtract))
            for j in range(8):
                w = [("V", v_sub)]
                if h2 == 1 and j == 0:
                    w.append(("V", v_rkev[0]))
                top(lambda E, j=j: E.transpose(
                    A(YS, j * 8, [[1024, 128], [1, 8]]),
                    inclh[0:8, j * 128:(j + 1) * 128], ident[0:8, 0:8]),
                    waits=(w if j == 0 else ()), inc=(j == 7))
            t_rt = S.nt
            v_rkev[h2] = vop(lambda E, h2=h2: E.tensor_copy(
                rktok[:, h2 * 64:(h2 + 1) * 64],
                A(YS, 0, [[1024, 128], [1, 64]])), waits=[("T", t_rt)])
        rk2 = kscr
        prod = convb
        r_all = rsc[:, 0:32]
        sl_all = wtok[:, 32:64]
        ov_all = wtok[:, 0:32]
        ot_all = wtok[:, 64:96]
        tr_all = wtok[:, 96:128]
        for k2 in range(2):
            vop(lambda E, k2=k2: E.tensor_copy(
                A(rk2, k2 * 8, [[1024, 128], [16, 16], [1, 8]]),
                A(rktok, 0, [[128, 128], [8, 16], [1, 8]])),
                waits=([("WD", 16)] if k2 == 0 else []))
        vop(lambda E: E.tensor_tensor(prod[:, 0:256], eqsav[:, :],
                                      rk2[:, 0:256], op=OP.mult))
        vop(lambda E: E.tensor_reduce(
            r_all, A(prod, 0, [[1024, 128], [8, 32], [1, 8]]),
            axis=AX.X, op=OP.add))
        vop(lambda E: E.tensor_scalar(scol[:, 6:7], iotap[:, 0:1], 1.0,
                                      float(NSLOT), op0=OP.mult, op1=OP.add),
            waits=[("IO", v_ti)])
        vop(lambda E: E.scalar_tensor_tensor(
            sl_all, e12[:, :], float(CAP), r_all, op0=OP.mult, op1=OP.add))
        vop(lambda E: E.tensor_scalar(ov_all, r_all, float(CAP), None,
                                      op0=OP.is_ge))
        vop(lambda E: E.tensor_scalar(ot_all, ov_all, -1.0, 1.0,
                                      op0=OP.mult, op1=OP.add))
        vop(lambda E: E.tensor_tensor(sl_all, sl_all, ot_all, op=OP.mult))
        vop(lambda E: E.tensor_scalar(tr_all, ov_all, scol[:, 6:7], None,
                                      op0=OP.mult))
        vop(lambda E: E.tensor_tensor(sl_all, sl_all, tr_all, op=OP.add))
        v_slots = vop(lambda E: E.tensor_copy(iotu[:, 0:32], sl_all),
                      waits=[("TI", 16)])
        for ii in range(32):
            w_tk = [("V", v_slots), ("TI", 16), ("IO", 5)]
            if ii == 0:
                w_tk += [("XO", 16), ("ZY", zy_total), ("XT", 16)]
            dq("gpsimd", lambda E, ii=ii: E.indirect_dma_start(
                out=tok_d[:, :],
                out_offset=IndirectOffsetOnAxis(ap=iotu[:, ii:ii + 1], axis=0),
                in_=tki[:, ii:ii + 1], in_offset=None), "TK",
                waits=(w_tk if ii == 0 else []))
        v_passF = S.nv
        tk_total = S.dma["TK"]

        fg = 0
        silu_clk = {}
        gmul_clk = {}
        kred = {}
        freeA = {}
        freeB = {}
        extra_h1 = [0]
        extra_h3 = [0]
        extra_ys = [0]
        hend = {}
        yend_v = {}
        yel = {}
        kstate = {"e1": None}

        def emit_kstep(l):
            E1s = kscr[:, 0:512]
            CH = kscr[:, 512:1024]
            M = convb[:, 0:512]
            if l == 0:
                kred[0] = vop(lambda E: E.tensor_reduce(
                    A(kt, 0, [[NC_DM * LL, 128], [LL, NC_DM], [1, 1]]),
                    A(bct, 0, [[512, 128], [64, 8], [1, DS]]),
                    axis=AX.X, op=OP.add),
                    waits=[("V", v_bct), ("V", v_passF)])
                kstate["e1"] = sop(lambda E: E.activation(E1s, pt_sb[:, :], AF.Exp),
                                   waits=[("V", v_ptdone), ("V", v_passF)])
                return
            if l == 1:
                cur = E1s
            elif l == 2:
                vop(lambda E: E.tensor_tensor(CH, E1s, E1s, op=OP.mult),
                    waits=[("S", kstate["e1"])])
                cur = CH
            else:
                vop(lambda E: E.tensor_tensor(CH, CH, E1s, op=OP.mult))
                cur = CH
            vop(lambda E, cur=cur: E.tensor_tensor(M, cur, bct[:, :], op=OP.mult),
                waits=([("S", kstate["e1"])] if l == 1 else []))
            kred[l] = vop(lambda E, l=l: E.tensor_reduce(
                A(kt, l, [[NC_DM * LL, 128], [LL, NC_DM], [1, 1]]),
                A(convb, 0, [[1024, 128], [64, 8], [1, DS]]),
                axis=AX.X, op=OP.add))

        def emit_conv():
            conv_ev = {}
            caf_prev = [0]
            ev_bank = {0: 0, 1: 0}
            for c in range(NC_DM):
                dg_clk = {}
                cmm_clk = {}
                for l in range(LL):
                    w = [("V", kred[LL - 1])]
                    if l >= 2:
                        w.append(("T", cmm_clk[l - 2]))
                    dg_clk[l] = vop(lambda E, c=c, l=l: E.tensor_scalar(
                        A(diag, (l % 2) * 128, [[256, 128], [1, 128]]),
                        ident[:, :], kt[:, c * LL + l:c * LL + l + 1], None,
                        op0=OP.mult), waits=w)
                    tw = [("V", dg_clk[l])]
                    if l == 0 and c > 0:
                        tw.append(("V", conv_ev[(c - 1, 1)]))
                    elif l == 0:
                        tw.append(("S", silu_clk[fg - 1]))
                        tw.append(("V", gmul_clk[fg - 1]))
                    first = True
                    for h in range(2):
                        PS = H1 if h == 0 else H3
                        for tc in range(2):
                            top(lambda E, c=c, l=l, tc=tc, h=h, PS=PS: E.matmul(
                                PS[:, tc * 512:(tc + 1) * 512],
                                A(diag, (l % 2) * 128, [[256, 128], [1, 128]]),
                                xnf[:, c * CW + h * TH + tc * 512 + 63 - l:
                                    c * CW + h * TH + tc * 512 + 63 - l + 512],
                                start=(l == 0), stop=(l == LL - 1)),
                                waits=(tw if first else ()),
                                inc=(h == 1 and tc == 1))
                            first = False
                    cmm_clk[l] = S.nt
                for h in range(2):
                    PS = H1 if h == 0 else H3
                    conv_ev[(c, h)] = vop(lambda E, c=c, h=h, PS=PS:
                                          E.scalar_tensor_tensor(
                        convb[:, :],
                        xnf[:, c * CW + PADL + h * TH:c * CW + PADL + (h + 1) * TH],
                        d_sb[:, c:c + 1], PS[:, :], op0=OP.mult, op1=OP.add),
                        waits=[("T", cmm_clk[LL - 1])])
                    tw = [("V", conv_ev[(c, h)])]
                    if c == 0 and h == 0:
                        tw.append(("V", yel[1]))
                    if ev_bank[0]:
                        tw.append(("V", ev_bank[0]))
                    for tb in range(4):
                        top(lambda E, tb=tb: E.transpose(
                            YS[:, tb * 128:(tb + 1) * 128],
                            convb[:, tb * 128:(tb + 1) * 128], ident[:, :]),
                            waits=(tw if tb == 0 else ()), inc=(tb == 3))
                    t_b0 = S.nt
                    for tb in range(4, 8):
                        top(lambda E, tb=tb: E.transpose(
                            YS[:, tb * 128:(tb + 1) * 128],
                            convb[:, tb * 128:(tb + 1) * 128], ident[:, :]),
                            waits=([("V", ev_bank[1])]
                                   if (tb == 4 and ev_bank[1]) else ()),
                            inc=(tb == 7))
                    t_b1 = S.nt
                    w2 = [("T", t_b0)]
                    if caf_prev[0]:
                        w2.append(("CAF", caf_prev[0]))
                    ev_bank[0] = vop(lambda E: E.tensor_copy(
                        tmp32[:, 0:512], YS[:, 0:512]), waits=w2)
                    ev_bank[1] = vop(lambda E: E.tensor_copy(
                        tmp32[:, 512:1024], YS[:, 512:1024]), waits=[("T", t_b1)])
                    v_ev = ev_bank[1]
                    dq("gpsimd", lambda E, c=c, h=h: E.dma_start(
                        out=A(out_t, h * TH * DM + c * 128,
                              [[1024, 128], [131072, 8], [1, 128]]),
                        in_=A(tmp32, 0, [[1024, 128], [128, 8], [1, 128]]),
                        accum_op=OP.add), "CA",
                        waits=[("V", v_ev), ("XO", 16)])
                    caf_prev[0] = fence("gpsimd", "CA", "CAF")
            extra_h1[0] = conv_ev[(NC_DM - 1, 0)]
            extra_h3[0] = conv_ev[(NC_DM - 1, 1)]
            extra_ys[0] = S.nt

        def emit_wgroup(e, g):
            segi = e * NG + g
            for j in range(4):
                gsA = segi * 4 + j
                pA = gsA % RA
                pB = gsA % RB
                fcol = g * 1024 + j * 256
                if gsA >= RA:
                    S.wait("gpsimd", "T", freeA[gsA - RA])
                dq("gpsimd", lambda E, e=e, fcol=fcol, pA=pA: E.dma_start(
                    out=A(w1r, pA * 2048, [[RA * 2048, 128], [256, 8], [1, 256]]),
                    in_=A(w1_t, e * DM * FF + fcol,
                          [[FF, 128], [128 * FF, 8], [1, 256]])), "DW")
                dq("gpsimd", lambda E, e=e, fcol=fcol, pA=pA: E.dma_start(
                    out=A(w3r, pA * 2048, [[RA * 2048, 128], [256, 8], [1, 256]]),
                    in_=A(w3_t, e * DM * FF + fcol,
                          [[FF, 128], [128 * FF, 8], [1, 256]])), "DW")
                if gsA >= RB:
                    S.wait("gpsimd", "T", freeB[gsA - RB])
                dq("gpsimd", lambda E, e=e, fcol=fcol, pB=pB: E.dma_start(
                    out=A(w2r, pB * 2048, [[RB * 2048, 128], [1024, 2], [1, 1024]]),
                    in_=A(w2_t, e * FF * DM + fcol * DM,
                          [[DM, 128], [128 * DM, 2], [1, 1024]])), "DW")
                S.wait("gpsimd", "DW", S.dma["DW"])
                S.emit("gpsimd", lambda E, sems: E.sem_inc(sems["GFS"], 1))
                S.dma["GFS"] = S.dma.get("GFS", 0) + 1

        gaf_pair = {}
        xg_ev = {}
        v_xg = {}
        tr_last = {}
        scf_of = {}

        def xg_of(ee):
            return xgA

        def emit_prefetch(ee):
            ib = (ee % 3) * NSC
            for i in range(NSC):
                dq("sync", lambda E, ee=ee, i=i, ib=ib: E.dma_start(
                    out=A(idx_sb, ib + i, [[3 * NSC, 128], [1, 1]]),
                    in_=A(tok_d, ee * CAP + i * 128, [[1, 128], [1, 1]])), "XD",
                    waits=([("TK", tk_total), ("GAF", S.dma.get("GAF", 0)),
                            ("SCF", S.dma.get("SCF", 0))]
                           if i == 0 else []))
            S.wait("sync", "XD", S.dma["XD"])
            S.emit("sync", lambda E, sems: E.sem_inc(sems["XDF"], 1))
            S.dma["XDF"] = S.dma.get("XDF", 0) + 1
            xdf_e = S.dma["XDF"]
            gw = [("XDF", xdf_e), ("XT", 16), ("WD", 16)]
            if ee >= 1:
                gw.append(("T", tr_last[ee - 1]))
            if ee >= 2:
                gw.append(("V", yend_v[ee - 2]))
            wb = (ee % 2) * NSC * NE
            for pr in range(3):
                for q in range(2):
                    i = pr * 2 + q
                    dq("gpsimd", lambda E, pr=pr, q=q, i=i, ib=ib:
                       E.indirect_dma_start(
                        out=A(xgt, pr * 2048 + q * 1024, [[6144, 128], [1, 1024]]),
                        out_offset=None,
                        in_=xnt_d[:, :],
                        in_offset=IndirectOffsetOnAxis(
                            ap=idx_sb[:, ib + i:ib + i + 1], axis=0)),
                        "GA", waits=gw)
                    gw = []
                for q in range(2):
                    i = pr * 2 + q
                    dq("gpsimd", lambda E, i=i, ib=ib, wb=wb:
                       E.indirect_dma_start(
                        out=wsl[:, wb + i * 8:wb + (i + 1) * 8], out_offset=None,
                        in_=wt_d[:, :],
                        in_offset=IndirectOffsetOnAxis(
                            ap=idx_sb[:, ib + i:ib + i + 1], axis=0)),
                        "GA")
                gaf_pair[(ee, pr)] = fence("gpsimd", "GA", "GAF")

        def emit_transposes(ee):
            xge = xg_of(ee)
            for pr in range(3):
                for q in range(2):
                    i = pr * 2 + q
                    TR = TRA if i % 2 == 0 else TRB
                    w = [("GAF", gaf_pair[(ee, pr)]), ("V", v_idb)]
                    if i >= 2:
                        w.append(("V", xg_ev[(ee, i - 2)]))
                    elif ee >= 1:
                        w.append(("V", xg_ev[(ee - 1, 4 + i)]))
                    for c in range(NC_DM):
                        top(lambda E, pr=pr, q=q, c=c, TR=TR: E.transpose(
                            TR[:, c * 128:(c + 1) * 128],
                            A(xgt, pr * 2048 + q * 1024 + c * 128,
                              [[6144, 128], [1, 128]]),
                            idb[:, :]), waits=(w if c == 0 else ()), inc=(c == 7))
                        w = ()
                    t_tr = S.nt
                    xg_ev[(ee, i)] = vop(lambda E, i=i, TR=TR, xge=xge:
                                         E.tensor_copy(
                        A(xge, i * 128, [[NC_DM * CAP, 128], [CAP, NC_DM], [1, 128]]),
                        A(TR, 0, [[1024, 128], [128, 8], [1, 128]])),
                        waits=[("T", t_tr)])
            tr_last[ee] = S.nt
            v_xg[ee] = xg_ev[(ee, 5)]

        for e in range(NE):
            if e == 0:
                emit_wgroup(0, 0)
            if e > 0:
                ibp = ((e - 1) % 3) * NSC
                for sc in range(NSC):
                    dq("gpsimd", lambda E, sc=sc, ibp=ibp: E.indirect_dma_start(
                        out=yac_d[:, :],
                        out_offset=IndirectOffsetOnAxis(
                            ap=idx_sb[:, ibp + sc:ibp + sc + 1], axis=0),
                        in_=ysw[:, sc * 1024:(sc + 1) * 1024],
                        in_offset=None, compute_op=OP.add), "SC",
                        waits=[("V", yend_v[e - 1]), ("ZY", zy_total),
                               ("SC", S.dma.get("SC", 0))])
                scf_of[e - 1] = fence("gpsimd", "SC", "SCF")
            if e == 0:
                emit_prefetch(0)
            emit_transposes(e)
            xg = xg_of(e)
            ysl_clk = 0
            for g in range(NG):
                segi = e * NG + g
                for fl in range(GF):
                    gsA = segi * 4 + fl // 2
                    pA = gsA % RA
                    scol_w = pA * 2048 + (fl % 2) * 128
                    w = [("GFS", segi * 4 + fl // 2 + 1), ("V", v_xg[e])]
                    if fg > 0:
                        w.append(("S", silu_clk[fg - 1]))
                    if extra_h1[0]:
                        w.append(("V", extra_h1[0])); extra_h1[0] = 0
                    for k in range(NC_DM):
                        for nc2 in range(2):
                            n0 = nc2 * 512
                            nn = 512 if nc2 == 0 else CAP - 512
                            top(lambda E, k=k, n0=n0, nn=nn, sw=scol_w,
                                xg=xg: E.matmul(
                                H1[:, n0:n0 + nn],
                                w1r[:, sw + k * 256:sw + k * 256 + 128],
                                xg[:, k * CAP + n0:k * CAP + n0 + nn],
                                start=(k == 0), stop=(k == NC_DM - 1)),
                                waits=(w if (k == 0 and nc2 == 0) else ()),
                                inc=(k == NC_DM - 1 and nc2 == 1))
                    t_h1 = S.nt
                    w = []
                    if fg > 0:
                        w.append(("V", gmul_clk[fg - 1]))
                    if extra_h3[0]:
                        w.append(("V", extra_h3[0])); extra_h3[0] = 0
                    for k in range(NC_DM):
                        for nc2 in range(2):
                            n0 = nc2 * 512
                            nn = 512 if nc2 == 0 else CAP - 512
                            top(lambda E, k=k, n0=n0, nn=nn, sw=scol_w,
                                xg=xg: E.matmul(
                                H3[:, n0:n0 + nn],
                                w3r[:, sw + k * 256:sw + k * 256 + 128],
                                xg[:, k * CAP + n0:k * CAP + n0 + nn],
                                start=(k == 0), stop=(k == NC_DM - 1)),
                                waits=(w if (k == 0 and nc2 == 0) else ()),
                                inc=(k == NC_DM - 1 and nc2 == 1))
                    t_h3 = S.nt
                    if fl % 2 == 1:
                        freeA[gsA] = t_h3
                    hend[e] = t_h3
                    w = [("T", t_h1)]
                    if fg >= 2:
                        w.append(("V", gmul_clk[fg - 2]))
                    silu_clk[fg] = sop(lambda E, fl=fl: E.activation(
                        s_sb[:, (fl % 2) * CAP:(fl % 2 + 1) * CAP],
                        H1[:, 0:CAP], AF.Silu), waits=w)
                    w = [("T", t_h3), ("S", silu_clk[fg])]
                    if ysl_clk:
                        w.append(("T", ysl_clk))
                    gmul_clk[fg] = vop(lambda E, fl=fl: E.tensor_tensor(
                        g_sb[:, fl * CAP:(fl + 1) * CAP],
                        s_sb[:, (fl % 2) * CAP:(fl % 2 + 1) * CAP],
                        H3[:, 0:CAP], op=OP.mult), waits=w)
                    if e < 2 and fg < LL:
                        emit_kstep(fg)
                    fg += 1
                for sc in range(NSC):
                    w = [("V", gmul_clk[fg - 1])]
                    if extra_ys[0]:
                        w.append(("T", extra_ys[0])); extra_ys[0] = 0
                    if (g, sc) != (0, 0):
                        w.append(("V", S.nv))
                    for fl in range(GF):
                        pB = (segi * 4 + fl // 2) % RB
                        for dmh in range(2):
                            top(lambda E, fl=fl, dmh=dmh, sc=sc, pB=pB: E.matmul(
                                YS[:, dmh * 512:(dmh + 1) * 512],
                                g_sb[:, fl * CAP + sc * 128:fl * CAP + (sc + 1) * 128],
                                w2r[:, pB * 2048 + (fl % 2) * 1024 + dmh * 512:
                                    pB * 2048 + (fl % 2) * 1024 + dmh * 512 + 512],
                                start=(fl == 0), stop=(fl == GF - 1)),
                                waits=(w if (fl == 0 and dmh == 0) else ()),
                                inc=(fl == GF - 1 and dmh == 1))
                    t_y = S.nt
                    ysl_clk = t_y
                    if g == 0:
                        vop(lambda E, sc=sc: E.tensor_copy(
                            ysl[:, sc * 1024:(sc + 1) * 1024], YS[:, :]),
                            waits=[("T", t_y)])
                    else:
                        vop(lambda E, sc=sc: E.tensor_tensor(
                            ysl[:, sc * 1024:(sc + 1) * 1024],
                            ysl[:, sc * 1024:(sc + 1) * 1024], YS[:, :], op=OP.add),
                            waits=[("T", t_y)])
                for j in range(4):
                    freeB[segi * 4 + j] = ysl_clk
                if g == 0 and e + 1 < NE:
                    emit_prefetch(e + 1)
                if g < NG - 1:
                    emit_wgroup(e, g + 1)
                elif e + 1 < NE:
                    emit_wgroup(e + 1, 0)
            yel[e] = S.nv
            for sc in range(NSC):
                wb = (e % 2) * NSC * NE
                wgs = ([("SCF", scf_of[e - 1])] if (e > 0 and sc == 0) else [])
                yend_v[e] = vop(lambda E, sc=sc, e=e, wb=wb: E.tensor_scalar(
                    ysw[:, sc * 1024:(sc + 1) * 1024],
                    ysl[:, sc * 1024:(sc + 1) * 1024],
                    wsl[:, wb + sc * 8 + e:wb + sc * 8 + e + 1], None,
                    op0=OP.mult), waits=wgs)
            if e == 1:
                emit_conv()
        ib7 = ((NE - 1) % 3) * NSC
        for sc in range(NSC):
            dq("gpsimd", lambda E, sc=sc: E.indirect_dma_start(
                out=yac_d[:, :],
                out_offset=IndirectOffsetOnAxis(
                    ap=idx_sb[:, ib7 + sc:ib7 + sc + 1], axis=0),
                in_=ysw[:, sc * 1024:(sc + 1) * 1024],
                in_offset=None, compute_op=OP.add), "SC",
                waits=([("V", yend_v[NE - 1]), ("SC", S.dma.get("SC", 0))]
                       if sc == 0 else []))
        S.wait("gpsimd", "SC", S.dma["SC"])
        S.wait("gpsimd", "CA", S.dma.get("CA", 0))
        S.wait("gpsimd", "XO", 16)
        def fl_ap(sl):
            if sl < 3:
                return [tmp32, kscr, convb][sl][:, :]
            return A(BIG, 0, [[16384, 128], [1, 1024]])
        fsem = ["FL", "FL1", "FL2", "FL3"]

        def emit_fl(c):
            slq = c % 4
            w = [("FA", 16 * (c - 3))] if c >= 4 else []
            dq("gpsimd", lambda E, c=c, slq=slq: E.dma_start(
                out=fl_ap(slq),
                in_=yac_d[c * 128:(c + 1) * 128, :]), fsem[slq], waits=w)

        for c in range(4):
            emit_fl(c)
        for c in range(16):
            slq = c % 4
            dq("gpsimd", lambda E, c=c, slq=slq: E.dma_start(
                out=out_t[c * 128:(c + 1) * 128, :],
                in_=fl_ap(slq),
                accum_op=OP.add), "FA",
                waits=[(fsem[slq], 16 * (c // 4 + 1))])
            if c + 4 < 16:
                emit_fl(c + 4)
        S.wait("gpsimd", "FA", 256)


        with nc.Block() as block:
            @block.sync
            def _(E):
                for fn in S.ops["sync"]:
                    fn(E, sem_handles)

            @block.gpsimd
            def _(E):
                for fn in S.ops["gpsimd"]:
                    fn(E, sem_handles)

            @block.scalar
            def _(E):
                for fn in S.ops["scalar"]:
                    fn(E, sem_handles)

            @block.vector
            def _(E):
                for fn in S.ops["vector"]:
                    fn(E, sem_handles)

            @block.tensor
            def _(E):
                for fn in S.ops["tensor"]:
                    fn(E, sem_handles)
    return nc


def _layouts(inputs):
    Wg = np.ascontiguousarray(np.asarray(inputs["L0_ffn_Wg"], np.float32))
    wg2 = np.ascontiguousarray(Wg.reshape(8, 128, 8).transpose(1, 0, 2).reshape(128, 64))
    d2 = np.ascontiguousarray(np.asarray(inputs["L0_attn_D"], np.float32).reshape(8, 128).T)
    ldt2 = np.ascontiguousarray(
        np.asarray(inputs["L0_attn_log_dt"], np.float32).reshape(8, 128).T)
    ident = np.eye(128, dtype=np.float32)
    return wg2, d2, ldt2, ident


def kernel(L0_attn_u, L0_attn_A, L0_attn_B, L0_attn_C, L0_attn_D, L0_attn_log_dt,
           L0_ffn_Wg, L0_ffn_W1, L0_ffn_W3, L0_ffn_W2):
    u = np.ascontiguousarray(np.asarray(L0_attn_u, np.float32))
    W1 = np.ascontiguousarray(np.asarray(L0_ffn_W1, np.float32))
    W3 = np.ascontiguousarray(np.asarray(L0_ffn_W3, np.float32))
    W2 = np.ascontiguousarray(np.asarray(L0_ffn_W2, np.float32))
    Am = np.ascontiguousarray(np.asarray(L0_attn_A, np.float32).reshape(1, DS))
    Bm = np.ascontiguousarray(np.asarray(L0_attn_B, np.float32))
    Cm = np.ascontiguousarray(np.asarray(L0_attn_C, np.float32))
    ins = {"L0_ffn_Wg": L0_ffn_Wg, "L0_attn_D": L0_attn_D,
           "L0_attn_log_dt": L0_attn_log_dt}
    wg2, d2, ldt2, ident = _layouts(ins)
    nc = build_nc()
    in_maps = []
    for b in range(NCORES):
        in_maps.append({
            "u": np.ascontiguousarray(u[b]),
            "w1": W1, "w3": W3, "w2": W2, "wg2": wg2,
            "a_v": Am, "b_m": Bm, "c_m": Cm, "d2": d2, "logdt2": ldt2,
            "ident": ident,
        })
    res = run_bass_kernel_spmd(nc, in_maps, core_ids=list(range(NCORES)))
    out = np.stack([np.asarray(res.results[b]["out"]) for b in range(NCORES)], axis=0)
    return out


if __name__ == "__main__":
    nc = build_nc()
    print("build ok")


# version 13
# speedup vs baseline: 1.0548x; 1.0121x over previous
import numpy as np
import concourse.bass as bass
import concourse.mybir as mybir
from concourse.bass_utils import run_bass_kernel_spmd
from concourse.bass_core_rust import IndirectOffsetOnAxis

F32 = mybir.dt.float32
BF16 = mybir.dt.bfloat16
U32 = mybir.dt.uint32
AF = mybir.ActivationFunctionType
OP = mybir.AluOpType
AX = mybir.AxisListType

NCORES = 8
T = 2048
DM = 1024
DS = 64
LL = 64
FF = 4096
NE = 8
TH = 1024
NC_DM = 8
NG = 4
GF = 8
RA = 6
RB = 5
CAP = 768
NSC = CAP // 128
NSLOT = NE * CAP
PADL, PADR = 32, 31
CW = PADL + T + PADR


class _Sched:
    def __init__(self):
        self.ops = {k: [] for k in ("sync", "gpsimd", "scalar", "vector", "tensor")}
        self.nv = 0
        self.ns = 0
        self.nt = 0
        self.lw = {k: {} for k in self.ops}
        self.dma = {}

    def emit(self, eng, fn):
        self.ops[eng].append(fn)

    def wait(self, eng, key, val):
        if val is None or val <= 0:
            return
        if self.lw[eng].get(key, 0) >= val:
            return
        self.lw[eng][key] = val
        self.emit(eng, lambda E, sems, k=key, v=val: E.wait_ge(sems[k], v))


def build_nc():
    nc = bass.Bass("TRN2", target_bir_lowering=False, detect_race_conditions=False)

    u_t = nc.dram_tensor("u", [T, DM], F32, kind="ExternalInput")
    w1_t = nc.dram_tensor("w1", [NE, DM, FF], F32, kind="ExternalInput")
    w3_t = nc.dram_tensor("w3", [NE, DM, FF], F32, kind="ExternalInput")
    w2_t = nc.dram_tensor("w2", [NE, FF, DM], F32, kind="ExternalInput")
    wg_t = nc.dram_tensor("wg2", [128, 64], F32, kind="ExternalInput")
    a_t = nc.dram_tensor("a_v", [1, DS], F32, kind="ExternalInput")
    b_t = nc.dram_tensor("b_m", [DS, DM], F32, kind="ExternalInput")
    c_t = nc.dram_tensor("c_m", [DM, DS], F32, kind="ExternalInput")
    d_t = nc.dram_tensor("d2", [128, NC_DM], F32, kind="ExternalInput")
    ldt_t = nc.dram_tensor("logdt2", [128, NC_DM], F32, kind="ExternalInput")
    id_t = nc.dram_tensor("ident", [128, 128], F32, kind="ExternalInput")
    out_t = nc.dram_tensor("out", [T, DM], F32, kind="ExternalOutput")
    xnt_d = nc.dram_tensor("xnt_d", [T + 128, DM], BF16, kind="Internal")
    yac_d = nc.dram_tensor("yac_d", [T + 128, DM], F32, kind="Internal")
    tok_d = nc.dram_tensor("tok_d", [NSLOT + 128, 1], U32, kind="Internal")
    wt_d = nc.dram_tensor("wt_d", [T + 128, NE], F32, kind="Internal")

    from contextlib import ExitStack
    with ExitStack() as ctx:
        EC = ctx.enter_context
        BIG = EC(nc.sbuf_tensor("BIG", [128, 16384], F32))
        xnf = EC(nc.sbuf_tensor("xnf", [128, NC_DM * CW], BF16))
        w1r = EC(nc.sbuf_tensor("w1r", [128, RA * 2048], BF16))
        w3r = EC(nc.sbuf_tensor("w3r", [128, RA * 2048], BF16))
        w2r = EC(nc.sbuf_tensor("w2r", [128, RB * 2048], BF16))
        convb = EC(nc.sbuf_tensor("convb", [128, TH], F32))
        kscr = EC(nc.sbuf_tensor("kscr", [128, 1024], F32))
        tmp32 = EC(nc.sbuf_tensor("tmp32", [128, 1024], F32))
        kt = EC(nc.sbuf_tensor("kt", [128, NC_DM * LL], F32))
        bct = EC(nc.sbuf_tensor("bct", [128, NC_DM * DS], F32))
        pt_sb = EC(nc.sbuf_tensor("pt_sb", [128, NC_DM * DS], F32))
        c_sb = EC(nc.sbuf_tensor("c_sb", [128, NC_DM * DS], F32))
        a_b = EC(nc.sbuf_tensor("a_b", [128, DS], F32))
        dtc = EC(nc.sbuf_tensor("dtc", [128, NC_DM], F32))
        d_sb = EC(nc.sbuf_tensor("d_sb", [128, NC_DM], F32))
        ldt_sb = EC(nc.sbuf_tensor("ldt_sb", [128, NC_DM], F32))
        wg_sb = EC(nc.sbuf_tensor("wg_sb", [128, 64], F32))
        ident = EC(nc.sbuf_tensor("identt", [128, 128], F32))
        idb = EC(nc.sbuf_tensor("idb", [128, 128], BF16))
        diag = EC(nc.sbuf_tensor("diag", [128, 2 * 128], BF16))
        csr = EC(nc.sbuf_tensor("csr", [1, DS], F32))
        csc = EC(nc.sbuf_tensor("csc", [DS, 1], F32))
        ones_c = EC(nc.sbuf_tensor("ones_c", [128, 1], F32))
        ones_r = EC(nc.sbuf_tensor("ones_r", [1, 128], F32))
        stats = EC(nc.sbuf_tensor("stats", [128, 12], F32))
        aggr16 = EC(nc.sbuf_tensor("aggr16", [128, 32], F32))
        rstd16 = EC(nc.sbuf_tensor("rstd16", [128, 16], F32))
        sq16 = EC(nc.sbuf_tensor("sq16", [128, 16], F32))
        lgb = EC(nc.sbuf_tensor("lgb", [128, 8], F32))
        eps_sb = EC(nc.sbuf_tensor("eps_sb", [128, 1], F32))
        rsc = EC(nc.sbuf_tensor("rsc", [128, 48], F32))
        wtok = EC(nc.sbuf_tensor("wtok", [128, 128], F32))
        eqsav = EC(nc.sbuf_tensor("eqsav", [128, 256], F32))
        e12 = EC(nc.sbuf_tensor("e12", [128, 32], F32))
        rktok = EC(nc.sbuf_tensor("rktok", [128, 128], F32))
        iotap = EC(nc.sbuf_tensor("iotap", [128, 1], F32))
        iota8 = EC(nc.sbuf_tensor("iota8", [128, 8], F32))
        iotu = EC(nc.sbuf_tensor("iotu", [128, 49], U32))
        carry = EC(nc.sbuf_tensor("carry", [8, 2], F32))
        t8 = EC(nc.sbuf_tensor("t8", [128, 8], F32))
        scol = EC(nc.sbuf_tensor("scol", [128, 8], F32))
        sidx = EC(nc.sbuf_tensor("sidx", [128, 4], U32))
        toku = EC(nc.sbuf_tensor("toku", [128, 2], U32))
        tki = EC(nc.sbuf_tensor("tki", [128, 32], U32))
        idx_sb = EC(nc.sbuf_tensor("idx_sb", [128, 3 * NSC], U32))
        wsl = EC(nc.sbuf_tensor("wsl", [128, 2 * NSC * NE], F32))
        s_sb = EC(nc.sbuf_tensor("s_sb", [128, 2 * CAP], BF16))
        xgt = EC(nc.sbuf_tensor("xgt", [128, 3 * 2048], BF16))
        b_sb = convb
        big_addr = nc.lookup_mloc(BIG).addr

        def at(name, shape, dtype, off):
            return nc.alloc_sbuf_tensor_at(name, shape, dtype, offset=big_addr + off)
        g_sb = at("g_sb", [128, GF * CAP], BF16, 0)
        xgA = at("xgA", [128, NC_DM * CAP], BF16, 12288)
        ysl = at("ysl", [128, NSC * 1024], F32, 24576)
        ysw = at("ysw", [128, NSC * 1024], BF16, 49152)
        flb = nc.alloc_sbuf_tensor_at(
            "flb", [128, 1024], F32, offset=nc.lookup_mloc(xnf).addr)

        H1 = EC(nc.psum_tensor("H1", [128, 1024], F32))
        H3 = EC(nc.psum_tensor("H3", [128, 1024], F32))
        YS = EC(nc.psum_tensor("YS", [128, 1024], F32))
        TRA = EC(nc.psum_tensor("TRA", [128, 1024], BF16))
        TRB = EC(nc.psum_tensor("TRB", [128, 1024], BF16))

        sem_handles = {}
        for k in ("T", "V", "S", "DW", "GFS", "U", "P", "XT", "TI", "TK", "WD",
                  "GA", "GAF", "SC", "SCF", "XD", "XDF", "XO", "CA", "CAF",
                  "ZY", "FA", "IO", "TKF", "FL", "FL1", "FL2", "FL3",
                  "U1", "U2", "U3"):
            sem_handles[k] = EC(nc.semaphore("sem" + k))

        S = _Sched()

        def vop(fn, waits=()):
            for k, v in waits:
                S.wait("vector", k, v)
            S.wait("vector", "V", S.nv)
            S.emit("vector", lambda E, sems, f=fn: f(E).then_inc(sems["V"], 1))
            S.nv += 1
            return S.nv

        def sop(fn, waits=()):
            for k, v in waits:
                S.wait("scalar", k, v)
            S.wait("scalar", "S", S.ns)
            S.emit("scalar", lambda E, sems, f=fn: f(E).then_inc(sems["S"], 1))
            S.ns += 1
            return S.ns

        def top(fn, waits=(), inc=False):
            for k, v in waits:
                S.wait("tensor", k, v)
            if inc:
                S.emit("tensor", lambda E, sems, f=fn: f(E).then_inc(sems["T"], 1))
                S.nt += 1
                return S.nt
            S.emit("tensor", lambda E, sems, f=fn: f(E))
            return None

        def dq(eng, fn, sem, waits=()):
            for k, v in waits:
                S.wait(eng, k, v)
            S.emit(eng, lambda E, sems, f=fn, sm=sem: f(E).then_inc(sems[sm], 16))
            S.dma[sem] = S.dma.get(sem, 0) + 16
            return S.dma[sem]

        def fence(eng, sem_dma, sem_f):
            S.wait(eng, sem_dma, S.dma.get(sem_dma, 0))
            S.emit(eng, lambda E, sems, sf=sem_f: E.sem_inc(sems[sf], 1))
            S.dma[sem_f] = S.dma.get(sem_f, 0) + 1
            return S.dma[sem_f]

        A = bass.AP

        usem = ["U", "U1", "U2", "U3"]
        for tb in range(16):
            if tb >= 4:
                S.wait("sync", usem[tb % 4], 16 * (tb // 4))
            dq("sync", lambda E, tb=tb: E.dma_start(
                out=BIG[:, tb * 1024:(tb + 1) * 1024],
                in_=A(u_t, tb * 128 * 1024, [[1024, 128], [1, 1024]])),
                usem[tb % 4])
        for fn in (lambda E: E.dma_start(out=ident[:, :], in_=id_t[:, :]),
                   lambda E: E.dma_start(out=A(c_sb, 0, [[512, 128], [64, 8], [1, 64]]),
                                         in_=A(c_t, 0, [[64, 128], [8192, 8], [1, 64]])),
                   lambda E: E.dma_start(out=b_sb[0:DS, 0:DM], in_=b_t[:, :]),
                   lambda E: E.dma_start(out=A(a_b, 0, [[64, 1], [1, 64]]), in_=a_t[:, :]),
                   lambda E: E.dma_start(out=ldt_sb[:, :], in_=ldt_t[:, :]),
                   lambda E: E.dma_start(out=d_sb[:, :], in_=d_t[:, :]),
                   lambda E: E.dma_start(out=wg_sb[:, :], in_=wg_t[:, :])):
            dq("sync", fn, "P")
        v_par = S.dma["P"]

        dq("gpsimd", lambda E: E.dma_start(out=out_t[:, :], in_=u_t[:, :]), "XO")
        S.emit("gpsimd", lambda E, sems: E.iota(
            iotu[:, :], [[0, 49]], base=T, channel_multiplier=1,
            allow_small_or_imprecise_dtypes=True).then_inc(sems["IO"], 1))
        S.emit("gpsimd", lambda E, sems: E.iota(
            iotap[:, 0:1], [[0, 1]], base=0, channel_multiplier=1,
            allow_small_or_imprecise_dtypes=True).then_inc(sems["IO"], 1))
        S.emit("gpsimd", lambda E, sems: E.iota(
            iota8[:, :], [[1, 8]], base=0, channel_multiplier=0,
            allow_small_or_imprecise_dtypes=True).then_inc(sems["IO"], 1))
        S.emit("gpsimd", lambda E, sems: E.memset(
            tmp32[:, :], 0).then_inc(sems["IO"], 1))
        S.emit("gpsimd", lambda E, sems: E.iota(
            tki[:, :], [[128, 16], [0, 2]], base=0, channel_multiplier=1,
            allow_small_or_imprecise_dtypes=True).then_inc(sems["IO"], 1))
        S.wait("gpsimd", "IO", 5)
        dq("gpsimd", lambda E: E.dma_start(
            out=A(tok_d, 0, [[49, 128], [1, 49]]),
            in_=A(iotu, 0, [[49, 128], [1, 49]])), "TI")
        v_ti = 3
        dq("gpsimd", lambda E: E.dma_start(
            out=A(yac_d, 0, [[1024, 128], [131072, 17], [1, 1024]]),
            in_=A(tmp32, 0, [[1024, 128], [0, 17], [1, 1024]])), "ZY")
        dq("gpsimd", lambda E: E.dma_start(
            out=A(xnt_d, T * DM, [[1024, 128], [1, 1024]]),
            in_=A(tmp32, 0, [[1024, 128], [1, 1024]])), "ZY")
        dq("gpsimd", lambda E: E.dma_start(
            out=A(wt_d, T * NE, [[8, 128], [1, 8]]),
            in_=A(tmp32, 0, [[1024, 128], [1, 8]])), "ZY")
        zy_total = S.dma["ZY"]

        vop(lambda E: E.memset(ones_c[:, :], 1.0))
        vop(lambda E: E.memset(eps_sb[:, :], 1e-5))
        vop(lambda E: E.memset(ones_r[:, :], 1.0))
        vop(lambda E: E.memset(A(xnf, 0, [[NC_DM * CW, 128], [CW, NC_DM], [1, PADL]]), 0.0))
        v_const = vop(lambda E: E.memset(
            A(xnf, PADL + T, [[NC_DM * CW, 128], [CW, NC_DM], [1, PADR]]), 0.0))
        v_idb = vop(lambda E: E.tensor_copy(idb[:, :], ident[:, :]),
                    waits=[("P", v_par)])

        for cp in range(NC_DM):
            t_csum = top(lambda E, cp=cp: E.matmul(
                H1[0:1, cp * 64:(cp + 1) * 64], ones_c[:, 0:1],
                c_sb[:, cp * 64:(cp + 1) * 64], start=True, stop=True),
                waits=([("P", v_par), ("V", v_const)] if cp == 0 else ()),
                inc=(cp == NC_DM - 1))
        v_csr = vop(lambda E: E.tensor_reduce(
            csr[0:1, 0:DS], A(H1, 0, [[1024, 1], [1, DS], [64, NC_DM]]),
            axis=AX.X, op=OP.add), waits=[("T", t_csum)])
        t_csc = top(lambda E: E.transpose(H3[0:DS, 0:1], csr[0:1, 0:DS], ident[0:1, 0:1]),
                    waits=[("V", v_csr)], inc=True)
        v_csc = vop(lambda E: E.tensor_copy(csc[:, :], H3[0:DS, 0:1]), waits=[("T", t_csc)])
        v_bc = vop(lambda E: E.tensor_scalar(b_sb[0:DS, 0:DM], b_sb[0:DS, 0:DM],
                                             csc[:, 0:1], None, op0=OP.mult))
        for cp in range(NC_DM):
            t_bct = top(lambda E, cp=cp: E.transpose(
                H1[:, cp * 64:(cp + 1) * 64], b_sb[0:DS, cp * 128:(cp + 1) * 128],
                ident[0:DS, 0:DS]),
                waits=([("V", v_bc)] if cp == 0 else ()), inc=(cp == NC_DM - 1))
        v_bct = vop(lambda E: E.tensor_copy(bct[:, :], H1[:, 0:512]), waits=[("T", t_bct)])
        t_ab = top(lambda E: E.matmul(H3[:, 0:DS], ones_r[0:1, 0:128],
                                      A(a_b, 0, [[64, 1], [1, 64]]), start=True, stop=True),
                   waits=[("V", v_csc)], inc=True)
        v_ab = vop(lambda E: E.tensor_copy(a_b[:, :], H3[:, 0:DS]), waits=[("T", t_ab)])
        s_dtc = sop(lambda E: E.activation(dtc[:, :], ldt_sb[:, :], AF.Exp),
                    waits=[("P", v_par)])
        for c in range(NC_DM):
            v_pt = vop(lambda E, c=c: E.tensor_scalar(
                pt_sb[:, c * DS:(c + 1) * DS], a_b[:, :], dtc[:, c:c + 1], None,
                op0=OP.mult), waits=[("S", s_dtc)])
        v_ptdone = v_pt

        v_apply = {}
        for tb in range(16):
            vop(lambda E, tb=tb: E.bn_stats(stats[:, 0:6], BIG[:, tb * 1024:tb * 1024 + 512]),
                waits=[(usem[tb % 4], 16 * (tb // 4 + 1))])
            vop(lambda E, tb=tb: E.bn_stats(stats[:, 6:12],
                                            BIG[:, tb * 1024 + 512:tb * 1024 + 1024]))
            v_ag = vop(lambda E, tb=tb: E.bn_aggr(aggr16[:, 2 * tb:2 * tb + 2],
                                                  stats[:, 0:12]))
            s_sq = sop(lambda E, tb=tb: E.activation(sq16[:, tb:tb + 1],
                                                     aggr16[:, 2 * tb + 1:2 * tb + 2],
                                                     AF.Sqrt, bias=eps_sb[:, 0:1]),
                       waits=[("V", v_ag)])
            vop(lambda E, tb=tb: E.reciprocal(rstd16[:, tb:tb + 1], sq16[:, tb:tb + 1]),
                waits=[("S", s_sq)])
            v_apply[tb] = vop(lambda E, tb=tb: E.tensor_scalar(
                BIG[:, tb * 1024:(tb + 1) * 1024], BIG[:, tb * 1024:(tb + 1) * 1024],
                aggr16[:, 2 * tb:2 * tb + 1], rstd16[:, tb:tb + 1],
                op0=OP.subtract, op1=OP.mult))
        v_lnapply = S.nv
        dq("gpsimd", lambda E: E.dma_start(
            out=A(xnt_d, 0, [[1024, 128], [131072, 16], [1, 1024]]),
            in_=A(BIG, 0, [[16384, 128], [1024, 16], [1, 1024]])), "XT",
            waits=[("V", v_lnapply)])

        ev_half = {}
        ev_clk = {}
        mm_clk = {}
        lg_clk = {}
        for tb in range(16):
            t_half = {}
            for half in range(2):
                for c in range(4 * half, 4 * half + 4):
                    w = [("V", v_apply[tb])]
                    if tb > 0:
                        w.append(("V", ev_half[(tb - 1, half)]))
                    elif half == 0 and c == 0:
                        w.append(("V", v_ab))
                    top(lambda E, tb=tb, c=c: E.transpose(
                        H3[:, c * 128:(c + 1) * 128],
                        BIG[:, tb * 1024 + c * 128:tb * 1024 + (c + 1) * 128],
                        ident[:, :]), waits=(w if c == 4 * half else ()),
                        inc=(c == 4 * half + 3))
                t_half[half] = S.nt
            for half in range(2):
                for c in range(4 * half, 4 * half + 4):
                    vop(lambda E, tb=tb, c=c: E.tensor_copy(
                        xnf[:, c * CW + PADL + tb * 128:c * CW + PADL + (tb + 1) * 128],
                        H3[:, c * 128:(c + 1) * 128]), waits=[("T", t_half[half])])
                    ev_clk[(tb, c)] = vop(lambda E, tb=tb, c=c: E.tensor_copy(
                        tmp32[:, c * 128:(c + 1) * 128],
                        H3[:, c * 128:(c + 1) * 128]),
                        waits=([("ZY", zy_total)] if (tb, c) == (0, 0) else ()))
                ev_half[(tb, half)] = ev_clk[(tb, 4 * half + 3)]
            for c in range(NC_DM):
                w = [("V", ev_clk[(tb, c)]), ("P", v_par)]
                if c == 0 and tb > 0:
                    w.append(("V", lg_clk[tb - 1]))
                elif c == 0:
                    w.append(("V", v_bct))
                mm_clk[(tb, c)] = top(lambda E, tb=tb, c=c: E.matmul(
                    H1[:, 0:8], tmp32[:, c * 128:(c + 1) * 128],
                    wg_sb[:, c * 8:(c + 1) * 8], start=(c == 0), stop=(c == NC_DM - 1)),
                    waits=w, inc=True)
            t_logit_last = mm_clk[(tb, NC_DM - 1)]
            m1 = rsc[:, 0:1]; m2 = rsc[:, 1:2]; dd = rsc[:, 2:3]
            w1s = rsc[:, 3:4]; w2s = rsc[:, 4:5]
            l2 = rsc[:, 8:16]
            eq1 = eqsav[:, tb * 16:tb * 16 + 8]
            eq2 = eqsav[:, tb * 16 + 8:tb * 16 + 16]
            sc1 = rsc[:, 16:24]; sc2 = rsc[:, 24:32]
            lg_clk[tb] = vop(lambda E: E.tensor_copy(lgb[:, :], H1[:, 0:8]),
                             waits=[("T", t_logit_last)])
            vop(lambda E: E.reduce_max(m1, lgb[:, :], axis=AX.X))
            vop(lambda E, eq1=eq1: E.tensor_scalar(eq1, lgb[:, :], m1, None,
                                                   op0=OP.is_equal))
            vop(lambda E, eq1=eq1: E.scalar_tensor_tensor(l2, eq1, -1e30, lgb[:, :],
                                                          op0=OP.mult, op1=OP.add))
            vop(lambda E: E.reduce_max(m2, l2, axis=AX.X))
            vop(lambda E, eq2=eq2: E.tensor_scalar(eq2, l2, m2, None, op0=OP.is_equal))
            vop(lambda E, eq1=eq1: E.tensor_tensor(t8[:, :], eq1, iota8[:, :], op=OP.mult),
                waits=[("IO", v_ti)])
            vop(lambda E, tb=tb: E.tensor_reduce(e12[:, 2 * tb:2 * tb + 1], t8[:, :],
                                                 axis=AX.X, op=OP.add))
            vop(lambda E, eq2=eq2: E.tensor_tensor(t8[:, :], eq2, iota8[:, :], op=OP.mult))
            vop(lambda E, tb=tb: E.tensor_reduce(e12[:, 2 * tb + 1:2 * tb + 2], t8[:, :],
                                                 axis=AX.X, op=OP.add))
            v_d2 = vop(lambda E: E.tensor_tensor(dd, m1, m2, op=OP.subtract))
            s_sig = sop(lambda E: E.activation(w1s, dd, AF.Sigmoid), waits=[("V", v_d2)])
            vop(lambda E: E.tensor_scalar(w2s, w1s, -1.0, 1.0, op0=OP.mult, op1=OP.add),
                waits=[("S", s_sig)])
            vop(lambda E, eq1=eq1: E.tensor_scalar(sc1, eq1, w1s, None, op0=OP.mult))
            vop(lambda E, eq2=eq2: E.tensor_scalar(sc2, eq2, w2s, None, op0=OP.mult))
            vop(lambda E, tb=tb: E.tensor_tensor(wtok[:, tb * 8:(tb + 1) * 8],
                                                 sc1, sc2, op=OP.add))
        v_passE = S.nv
        t_passD = t_logit_last
        dq("sync", lambda E: E.dma_start(
            out=A(wt_d, 0, [[8, 128], [1024, 16], [1, 8]]),
            in_=A(wtok, 0, [[128, 128], [8, 16], [1, 8]])), "WD",
            waits=[("V", v_passE)])

        maskh = kscr[0:8, 0:1024]
        inclh = convb[0:8, 0:1024]
        v_mev = {}
        t_mtr = {}
        v_rkev = {}
        for h2 in range(2):
            v_t8v = {}
            for j in range(8):
                tb = h2 * 8 + j
                w = []
                if h2 == 1 and j == 0:
                    w.append(("T", t_mtr[(0, 1)]))
                v_t8v[j] = vop(lambda E, tb=tb, j=j: E.tensor_tensor(
                    tmp32[:, j * 8:(j + 1) * 8], eqsav[:, tb * 16:tb * 16 + 8],
                    eqsav[:, tb * 16 + 8:tb * 16 + 16], op=OP.add), waits=w)
            for jh in range(2):
                w = [("V", v_t8v[7]), ("T", t_passD)]
                if h2 == 1:
                    w.append(("V", v_mev[(0, 4 * jh + 3)]))
                for j in range(4 * jh, 4 * jh + 4):
                    top(lambda E, j=j: E.transpose(
                        A(YS, j * 128, [[1024, 8], [1, 128]]),
                        tmp32[:, j * 8:(j + 1) * 8], ident[:, :]),
                        waits=(w if j == 4 * jh else ()), inc=(j == 4 * jh + 3))
                t_mtr[(h2, jh)] = S.nt
                for j in range(4 * jh, 4 * jh + 4):
                    v_mev[(h2, j)] = vop(lambda E, j=j: E.tensor_copy(
                        maskh[0:8, j * 128:(j + 1) * 128],
                        A(YS, j * 128, [[1024, 8], [1, 128]])),
                        waits=[("T", t_mtr[(h2, jh)])])
            zb8 = ysl[0:8, 0:1024]
            vop(lambda E, zb8=zb8: E.memset(zb8, 0.0))
            if h2 == 0:
                vop(lambda E, zb8=zb8: E.tensor_tensor_scan(
                    inclh, maskh, zb8, 0.0, op0=OP.add, op1=OP.add))
            else:
                vop(lambda E, zb8=zb8: E.tensor_tensor_scan(
                    inclh, maskh, zb8, carry[:, 0:1], op0=OP.add, op1=OP.add))
            vop(lambda E: E.tensor_copy(carry[:, 0:1], inclh[0:8, 1023:1024]))
            v_sub = vop(lambda E: E.tensor_tensor(inclh, inclh, maskh, op=OP.subtract))
            for j in range(8):
                w = [("V", v_sub)]
                if h2 == 1 and j == 0:
                    w.append(("V", v_rkev[0]))
                top(lambda E, j=j: E.transpose(
                    A(YS, j * 8, [[1024, 128], [1, 8]]),
                    inclh[0:8, j * 128:(j + 1) * 128], ident[0:8, 0:8]),
                    waits=(w if j == 0 else ()), inc=(j == 7))
            t_rt = S.nt
            v_rkev[h2] = vop(lambda E, h2=h2: E.tensor_copy(
                rktok[:, h2 * 64:(h2 + 1) * 64],
                A(YS, 0, [[1024, 128], [1, 64]])), waits=[("T", t_rt)])
        rk2 = kscr
        prod = convb
        r_all = rsc[:, 0:32]
        sl_all = wtok[:, 32:64]
        ov_all = wtok[:, 0:32]
        ot_all = wtok[:, 64:96]
        tr_all = wtok[:, 96:128]
        for k2 in range(2):
            vop(lambda E, k2=k2: E.tensor_copy(
                A(rk2, k2 * 8, [[1024, 128], [16, 16], [1, 8]]),
                A(rktok, 0, [[128, 128], [8, 16], [1, 8]])),
                waits=([("WD", 16)] if k2 == 0 else []))
        vop(lambda E: E.tensor_tensor(prod[:, 0:256], eqsav[:, :],
                                      rk2[:, 0:256], op=OP.mult))
        vop(lambda E: E.tensor_reduce(
            r_all, A(prod, 0, [[1024, 128], [8, 32], [1, 8]]),
            axis=AX.X, op=OP.add))
        vop(lambda E: E.tensor_scalar(scol[:, 6:7], iotap[:, 0:1], 1.0,
                                      float(NSLOT), op0=OP.mult, op1=OP.add),
            waits=[("IO", v_ti)])
        vop(lambda E: E.scalar_tensor_tensor(
            sl_all, e12[:, :], float(CAP), r_all, op0=OP.mult, op1=OP.add))
        vop(lambda E: E.tensor_scalar(ov_all, r_all, float(CAP), None,
                                      op0=OP.is_ge))
        vop(lambda E: E.tensor_scalar(ot_all, ov_all, -1.0, 1.0,
                                      op0=OP.mult, op1=OP.add))
        vop(lambda E: E.tensor_tensor(sl_all, sl_all, ot_all, op=OP.mult))
        vop(lambda E: E.tensor_scalar(tr_all, ov_all, scol[:, 6:7], None,
                                      op0=OP.mult))
        vop(lambda E: E.tensor_tensor(sl_all, sl_all, tr_all, op=OP.add))
        v_slots = vop(lambda E: E.tensor_copy(iotu[:, 0:32], sl_all),
                      waits=[("TI", 16)])
        for ii in range(32):
            w_tk = [("V", v_slots), ("TI", 16), ("IO", 5)]
            if ii == 0:
                w_tk += [("XO", 16), ("ZY", zy_total), ("XT", 16)]
            dq("gpsimd", lambda E, ii=ii: E.indirect_dma_start(
                out=tok_d[:, :],
                out_offset=IndirectOffsetOnAxis(ap=iotu[:, ii:ii + 1], axis=0),
                in_=tki[:, ii:ii + 1], in_offset=None), "TK",
                waits=(w_tk if ii == 0 else []))
        v_passF = S.nv
        tk_total = S.dma["TK"]

        fg = 0
        silu_clk = {}
        gmul_clk = {}
        kred = {}
        freeA = {}
        freeB = {}
        extra_h1 = [0]
        extra_h3 = [0]
        extra_ys = [0]
        conv_t_end = [0]
        hend = {}
        yend_v = {}
        yel = {}
        kstate = {"e1": None}

        def emit_kstep(l):
            E1s = kscr[:, 0:512]
            CH = kscr[:, 512:1024]
            M = convb[:, 0:512]
            if l == 0:
                kred[0] = vop(lambda E: E.tensor_reduce(
                    A(kt, 0, [[NC_DM * LL, 128], [LL, NC_DM], [1, 1]]),
                    A(bct, 0, [[512, 128], [64, 8], [1, DS]]),
                    axis=AX.X, op=OP.add),
                    waits=[("V", v_bct), ("V", v_passF)])
                kstate["e1"] = sop(lambda E: E.activation(E1s, pt_sb[:, :], AF.Exp),
                                   waits=[("V", v_ptdone), ("V", v_passF)])
                return
            if l == 1:
                cur = E1s
            elif l == 2:
                vop(lambda E: E.tensor_tensor(CH, E1s, E1s, op=OP.mult),
                    waits=[("S", kstate["e1"])])
                cur = CH
            else:
                vop(lambda E: E.tensor_tensor(CH, CH, E1s, op=OP.mult))
                cur = CH
            vop(lambda E, cur=cur: E.tensor_tensor(M, cur, bct[:, :], op=OP.mult),
                waits=([("S", kstate["e1"])] if l == 1 else []))
            kred[l] = vop(lambda E, l=l: E.tensor_reduce(
                A(kt, l, [[NC_DM * LL, 128], [LL, NC_DM], [1, 1]]),
                A(convb, 0, [[1024, 128], [64, 8], [1, DS]]),
                axis=AX.X, op=OP.add))

        def emit_conv():
            conv_ev = {}
            caf_prev = [0]
            ev_bank = {0: 0, 1: 0}
            for c in range(NC_DM):
                dg_clk = {}
                cmm_clk = {}
                for l in range(LL):
                    w = [("V", kred[LL - 1])]
                    if l >= 2:
                        w.append(("T", cmm_clk[l - 2]))
                    dg_clk[l] = vop(lambda E, c=c, l=l: E.tensor_scalar(
                        A(diag, (l % 2) * 128, [[256, 128], [1, 128]]),
                        ident[:, :], kt[:, c * LL + l:c * LL + l + 1], None,
                        op0=OP.mult), waits=w)
                    tw = [("V", dg_clk[l])]
                    if l == 0 and c > 0:
                        tw.append(("V", conv_ev[(c - 1, 1)]))
                    elif l == 0:
                        tw.append(("S", silu_clk[fg - 1]))
                        tw.append(("V", gmul_clk[fg - 1]))
                    first = True
                    for h in range(2):
                        PS = H1 if h == 0 else H3
                        for tc in range(2):
                            top(lambda E, c=c, l=l, tc=tc, h=h, PS=PS: E.matmul(
                                PS[:, tc * 512:(tc + 1) * 512],
                                A(diag, (l % 2) * 128, [[256, 128], [1, 128]]),
                                xnf[:, c * CW + h * TH + tc * 512 + 63 - l:
                                    c * CW + h * TH + tc * 512 + 63 - l + 512],
                                start=(l == 0), stop=(l == LL - 1)),
                                waits=(tw if first else ()),
                                inc=(h == 1 and tc == 1))
                            first = False
                    cmm_clk[l] = S.nt
                for h in range(2):
                    PS = H1 if h == 0 else H3
                    conv_ev[(c, h)] = vop(lambda E, c=c, h=h, PS=PS:
                                          E.scalar_tensor_tensor(
                        convb[:, :],
                        xnf[:, c * CW + PADL + h * TH:c * CW + PADL + (h + 1) * TH],
                        d_sb[:, c:c + 1], PS[:, :], op0=OP.mult, op1=OP.add),
                        waits=[("T", cmm_clk[LL - 1])])
                    tw = [("V", conv_ev[(c, h)])]
                    if c == 0 and h == 0:
                        tw.append(("V", yel[1]))
                    if ev_bank[0]:
                        tw.append(("V", ev_bank[0]))
                    for tb in range(4):
                        top(lambda E, tb=tb: E.transpose(
                            YS[:, tb * 128:(tb + 1) * 128],
                            convb[:, tb * 128:(tb + 1) * 128], ident[:, :]),
                            waits=(tw if tb == 0 else ()), inc=(tb == 3))
                    t_b0 = S.nt
                    for tb in range(4, 8):
                        top(lambda E, tb=tb: E.transpose(
                            YS[:, tb * 128:(tb + 1) * 128],
                            convb[:, tb * 128:(tb + 1) * 128], ident[:, :]),
                            waits=([("V", ev_bank[1])]
                                   if (tb == 4 and ev_bank[1]) else ()),
                            inc=(tb == 7))
                    t_b1 = S.nt
                    w2 = [("T", t_b0)]
                    if caf_prev[0]:
                        w2.append(("CAF", caf_prev[0]))
                    ev_bank[0] = vop(lambda E: E.tensor_copy(
                        tmp32[:, 0:512], YS[:, 0:512]), waits=w2)
                    ev_bank[1] = vop(lambda E: E.tensor_copy(
                        tmp32[:, 512:1024], YS[:, 512:1024]), waits=[("T", t_b1)])
                    v_ev = ev_bank[1]
                    dq("gpsimd", lambda E, c=c, h=h: E.dma_start(
                        out=A(out_t, h * TH * DM + c * 128,
                              [[1024, 128], [131072, 8], [1, 128]]),
                        in_=A(tmp32, 0, [[1024, 128], [128, 8], [1, 128]]),
                        accum_op=OP.add), "CA",
                        waits=[("V", v_ev), ("XO", 16)])
                    caf_prev[0] = fence("gpsimd", "CA", "CAF")
            extra_h1[0] = conv_ev[(NC_DM - 1, 0)]
            extra_h3[0] = conv_ev[(NC_DM - 1, 1)]
            conv_t_end[0] = S.nt
            extra_ys[0] = S.nt

        def emit_wgroup(e, g):
            segi = e * NG + g
            for j in range(4):
                gsA = segi * 4 + j
                pA = gsA % RA
                pB = gsA % RB
                fcol = g * 1024 + j * 256
                if gsA >= RA:
                    S.wait("gpsimd", "T", freeA[gsA - RA])
                dq("gpsimd", lambda E, e=e, fcol=fcol, pA=pA: E.dma_start(
                    out=A(w1r, pA * 2048, [[RA * 2048, 128], [256, 8], [1, 256]]),
                    in_=A(w1_t, e * DM * FF + fcol,
                          [[FF, 128], [128 * FF, 8], [1, 256]])), "DW")
                dq("gpsimd", lambda E, e=e, fcol=fcol, pA=pA: E.dma_start(
                    out=A(w3r, pA * 2048, [[RA * 2048, 128], [256, 8], [1, 256]]),
                    in_=A(w3_t, e * DM * FF + fcol,
                          [[FF, 128], [128 * FF, 8], [1, 256]])), "DW")
                if gsA >= RB:
                    S.wait("gpsimd", "T", freeB[gsA - RB])
                dq("gpsimd", lambda E, e=e, fcol=fcol, pB=pB: E.dma_start(
                    out=A(w2r, pB * 2048, [[RB * 2048, 128], [1024, 2], [1, 1024]]),
                    in_=A(w2_t, e * FF * DM + fcol * DM,
                          [[DM, 128], [128 * DM, 2], [1, 1024]])), "DW")
                S.wait("gpsimd", "DW", S.dma["DW"])
                S.emit("gpsimd", lambda E, sems: E.sem_inc(sems["GFS"], 1))
                S.dma["GFS"] = S.dma.get("GFS", 0) + 1

        gaf_pair = {}
        xg_ev = {}
        v_xg = {}
        tr_last = {}
        scf_of = {}

        def xg_of(ee):
            return xgA

        def emit_prefetch(ee):
            ib = (ee % 3) * NSC
            for i in range(NSC):
                dq("sync", lambda E, ee=ee, i=i, ib=ib: E.dma_start(
                    out=A(idx_sb, ib + i, [[3 * NSC, 128], [1, 1]]),
                    in_=A(tok_d, ee * CAP + i * 128, [[1, 128], [1, 1]])), "XD",
                    waits=([("TK", tk_total), ("GAF", S.dma.get("GAF", 0)),
                            ("SCF", S.dma.get("SCF", 0))]
                           if i == 0 else []))
            S.wait("sync", "XD", S.dma["XD"])
            S.emit("sync", lambda E, sems: E.sem_inc(sems["XDF"], 1))
            S.dma["XDF"] = S.dma.get("XDF", 0) + 1
            xdf_e = S.dma["XDF"]
            gw = [("XDF", xdf_e), ("XT", 16), ("WD", 16)]
            if ee >= 1:
                gw.append(("T", tr_last[ee - 1]))
            if ee >= 2:
                gw.append(("V", yend_v[ee - 2]))
            wb = (ee % 2) * NSC * NE
            for pr in range(3):
                for q in range(2):
                    i = pr * 2 + q
                    dq("gpsimd", lambda E, pr=pr, q=q, i=i, ib=ib:
                       E.indirect_dma_start(
                        out=A(xgt, pr * 2048 + q * 1024, [[6144, 128], [1, 1024]]),
                        out_offset=None,
                        in_=xnt_d[:, :],
                        in_offset=IndirectOffsetOnAxis(
                            ap=idx_sb[:, ib + i:ib + i + 1], axis=0)),
                        "GA", waits=gw)
                    gw = []
                for q in range(2):
                    i = pr * 2 + q
                    dq("gpsimd", lambda E, i=i, ib=ib, wb=wb:
                       E.indirect_dma_start(
                        out=wsl[:, wb + i * 8:wb + (i + 1) * 8], out_offset=None,
                        in_=wt_d[:, :],
                        in_offset=IndirectOffsetOnAxis(
                            ap=idx_sb[:, ib + i:ib + i + 1], axis=0)),
                        "GA")
                gaf_pair[(ee, pr)] = fence("gpsimd", "GA", "GAF")

        def emit_transposes(ee):
            xge = xg_of(ee)
            for pr in range(3):
                for q in range(2):
                    i = pr * 2 + q
                    TR = TRA if i % 2 == 0 else TRB
                    w = [("GAF", gaf_pair[(ee, pr)]), ("V", v_idb)]
                    if i >= 2:
                        w.append(("V", xg_ev[(ee, i - 2)]))
                    elif ee >= 1:
                        w.append(("V", xg_ev[(ee - 1, 4 + i)]))
                    for c in range(NC_DM):
                        top(lambda E, pr=pr, q=q, c=c, TR=TR: E.transpose(
                            TR[:, c * 128:(c + 1) * 128],
                            A(xgt, pr * 2048 + q * 1024 + c * 128,
                              [[6144, 128], [1, 128]]),
                            idb[:, :]), waits=(w if c == 0 else ()), inc=(c == 7))
                        w = ()
                    t_tr = S.nt
                    xg_ev[(ee, i)] = vop(lambda E, i=i, TR=TR, xge=xge:
                                         E.tensor_copy(
                        A(xge, i * 128, [[NC_DM * CAP, 128], [CAP, NC_DM], [1, 128]]),
                        A(TR, 0, [[1024, 128], [128, 8], [1, 128]])),
                        waits=[("T", t_tr)])
            tr_last[ee] = S.nt
            v_xg[ee] = xg_ev[(ee, 5)]

        def fl_ap(slq):
            if slq < 3:
                return [tmp32, kscr, convb][slq][:, :]
            return flb[:, :]
        fsem = ["FL", "FL1", "FL2", "FL3"]

        def emit_merge(c_lo, c_hi):
            for c in range(c_lo, c_hi):
                slq = c % 4
                w = [("FA", 16 * (c - 3))] if c >= 4 else []
                if c == 0:
                    w += [("SC", S.dma.get("SC", 0)),
                          ("CA", S.dma.get("CA", 0)), ("XO", 16),
                          ("T", conv_t_end[0])]
                dq("gpsimd", lambda E, c=c, slq=slq: E.dma_start(
                    out=fl_ap(slq),
                    in_=yac_d[c * 128:(c + 1) * 128, :]), fsem[slq], waits=w)
                dq("gpsimd", lambda E, c=c, slq=slq: E.dma_start(
                    out=out_t[c * 128:(c + 1) * 128, :],
                    in_=fl_ap(slq),
                    accum_op=OP.add), "FA",
                    waits=[(fsem[slq], 16 * (c // 4 + 1)),
                           ("FA", 16 * c)])

        for e in range(NE):
            if e == 0:
                emit_wgroup(0, 0)
            if e > 0:
                ibp = ((e - 1) % 3) * NSC
                for sc in range(NSC):
                    dq("gpsimd", lambda E, sc=sc, ibp=ibp: E.indirect_dma_start(
                        out=yac_d[:, :],
                        out_offset=IndirectOffsetOnAxis(
                            ap=idx_sb[:, ibp + sc:ibp + sc + 1], axis=0),
                        in_=ysw[:, sc * 1024:(sc + 1) * 1024],
                        in_offset=None, compute_op=OP.add), "SC",
                        waits=[("V", yend_v[e - 1]), ("ZY", zy_total),
                               ("SC", S.dma.get("SC", 0))])
                scf_of[e - 1] = fence("gpsimd", "SC", "SCF")
            if e == 0:
                emit_prefetch(0)
            emit_transposes(e)
            xg = xg_of(e)
            ysl_clk = 0
            for g in range(NG):
                segi = e * NG + g
                for fl in range(GF):
                    gsA = segi * 4 + fl // 2
                    pA = gsA % RA
                    scol_w = pA * 2048 + (fl % 2) * 128
                    w = [("GFS", segi * 4 + fl // 2 + 1), ("V", v_xg[e])]
                    if fg > 0:
                        w.append(("S", silu_clk[fg - 1]))
                    if extra_h1[0]:
                        w.append(("V", extra_h1[0])); extra_h1[0] = 0
                    for k in range(NC_DM):
                        for nc2 in range(2):
                            n0 = nc2 * 512
                            nn = 512 if nc2 == 0 else CAP - 512
                            top(lambda E, k=k, n0=n0, nn=nn, sw=scol_w,
                                xg=xg: E.matmul(
                                H1[:, n0:n0 + nn],
                                w1r[:, sw + k * 256:sw + k * 256 + 128],
                                xg[:, k * CAP + n0:k * CAP + n0 + nn],
                                start=(k == 0), stop=(k == NC_DM - 1)),
                                waits=(w if (k == 0 and nc2 == 0) else ()),
                                inc=(k == NC_DM - 1 and nc2 == 1))
                    t_h1 = S.nt
                    w = []
                    if fg > 0:
                        w.append(("V", gmul_clk[fg - 1]))
                    if extra_h3[0]:
                        w.append(("V", extra_h3[0])); extra_h3[0] = 0
                    for k in range(NC_DM):
                        for nc2 in range(2):
                            n0 = nc2 * 512
                            nn = 512 if nc2 == 0 else CAP - 512
                            top(lambda E, k=k, n0=n0, nn=nn, sw=scol_w,
                                xg=xg: E.matmul(
                                H3[:, n0:n0 + nn],
                                w3r[:, sw + k * 256:sw + k * 256 + 128],
                                xg[:, k * CAP + n0:k * CAP + n0 + nn],
                                start=(k == 0), stop=(k == NC_DM - 1)),
                                waits=(w if (k == 0 and nc2 == 0) else ()),
                                inc=(k == NC_DM - 1 and nc2 == 1))
                    t_h3 = S.nt
                    if fl % 2 == 1:
                        freeA[gsA] = t_h3
                    hend[e] = t_h3
                    w = [("T", t_h1)]
                    if fg >= 2:
                        w.append(("V", gmul_clk[fg - 2]))
                    silu_clk[fg] = sop(lambda E, fl=fl: E.activation(
                        s_sb[:, (fl % 2) * CAP:(fl % 2 + 1) * CAP],
                        H1[:, 0:CAP], AF.Silu), waits=w)
                    w = [("T", t_h3), ("S", silu_clk[fg])]
                    if ysl_clk:
                        w.append(("T", ysl_clk))
                    gmul_clk[fg] = vop(lambda E, fl=fl: E.tensor_tensor(
                        g_sb[:, fl * CAP:(fl + 1) * CAP],
                        s_sb[:, (fl % 2) * CAP:(fl % 2 + 1) * CAP],
                        H3[:, 0:CAP], op=OP.mult), waits=w)
                    if e < 2 and fg < LL:
                        emit_kstep(fg)
                    fg += 1
                for sc in range(NSC):
                    w = [("V", gmul_clk[fg - 1])]
                    if extra_ys[0]:
                        w.append(("T", extra_ys[0])); extra_ys[0] = 0
                    if (g, sc) != (0, 0):
                        w.append(("V", S.nv))
                    for fl in range(GF):
                        pB = (segi * 4 + fl // 2) % RB
                        for dmh in range(2):
                            top(lambda E, fl=fl, dmh=dmh, sc=sc, pB=pB: E.matmul(
                                YS[:, dmh * 512:(dmh + 1) * 512],
                                g_sb[:, fl * CAP + sc * 128:fl * CAP + (sc + 1) * 128],
                                w2r[:, pB * 2048 + (fl % 2) * 1024 + dmh * 512:
                                    pB * 2048 + (fl % 2) * 1024 + dmh * 512 + 512],
                                start=(fl == 0), stop=(fl == GF - 1)),
                                waits=(w if (fl == 0 and dmh == 0) else ()),
                                inc=(fl == GF - 1 and dmh == 1))
                    t_y = S.nt
                    ysl_clk = t_y
                    if g == 0:
                        vop(lambda E, sc=sc: E.tensor_copy(
                            ysl[:, sc * 1024:(sc + 1) * 1024], YS[:, :]),
                            waits=[("T", t_y)])
                    else:
                        vop(lambda E, sc=sc: E.tensor_tensor(
                            ysl[:, sc * 1024:(sc + 1) * 1024],
                            ysl[:, sc * 1024:(sc + 1) * 1024], YS[:, :], op=OP.add),
                            waits=[("T", t_y)])
                for j in range(4):
                    freeB[segi * 4 + j] = ysl_clk
                if g == 0 and e + 1 < NE:
                    emit_prefetch(e + 1)
                if g < NG - 1:
                    emit_wgroup(e, g + 1)
                elif e + 1 < NE:
                    emit_wgroup(e + 1, 0)
                if e == NE - 1 and g < 3:
                    emit_merge([0, 6, 12][g], [6, 12, 16][g])
            yel[e] = S.nv
            for sc in range(NSC):
                wb = (e % 2) * NSC * NE
                wgs = ([("SCF", scf_of[e - 1])] if (e > 0 and sc == 0) else [])
                yend_v[e] = vop(lambda E, sc=sc, e=e, wb=wb: E.tensor_scalar(
                    ysw[:, sc * 1024:(sc + 1) * 1024],
                    ysl[:, sc * 1024:(sc + 1) * 1024],
                    wsl[:, wb + sc * 8 + e:wb + sc * 8 + e + 1], None,
                    op0=OP.mult), waits=wgs)
            if e == 1:
                emit_conv()
        ib7 = ((NE - 1) % 3) * NSC
        for sc in range(NSC):
            dq("gpsimd", lambda E, sc=sc: E.indirect_dma_start(
                out=out_t[:, :],
                out_offset=IndirectOffsetOnAxis(
                    ap=idx_sb[:, ib7 + sc:ib7 + sc + 1], axis=0),
                in_=ysw[:, sc * 1024:(sc + 1) * 1024],
                in_offset=None, compute_op=OP.add,
                bounds_check=T - 1, oob_is_err=False), "SC",
                waits=([("V", yend_v[NE - 1]), ("SC", S.dma.get("SC", 0)),
                        ("FA", 256)]
                       if sc == 0 else [("SC", S.dma.get("SC", 0))]))
        S.wait("gpsimd", "SC", S.dma["SC"])
        S.wait("gpsimd", "FA", 256)


        with nc.Block() as block:
            @block.sync
            def _(E):
                for fn in S.ops["sync"]:
                    fn(E, sem_handles)

            @block.gpsimd
            def _(E):
                for fn in S.ops["gpsimd"]:
                    fn(E, sem_handles)

            @block.scalar
            def _(E):
                for fn in S.ops["scalar"]:
                    fn(E, sem_handles)

            @block.vector
            def _(E):
                for fn in S.ops["vector"]:
                    fn(E, sem_handles)

            @block.tensor
            def _(E):
                for fn in S.ops["tensor"]:
                    fn(E, sem_handles)
    return nc


def _layouts(inputs):
    Wg = np.ascontiguousarray(np.asarray(inputs["L0_ffn_Wg"], np.float32))
    wg2 = np.ascontiguousarray(Wg.reshape(8, 128, 8).transpose(1, 0, 2).reshape(128, 64))
    d2 = np.ascontiguousarray(np.asarray(inputs["L0_attn_D"], np.float32).reshape(8, 128).T)
    ldt2 = np.ascontiguousarray(
        np.asarray(inputs["L0_attn_log_dt"], np.float32).reshape(8, 128).T)
    ident = np.eye(128, dtype=np.float32)
    return wg2, d2, ldt2, ident


def kernel(L0_attn_u, L0_attn_A, L0_attn_B, L0_attn_C, L0_attn_D, L0_attn_log_dt,
           L0_ffn_Wg, L0_ffn_W1, L0_ffn_W3, L0_ffn_W2):
    u = np.ascontiguousarray(np.asarray(L0_attn_u, np.float32))
    W1 = np.ascontiguousarray(np.asarray(L0_ffn_W1, np.float32))
    W3 = np.ascontiguousarray(np.asarray(L0_ffn_W3, np.float32))
    W2 = np.ascontiguousarray(np.asarray(L0_ffn_W2, np.float32))
    Am = np.ascontiguousarray(np.asarray(L0_attn_A, np.float32).reshape(1, DS))
    Bm = np.ascontiguousarray(np.asarray(L0_attn_B, np.float32))
    Cm = np.ascontiguousarray(np.asarray(L0_attn_C, np.float32))
    ins = {"L0_ffn_Wg": L0_ffn_Wg, "L0_attn_D": L0_attn_D,
           "L0_attn_log_dt": L0_attn_log_dt}
    wg2, d2, ldt2, ident = _layouts(ins)
    nc = build_nc()
    in_maps = []
    for b in range(NCORES):
        in_maps.append({
            "u": np.ascontiguousarray(u[b]),
            "w1": W1, "w3": W3, "w2": W2, "wg2": wg2,
            "a_v": Am, "b_m": Bm, "c_m": Cm, "d2": d2, "logdt2": ldt2,
            "ident": ident,
        })
    res = run_bass_kernel_spmd(nc, in_maps, core_ids=list(range(NCORES)))
    out = np.stack([np.asarray(res.results[b]["out"]) for b in range(NCORES)], axis=0)
    return out


if __name__ == "__main__":
    nc = build_nc()
    print("build ok")
